# version 16
# baseline (speedup 1.0000x reference)
import numpy as np
import concourse.bass as bass
import concourse.mybir as mybir
from concourse.bass_utils import run_bass_kernel_spmd
from contextlib import ExitStack

F32 = mybir.dt.float32
BF16 = mybir.dt.bfloat16
AF = mybir.ActivationFunctionType
ALU = mybir.AluOpType
AX = mybir.AxisListType

D = 1024
DI = 2048
NS = 16
T = 4096
TT = 512
NTILE = T // TT
NCH = TT // 128
NE = 8
NDC = D // 128
NFC = DI // 128
NEC = DI // 128
CAP = 256
SLOTS = NE * CAP
EPS = 1e-5
R1024 = 1.0 / 1024.0
EPSK = 1024.0 * EPS
K2 = 1024.0 * EPS * EPS
HL1024 = 3.4657359027997265


def _build():
    nc = bass.Bass("TRN2", target_bir_lowering=False)

    x_in = nc.dram_tensor("x", [T, D], F32, kind="ExternalInput")
    w1_in = nc.dram_tensor("w1", [NE, D, DI], F32, kind="ExternalInput")
    w2_in = nc.dram_tensor("w2", [NE, DI, D], F32, kind="ExternalInput")
    wg_in = nc.dram_tensor("wg", [128, NDC, NE], F32, kind="ExternalInput")
    win_in = nc.dram_tensor("win", [DI, D], F32, kind="ExternalInput")
    ab_in = nc.dram_tensor("ab", [128, NEC, 32], F32, kind="ExternalInput")
    c_in = nc.dram_tensor("cm", [128, NEC, NS], F32, kind="ExternalInput")
    woutt_in = nc.dram_tensor("woutt", [DI, D], F32, kind="ExternalInput")
    bin_in = nc.dram_tensor("binc", [128, NEC], F32, kind="ExternalInput")
    nw_in = nc.dram_tensor("nwc", [128, NDC], F32, kind="ExternalInput")
    nb_in = nc.dram_tensor("nbc", [128, NDC], F32, kind="ExternalInput")
    bout_in = nc.dram_tensor("bout", [1, D], F32, kind="ExternalInput")
    basec_in = nc.dram_tensor("basec", [32, 1], F32, kind="ExternalInput")
    capc_in = nc.dram_tensor("capc", [32, 1], F32, kind="ExternalInput")
    out_d = nc.dram_tensor("out", [T, D], F32, kind="ExternalOutput")
    xnd_d = nc.dram_tensor("xnd", [T, D], BF16)
    xsl_ds = [nc.dram_tensor("xsl%d" % j, [SLOTS, D], BF16) for j in range(2)]
    ysl_ds = [nc.dram_tensor("ysl%d" % j, [SLOTS, D], BF16) for j in range(2)]

    cnt = {}

    def inc(ins, sem, n=1):
        ins.then_inc(sem, n)
        k = id(sem)
        cnt[k] = cnt.get(k, 0) + n
        return cnt[k]

    def val(sem):
        return cnt.get(id(sem), 0)

    with ExitStack() as ctx:
        E = ctx.enter_context
        xnhi_fm = E(nc.sbuf_tensor("xnhi_fm", [128, 3, NDC, 128], BF16, is_transpose_dest=True))
        xnlo_fm = E(nc.sbuf_tensor("xnlo_fm", [128, NDC, 128], BF16, is_transpose_dest=True))
        w1bf = E(nc.sbuf_tensor("w1bf", [128, NDC, DI], BF16))
        w2bf = E(nc.sbuf_tensor("w2bf", [128, NFC, D], BF16))
        h_sb = E(nc.sbuf_tensor("h_sb", [128, NFC, CAP], BF16))
        sx_fm = E(nc.sbuf_tensor("sx_fm", [128, 2, NDC, CAP], BF16, is_transpose_dest=True))
        xb_sb = E(nc.sbuf_tensor("xb_sb", [128, 2, D], BF16))
        ysl_sb = E(nc.sbuf_tensor("ysl_sb", [128, 2, D], BF16))
        y1g = E(nc.sbuf_tensor("y1g", [128, D], BF16))
        y2g = E(nc.sbuf_tensor("y2g", [128, D], BF16))
        zz_sb = E(nc.sbuf_tensor("zz_sb", [128, D], BF16))
        ind32 = E(nc.sbuf_tensor("ind32", [128, 32], F32))
        indT = E(nc.sbuf_tensor("indT", [32, TT], F32))
        zeroT = E(nc.sbuf_tensor("zeroT", [32, TT], F32))
        cnt_fm = E(nc.sbuf_tensor("cnt_fm", [32, TT], F32))
        stm = E(nc.sbuf_tensor("stm", [128, 32], F32))
        eq4 = E(nc.sbuf_tensor("eq4", [128, NCH, NE], F32))
        ind4 = E(nc.sbuf_tensor("ind4", [128, NCH, NE], F32))
        tsa = E(nc.sbuf_tensor("tsa", [128, NE], F32))
        tsb = E(nc.sbuf_tensor("tsb", [128, NE], F32))
        s12f = E(nc.sbuf_tensor("s12f", [128, 2], F32))
        i12 = E(nc.sbuf_tensor("i12", [128, 2, NCH, 2], mybir.dt.int32))
        c12 = E(nc.sbuf_tensor("c12", [128, 2, NCH, 2], F32))
        basec_s = E(nc.sbuf_tensor("basec_s", [32, 1], F32))
        capc_s = E(nc.sbuf_tensor("capc_s", [32, 1], F32))
        oacc = E(nc.sbuf_tensor("oacc", [128, 2, NCH, D], F32))
        xst = E(nc.sbuf_tensor("xst", [128, 3, D], F32))
        outst = E(nc.sbuf_tensor("outst", [128, 2, D], F32))
        xn_tm = E(nc.sbuf_tensor("xn_tm", [128, D], F32))
        xc_s = E(nc.sbuf_tensor("xc_s", [128, 3, D], F32))
        sq_s = E(nc.sbuf_tensor("sq_s", [128, D], F32))
        xnhi_tm = E(nc.sbuf_tensor("xnhi_tm", [128, D], BF16))
        xnlo_tm = E(nc.sbuf_tensor("xnlo_tm", [128, D], BF16))
        auf_a = E(nc.sbuf_tensor("auf_a", [32, TT], F32))
        auf_u = E(nc.sbuf_tensor("auf_u", [32, TT], F32))
        st_f = E(nc.sbuf_tensor("st_f", [NS, TT], F32))
        st_bf = E(nc.sbuf_tensor("st_bf", [NS, 2, TT], BF16))
        carry_s = E(nc.sbuf_tensor("carry_s", [NS, 1], F32))
        au_tm = E(nc.sbuf_tensor("au_tm", [128, 128], F32))
        apre = E(nc.sbuf_tensor("apre", [128, 2, 32], F32))
        term = E(nc.sbuf_tensor("term", [128, 32], F32))
        sn1 = E(nc.sbuf_tensor("sn1", [128, NS], F32))
        sn2 = E(nc.sbuf_tensor("sn2", [128, NS], F32))
        lg_sb = E(nc.sbuf_tensor("lg_sb", [128, 2, NE], F32))
        eq_sb = E(nc.sbuf_tensor("eq_sb", [128, NE], F32))
        msk_sb = E(nc.sbuf_tensor("msk_sb", [128, NE], F32))
        ind_sb = E(nc.sbuf_tensor("ind_sb", [128, NE], F32))
        e_sb = E(nc.sbuf_tensor("e_sb", [128, NE], F32))
        c_sb = E(nc.sbuf_tensor("c_sb", [128, 2, NCH, NE], F32))
        sum_s = E(nc.sbuf_tensor("sum_s", [128, 1], F32))
        mu_s = E(nc.sbuf_tensor("mu_s", [128, 1], F32))
        tq_s = E(nc.sbuf_tensor("tq_s", [128, 1], F32))
        q_s = E(nc.sbuf_tensor("q_s", [128, 1], F32))
        vs_s = E(nc.sbuf_tensor("vs_s", [128, 3], F32))
        sxc_s = E(nc.sbuf_tensor("sxc_s", [128, 4], F32))
        w_s = E(nc.sbuf_tensor("w_s", [128, 3], F32))
        rs_s = E(nc.sbuf_tensor("rs_s", [128, 3], F32))
        u1_s = E(nc.sbuf_tensor("u1_s", [128, 1], F32))
        hu1_s = E(nc.sbuf_tensor("hu1_s", [128, 1], F32))
        u2_s = E(nc.sbuf_tensor("u2_s", [128, 1], F32))
        mu2_s = E(nc.sbuf_tensor("mu2_s", [128, 1], F32))
        rs2_c = E(nc.sbuf_tensor("rs2_c", [128, NCH], F32))
        rm2_c = E(nc.sbuf_tensor("rm2_c", [128, NCH], F32))
        m1_s = E(nc.sbuf_tensor("m1_s", [128, 2], F32))
        nm1_s = E(nc.sbuf_tensor("nm1_s", [128, 2], F32))
        m2_s = E(nc.sbuf_tensor("m2_s", [128, 2], F32))
        e2_s = E(nc.sbuf_tensor("e2_s", [128, 1], F32))
        den_s = E(nc.sbuf_tensor("den_s", [128, 1], F32))
        rden_s = E(nc.sbuf_tensor("rden_s", [128, 1], F32))
        epsk_s = E(nc.sbuf_tensor("epsk_s", [128, 1], F32))
        wgbf = E(nc.sbuf_tensor("wgbf", [128, NDC, NE], BF16))

        abhi = E(nc.sbuf_tensor("abhi", [128, NEC, 32], BF16))
        ablo = E(nc.sbuf_tensor("ablo", [128, NEC, 32], BF16))

        chi = E(nc.sbuf_tensor("chi", [128, NEC, NS], BF16))
        clo = E(nc.sbuf_tensor("clo", [128, NEC, NS], BF16))

        binhi = E(nc.sbuf_tensor("binhi", [128, NEC], BF16))
        binlo = E(nc.sbuf_tensor("binlo", [128, NEC], BF16))
        nw_f = E(nc.sbuf_tensor("nw_f", [128, NDC], F32))
        ab_f = xst[:, 0, 0:512].rearrange("p (a b) -> p a b", b=32)
        c_f = xst[:, 0, 512:768].rearrange("p (a b) -> p a b", b=16)
        bin_f = xst[:, 1, 0:16]
        nb_f = xst[:, 1, 16:24]

        nbhi = E(nc.sbuf_tensor("nbhi", [128, NDC], BF16))
        nblo = E(nc.sbuf_tensor("nblo", [128, NDC], BF16))
        bout_rep = E(nc.sbuf_tensor("bout_rep", [128, D], BF16))
        onesr_bf = E(nc.sbuf_tensor("onesr_bf", [1, 128], BF16))
        onec_bf = E(nc.sbuf_tensor("onec_bf", [128, 1], BF16))
        vwhi = E(nc.sbuf_tensor("vwhi", [1, 32], BF16))
        vwlo = E(nc.sbuf_tensor("vwlo", [1, 32], BF16))
        cbhi = E(nc.sbuf_tensor("cbhi", [1, 32], BF16))
        cblo = E(nc.sbuf_tensor("cblo", [1, 32], BF16))
        cab_f = E(nc.sbuf_tensor("cab_f", [128, NDC, 32], F32))
        cabhi = E(nc.sbuf_tensor("cabhi", [128, NDC, 32], BF16))
        cablo = E(nc.sbuf_tensor("cablo", [128, NDC, 32], BF16))
        cabw_f = E(nc.sbuf_tensor("cabw_f", [128, NDC, 32], F32))
        cabwhi = E(nc.sbuf_tensor("cabwhi", [128, NDC, 32], BF16))
        cabwlo = E(nc.sbuf_tensor("cabwlo", [128, NDC, 32], BF16))
        vwrow = E(nc.sbuf_tensor("vwrow", [1, 32], F32))
        cbrow = E(nc.sbuf_tensor("cbrow", [1, 32], F32))
        vwrep = E(nc.sbuf_tensor("vwrep", [128, 32], F32))
        cbrep = E(nc.sbuf_tensor("cbrep", [128, 32], F32))
        mtbf = E(nc.sbuf_tensor("mtbf", [NS, D], BF16))
        ps_h = E(nc.psum_tensor("ps_h", [128, 1024], F32))
        ps_ya = E(nc.psum_tensor("ps_ya", [128, 1024], F32))
        ps_yb = E(nc.psum_tensor("ps_yb", [128, 1024], F32))
        ps_ap = E(nc.psum_tensor("ps_ap", [128, 32], F32))
        ps_misc = E(nc.psum_tensor("ps_misc", [128, 512], F32))
        ps_tr = ps_misc[:, 0:128]
        ps_lg = ps_misc[:, 128:136]
        s_c = E(nc.semaphore("s_c"))
        s_xs = [E(nc.semaphore("s_x%d" % j)) for j in range(4)]
        s_tps = [E(nc.semaphore("s_tp%d" % j)) for j in range(8)]
        s_outs = [E(nc.semaphore("s_out%d" % j)) for j in range(2)]
        s_wg = E(nc.semaphore("s_wg"))
        s_w1 = E(nc.semaphore("s_w1"))
        s_w2 = E(nc.semaphore("s_w2"))
        s_t = E(nc.semaphore("s_t"))
        s_v = E(nc.semaphore("s_v"))
        s_sc = E(nc.semaphore("s_sc"))
        s_ye = E(nc.semaphore("s_ye"))
        s_re = E(nc.semaphore("s_re"))
        s_zz = [E(nc.semaphore("s_zz%d" % j)) for j in range(2)]
        s_scat = [E(nc.semaphore("s_scat%d" % j)) for j in range(2)]
        s_sxt = [E(nc.semaphore("s_sxt%d" % j)) for j in range(2)]
        s_ysl2 = [E(nc.semaphore("s_ysl%d" % j)) for j in range(4)]
        s_yg = [E(nc.semaphore("s_yg%d" % j)) for j in range(2)]
        s_xb = [E(nc.semaphore("s_xb%d" % j)) for j in range(2)]
        s_xnd = E(nc.semaphore("s_xnd"))

        M = {}
        LATE = {"lgcp": 0, "m6": 0, "apmm": 0}
        blk = E(nc.Block())

        @blk.sync
        def _(sy):
            pass
            inc(sy.dma_start(out=nw_f[:], in_=nw_in.ap()), s_c, 16)
            inc(sy.dma_start(out=nb_f, in_=nb_in.ap()), s_c, 16)
            inc(sy.dma_start(out=bin_f, in_=bin_in.ap()), s_c, 16)
            inc(sy.dma_start(out=ab_f, in_=ab_in.ap()), s_c, 16)
            inc(sy.dma_start(out=c_f, in_=c_in.ap()), s_c, 16)
            inc(sy.dma_start(out=basec_s[:], in_=basec_in.ap()), s_c, 16)
            inc(sy.dma_start(out=capc_s[:], in_=capc_in.ap()), s_c, 16)
            M["consts"] = val(s_c)

        @blk.gpsimd
        def _(g):
            inc(g.dma_start(out=bout_rep[:],
                            in_=bout_in.ap()[0:1, :].partition_broadcast(128)), s_wg, 16)
            inc(g.dma_start(out=wgbf[:], in_=wg_in.ap()), s_wg, 16)
            w1r = w1_in.ap()[0].rearrange("(dc p) f -> p dc f", p=128)
            inc(g.dma_start(out=w1bf[:, 0:4, :], in_=w1r[:, 0:4, :]), s_w1, 16)
            inc(g.dma_start(out=w1bf[:, 4:8, :], in_=w1r[:, 4:8, :]), s_w1, 16)
            w2r = w2_in.ap()[0].rearrange("(fc p) d -> p fc d", p=128)
            for q in range(4):
                inc(g.dma_start(out=w2bf[:, 4 * q:4 * q + 4, :],
                                in_=w2r[:, 4 * q:4 * q + 4, :]), s_w2, 16)

        @blk.vector
        def _(v):
            v.memset(onec_bf[:], 1.0)
            v.memset(onesr_bf[:], 1.0)
            v.memset(au_tm[:, 16:32], 0.0)
            v.memset(au_tm[:, 48:64], 0.0)
            v.memset(epsk_s[:], EPSK)
            v.memset(zz_sb[:], 0.0)
            v.memset(zeroT[:], 0.0)
            v.memset(ind32[:, 8:32], 0.0)

        @blk.scalar
        def _(s):
            s.wait_ge(s_c, M["consts"])
            s.copy(abhi[:], ab_f)
            s.copy(chi[:], c_f)
            s.copy(binhi[:], bin_f)
            inc(s.copy(nbhi[:], nb_f), s_sc)
            M["hi_consts"] = val(s_sc)

        @blk.vector
        def _(v):
            v.wait_ge(s_sc, M["hi_consts"])
            v.tensor_sub(ablo[:], ab_f, abhi[:])
            v.tensor_sub(clo[:], c_f, chi[:])
            v.tensor_sub(binlo[:], bin_f, binhi[:])
            inc(v.tensor_sub(nblo[:], nb_f, nbhi[:]), s_v)
            M["lo_consts"] = val(s_v)

        caps = [ps_h[:, 0:32], ps_h[:, 512:544],
                ps_ya[:, 0:32], ps_ya[:, 512:544],
                ps_yb[:, 0:32], ps_yb[:, 512:544],
                ps_ap[:, 0:32], ps_misc[:, 0:32]]
        for ec in range(NEC):
            @blk.sync
            def _(sy, ec=ec):
                if ec >= 2:
                    sy.wait_ge(s_sc, M[("whi", ec - 2)])
                    sy.wait_ge(s_v, M[("wlo", ec - 2)])
                else:
                    sy.wait_ge(s_sc, M["hi_consts"])
                    sy.wait_ge(s_v, M["lo_consts"])
                a = sy.dma_start(out=xst[:, ec % 2, :],
                                 in_=win_in.ap()[ec * 128:(ec + 1) * 128, :])
                M[("wld", ec)] = inc(a, s_xs[ec % 2], 16)

            @blk.scalar
            def _(s, ec=ec):
                s.wait_ge(s_xs[ec % 2], M[("wld", ec)])
                if ec >= 1:
                    s.wait_ge(s_t, M[("cabmm", ec - 1)])
                M[("whi", ec)] = inc(s.copy(xnhi_tm[:], xst[:, ec % 2, :]), s_sc)

            @blk.vector
            def _(v, ec=ec):
                v.wait_ge(s_sc, M[("whi", ec)])
                M[("wlo", ec)] = inc(
                    v.tensor_sub(xnlo_tm[:], xst[:, ec % 2, :], xnhi_tm[:]), s_v)

            @blk.tensor
            def _(t, ec=ec):
                t.wait_ge(s_v, M[("wlo", ec)])
                if ec == 0:
                    t.wait_ge(s_sc, M["hi_consts"])
                    t.wait_ge(s_v, M["lo_consts"])
                last = None
                for dc in range(NDC):
                    for p, (lh, rh) in enumerate([
                            (xnhi_tm, abhi), (xnlo_tm, abhi), (xnhi_tm, ablo)]):
                        last = t.matmul(caps[dc], lh[:, dc * 128:(dc + 1) * 128],
                                        rh[:, ec, :],
                                        start=(ec == 0 and p == 0),
                                        stop=(ec == NEC - 1 and p == 2))
                M[("cabmm", ec)] = inc(last, s_t)

        @blk.vector
        def _(v):
            v.wait_ge(s_t, M[("cabmm", NEC - 1)])
            for dc in range(NDC):
                v.tensor_copy(cab_f[:, dc, :], caps[dc])
            v.drain()
            last = None
            for dc in range(NDC):
                last = v.tensor_scalar(cabw_f[:, dc, :], cab_f[:, dc, :],
                                       nw_f[:, dc:dc + 1], None, ALU.mult)
            M["cab_f"] = inc(last, s_v)

        @blk.scalar
        def _(s):
            s.wait_ge(s_v, M["cab_f"])
            s.copy(cabhi[:], cab_f[:])
            M["cabhi"] = inc(s.copy(cabwhi[:], cabw_f[:]), s_sc)

        @blk.vector
        def _(v):
            v.wait_ge(s_sc, M["cabhi"])
            v.tensor_sub(cablo[:], cab_f[:], cabhi[:])
            M["cablo"] = inc(v.tensor_sub(cabwlo[:], cabw_f[:], cabwhi[:]), s_v)

        @blk.tensor
        def _(t):
            t.wait_ge(s_v, M["cablo"])
            for dc in range(NDC):
                for rh in (cabwhi, cabwlo):
                    t.matmul(ps_ap[0:1, :], onec_bf[:], rh[:, dc, :],
                             start=(dc == 0 and rh is cabwhi),
                             stop=(dc == NDC - 1 and rh is cabwlo))
            first = True
            last2 = None
            for dc in range(NDC):
                for lh, rh in [(nbhi, cabhi), (nblo, cabhi), (nbhi, cablo)]:
                    last2 = t.matmul(ps_misc[0:1, 0:32], lh[:, dc:dc + 1],
                                     rh[:, dc, :], start=first, stop=False)
                    first = False
            for ec in range(NEC):
                for p, (lh, rh) in enumerate([(binhi, abhi), (binlo, abhi),
                                              (binhi, ablo)]):
                    last2 = t.matmul(ps_misc[0:1, 0:32], lh[:, ec:ec + 1],
                                     rh[:, ec, :], start=False,
                                     stop=(ec == NEC - 1 and p == 2))
            M["vwcb"] = inc(last2, s_t)

        @blk.vector
        def _(v):
            v.wait_ge(s_t, M["vwcb"])
            v.tensor_copy(vwrow[:], ps_ap[0:1, :])
            M["rows"] = inc(v.tensor_copy(cbrow[:], ps_misc[0:1, 0:32]), s_v)

        @blk.scalar
        def _(s):
            s.wait_ge(s_v, M["rows"])
            s.copy(vwhi[:], vwrow[:])
            M["rephi"] = inc(s.copy(cbhi[:], cbrow[:]), s_sc)

        @blk.vector
        def _(v):
            v.wait_ge(s_sc, M["rephi"])
            v.tensor_sub(vwlo[:], vwrow[:], vwhi[:])
            M["replo"] = inc(v.tensor_sub(cblo[:], cbrow[:], cbhi[:]), s_v)

        @blk.tensor
        def _(t):
            t.wait_ge(s_v, M["replo"])
            t.matmul(ps_ap[:], onesr_bf[:], vwhi[:], start=True, stop=False)
            M["rep1"] = inc(t.matmul(ps_ap[:], onesr_bf[:], vwlo[:],
                                     start=False, stop=True), s_t)

        @blk.vector
        def _(v):
            v.wait_ge(s_t, M["rep1"])
            M["vwrep"] = inc(v.tensor_copy(vwrep[:], ps_ap[:]), s_v)

        @blk.tensor
        def _(t):
            t.wait_ge(s_v, M["vwrep"])
            t.matmul(ps_ap[:], onesr_bf[:], cbhi[:], start=True, stop=False)
            M["rep2"] = inc(t.matmul(ps_ap[:], onesr_bf[:], cblo[:],
                                     start=False, stop=True), s_t)

        @blk.vector
        def _(v):
            v.wait_ge(s_t, M["rep2"])
            M["cbrep"] = inc(v.tensor_copy(cbrep[:], ps_ap[:]), s_v)

        for ec in range(NEC):
            @blk.sync
            def _(sy, ec=ec):
                sy.wait_ge(s_sc, M[("mwhi", ec - 2)] if ec >= 2 else M[("whi", NEC - 1)])
                sy.wait_ge(s_v, M[("mwlo", ec - 2)] if ec >= 2 else M[("wlo", NEC - 1)])
                a = sy.dma_start(out=xst[:, ec % 2, :],
                                 in_=woutt_in.ap()[ec * 128:(ec + 1) * 128, :])
                M[("mwld", ec)] = inc(a, s_xs[ec % 2], 16)

            @blk.scalar
            def _(s, ec=ec):
                s.wait_ge(s_xs[ec % 2], M[("mwld", ec)])
                s.wait_ge(s_t, M[("mtmm", ec - 1)] if ec >= 1 else M[("cabmm", NEC - 1)])
                M[("mwhi", ec)] = inc(s.copy(xnhi_tm[:], xst[:, ec % 2, :]), s_sc)

            @blk.vector
            def _(v, ec=ec):
                v.wait_ge(s_sc, M[("mwhi", ec)])
                M[("mwlo", ec)] = inc(
                    v.tensor_sub(xnlo_tm[:], xst[:, ec % 2, :], xnhi_tm[:]), s_v)

            @blk.tensor
            def _(t, ec=ec):
                t.wait_ge(s_v, M[("mwlo", ec)])
                if ec == 0:
                    t.wait_ge(s_v, M["cbrep"])
                last = None
                for hf in range(2):
                    sl = slice(hf * 512, hf * 512 + 512)
                    for p, (lh, rh) in enumerate([(chi, xnhi_tm), (clo, xnhi_tm),
                                                  (chi, xnlo_tm)]):
                        last = t.matmul(ps_yb[0:NS, sl], lh[:, ec, :], rh[:, sl],
                                        start=(ec == 0 and p == 0),
                                        stop=(ec == NEC - 1 and p == 2))
                M[("mtmm", ec)] = inc(last, s_t)

        @blk.vector
        def _(v):
            v.wait_ge(s_t, M[("mtmm", NEC - 1)])
            M["mtbf"] = inc(v.tensor_copy(mtbf[:], ps_yb[0:NS, :]), s_v)

        ge_w1free = {}
        ge_w2free = {}
        xfree = {}
        stores = [0]

        def s1VA(ti, c):
            g = ti * NCH + c
            gp = g % 2
            g3 = g % 3
            g4 = g % 4
            xs = g % 3
            t0 = g * 128

            @blk.sync
            def _(sy):
                if g >= 3:
                    sy.wait_ge(s_v, xfree[g - 3])
                else:
                    sy.wait_ge(s_sc, M[("mwhi", NEC - 1)])
                    sy.wait_ge(s_v, M[("mwlo", NEC - 1)])
                a = sy.dma_start(out=xst[:, xs, :], in_=x_in.ap()[t0:t0 + 128, :])
                M[("x", g)] = inc(a, s_xs[xs], 16)

            @blk.vector
            def _(v):
                v.wait_ge(s_xs[xs], M[("x", g)])
                xt = xst[:, xs, :]
                v.reduce_sum(sum_s[:], xt, AX.X)
                v.drain()
                v.tensor_scalar_mul(mu_s[:], sum_s[:], R1024)
                v.drain()
                v.tensor_scalar_sub(xc_s[:, g3, :], xt, mu_s[:])
                v.drain()
                v.tensor_mul(sq_s[:], xc_s[:, g3, :], xc_s[:, g3, :])
                v.reduce_sum(sxc_s[:, g4:g4 + 1], xc_s[:, g3, :], AX.X)
                v.drain()
                v.reduce_sum(vs_s[:, g3:g3 + 1], sq_s[:], AX.X)
                v.tensor_mul(tq_s[:], sxc_s[:, g4:g4 + 1], sxc_s[:, g4:g4 + 1])
                v.drain()
                v.scalar_tensor_tensor(q_s[:], tq_s[:], -R1024, vs_s[:, g3:g3 + 1],
                                       ALU.mult, ALU.add)
                v.drain()
                v.scalar_tensor_tensor(w_s[:, g3:g3 + 1], vs_s[:, g3:g3 + 1], EPS,
                                       q_s[:], ALU.mult, ALU.add)
                v.drain()
                M[("m1", g)] = inc(
                    v.tensor_scalar_add(w_s[:, g3:g3 + 1], w_s[:, g3:g3 + 1], K2),
                    s_v)

        def s1SB(ti, c):
            g = ti * NCH + c
            g3 = g % 3

            @blk.scalar
            def _(s):
                s.wait_ge(s_v, M[("m1", g)])
                s.activation(u1_s[:], vs_s[:, g3:g3 + 1], AF.Ln, bias=epsk_s[:])
                s.drain()
                s.mul(hu1_s[:], u1_s[:], 0.5)
                s.activation(rs_s[:, g3:g3 + 1], u1_s[:], AF.Exp, scale=-0.5)
                s.drain()
                s.mul(rs_s[:, g3:g3 + 1], rs_s[:, g3:g3 + 1], 32.0)
                s.drain()
                s.activation(u2_s[:], w_s[:, g3:g3 + 1], AF.Ln)
                s.drain()
                M[("m2", g)] = inc(
                    s.activation(rs2_c[:, c:c + 1], u2_s[:], AF.Exp,
                                 scale=-0.5, bias=hu1_s[:]), s_sc)

        def s1VC(ti, c):
            g = ti * NCH + c
            gp = g % 2
            g3 = g % 3
            g4 = g % 4
            xs = g % 3
            tb = ti % 2

            @blk.vector
            def _(v):
                v.wait_ge(s_sc, M[("m2", g)])
                if g >= 1:
                    v.wait_ge(s_tps[(g - 1) % 8], M[("tpd", g - 1)])
                    v.wait_ge(s_xnd, M[("xnd", g - 1)])
                else:
                    v.wait_ge(s_t, M[("mtmm", NEC - 1)])
                    v.wait_ge(s_c, M["consts"])
                    v.wait_ge(s_wg, 32)
                v.tensor_scalar_mul(xn_tm[:], xc_s[:, g3, :], rs_s[:, g3:g3 + 1])
                v.drain()
                v.tensor_copy(xnhi_tm[:], xn_tm[:])
                v.drain()
                v.tensor_sub(xnlo_tm[:], xn_tm[:], xnhi_tm[:])
                v.tensor_scalar_mul(mu2_s[:], sxc_s[:, g4:g4 + 1], R1024)
                v.drain()
                v.tensor_mul(mu2_s[:], mu2_s[:], rs_s[:, g3:g3 + 1])
                v.drain()
                v.tensor_mul(rm2_c[:, c:c + 1], mu2_s[:], rs2_c[:, c:c + 1])
                v.tensor_add(oacc[:, tb, c, :], xst[:, xs, :], xn_tm[:])
                v.drain()
                ii = v.tensor_add(oacc[:, tb, c, :], oacc[:, tb, c, :], bout_rep[:])
                xfree[g] = inc(ii, s_v)

            @blk.sync
            def _(sy):
                sy.wait_ge(s_v, xfree[g])
                if g >= 1:
                    sy.wait_ge(s_t, LATE["apmm"])
                stp = s_tps[g % 8]
                for dc in range(NDC):
                    inc(sy.dma_start_transpose(
                        xnhi_fm[:, g % 3, dc, :],
                        xnhi_tm[:, dc * 128:(dc + 1) * 128]), stp, 16)
                for dc in range(NDC):
                    inc(sy.dma_start_transpose(
                        xnlo_fm[:, dc, :],
                        xnlo_tm[:, dc * 128:(dc + 1) * 128]), stp, 16)
                M[("tpd", g)] = val(stp)
                M[("xnd", g)] = inc(
                    sy.dma_start(out=xnd_d.ap()[g * 128:(g + 1) * 128, :],
                                 in_=xnhi_tm[:]), s_xnd, 16)

        def s1B1(ti, c):
            g = ti * NCH + c
            tb = ti % 2
            pc = c % 2

            @blk.tensor
            def _(t):
                if g == 0:
                    t.wait_ge(s_wg, 32)
                    t.wait_ge(s_sc, M["cabhi"])
                    t.wait_ge(s_v, M["cablo"])
                    t.wait_ge(s_v, M["mtbf"])
                    t.wait_ge(s_v, M["cbrep"])
                if LATE["lgcp"]:
                    t.wait_ge(s_v, LATE["lgcp"])
                t.wait_ge(s_tps[g % 8], M[("tpd", g)])
                last = None
                for dc in range(NDC):
                    last = t.matmul(ps_lg, xnhi_fm[:, g % 3, dc, :],
                                    wgbf[:, dc, :],
                                    start=(dc == 0), stop=(dc == NDC - 1))
                M[("lgmm", g)] = inc(last, s_t)
                if LATE["m6"]:
                    t.wait_ge(s_v, LATE["m6"])
                for p in range(3):
                    for dc in range(NDC):
                        lh = (xnhi_fm[:, g % 3, dc, :]
                              if p != 1 else xnlo_fm[:, dc, :])
                        rh = cabwhi[:, dc, :] if p != 2 else cabwlo[:, dc, :]
                        last = t.matmul(ps_ap[:], lh, rh,
                                        start=(p == 0 and dc == 0),
                                        stop=(p == 2 and dc == NDC - 1))
                M[("apmm", g)] = inc(last, s_t)
                LATE["apmm"] = M[("apmm", g)]

            @blk.vector
            def _(v):
                if g >= 2:
                    v.wait_ge(s_sc, M[("m10", g - 2)])
                v.wait_ge(s_t, M[("lgmm", g)])
                M[("lgcp", g)] = inc(v.tensor_copy(lg_sb[:, pc, :], ps_lg), s_v)
                LATE["lgcp"] = M[("lgcp", g)]
                v.wait_ge(s_t, M[("apmm", g)])
                v.scalar_tensor_tensor(term[:], vwrep[:], rm2_c[:, c:c + 1],
                                       cbrep[:], ALU.mult, ALU.subtract)
                v.drain()
                v.scalar_tensor_tensor(apre[:, pc, :], ps_ap[:],
                                        rs2_c[:, c:c + 1], term[:],
                                        ALU.mult, ALU.subtract)
                v.drain()
                M[("m6", g)] = inc(
                    v.tensor_scalar_max(apre[:, pc, 0:NS], apre[:, pc, 0:NS],
                                        -30.0), s_v)
                LATE["m6"] = M[("m6", g)]
                v.reduce_max(m1_s[:, pc:pc + 1], lg_sb[:, pc, :], AX.X)
                v.reduce_max(nm1_s[:, pc:pc + 1], lg_sb[:, pc, :], AX.X, negate=True)
                v.drain()
                v.tensor_scalar(eq4[:, c, :], lg_sb[:, pc, :], m1_s[:, pc:pc + 1],
                                None, ALU.is_equal)
                v.drain()
                v.scalar_tensor_tensor(msk_sb[:], eq4[:, c, :], -1e30, lg_sb[:, pc, :],
                                       ALU.mult, ALU.add)
                v.drain()
                M[("m9", g)] = inc(
                    v.reduce_max(m2_s[:, pc:pc + 1], msk_sb[:], AX.X), s_v)

        def s1B2(ti, c):
            g = ti * NCH + c
            tb = ti % 2
            pc = c % 2

            @blk.scalar
            def _(s):
                if g >= 1:
                    s.wait_ge(s_v, M[("router", g - 1)])
                s.wait_ge(s_v, M[("m6", g)])
                M[("m7", g)] = inc(
                    s.activation(sn1[:], apre[:, pc, 0:NS], AF.Exp, scale=-1.0),
                    s_sc)
                s.wait_ge(s_v, M[("m9", g)])
                s.activation(e_sb[:], lg_sb[:, pc, :], AF.Exp,
                             bias=nm1_s[:, pc:pc + 1])
                M[("m10", g)] = inc(
                    s.activation(e2_s[:], m2_s[:, pc:pc + 1], AF.Exp,
                                 bias=nm1_s[:, pc:pc + 1]), s_sc)

            @blk.vector
            def _(v):
                v.wait_ge(s_sc, M[("m7", g)])
                v.tensor_copy(au_tm[:, 32:48], apre[:, pc, 16:32])
                v.tensor_scalar_add(sn2[:], sn1[:], 1.0)
                v.drain()
                v.reciprocal(au_tm[:, 0:16], sn2[:])
                v.drain()
                for rb in range(4):
                    for dst, cbk in ((auf_a, 0), (auf_u, 1)):
                        v.transpose(
                            dst[0:32, c * 128 + 32 * rb:c * 128 + 32 * rb + 32],
                            au_tm[32 * rb:32 * rb + 32, 32 * cbk:32 * cbk + 32])
                v.drain()
                v.wait_ge(s_sc, M[("m10", g)])
                v.tensor_scalar_add(den_s[:], e2_s[:], 1.0)
                v.drain()
                v.reciprocal(rden_s[:], den_s[:])
                v.tensor_scalar(ind4[:, c, :], lg_sb[:, pc, :], m2_s[:, pc:pc + 1],
                                None, ALU.is_ge)
                v.drain()
                v.tensor_copy(ind32[:, 0:8], ind4[:, c, :])
                v.drain()
                for rb in range(4):
                    v.transpose(indT[0:32, c * 128 + 32 * rb:c * 128 + 32 * rb + 32],
                                ind32[32 * rb:32 * rb + 32, 0:32])
                v.drain()
                M[("router", g)] = inc(
                    v.scalar_tensor_tensor(c_sb[:, tb, c, :], ind4[:, c, :], rden_s[:],
                                           e_sb[:], ALU.mult, ALU.mult), s_v)

        def scan_emit(ti):
            tb = ti % 2

            @blk.vector
            def _(v):
                if ti % 4 == 0:
                    v.memset(carry_s[:], 0.0)
                v.drain()
                v.tensor_tensor_scan(st_f[:], auf_a[0:NS, :], auf_u[0:NS, :],
                                     carry_s[:], ALU.mult, ALU.add)
                v.drain()
                v.tensor_copy(carry_s[:], st_f[:, TT - 1:TT])
                M[("scan", ti)] = inc(v.tensor_copy(st_bf[:, tb, :], st_f[:]), s_v)

        def sx_prefetch(ti, e):
            if e >= NE:
                return
            ge = ti * NE + e
            tp = ti % 2
            pe = e % 2

            @blk.sync
            def _(sy):
                sy.wait_ge(s_scat[tp], M[("scat", ti)])
                if ge >= 2:
                    sy.wait_ge(s_t, M[("hps", ge - 2, NFC - 1)])
                stx = s_sxt[pe]
                for cc in range(CAP // 128):
                    r0 = e * CAP + cc * 128
                    for dc in range(NDC):
                        inc(sy.dma_start_transpose(
                            sx_fm[:, pe, dc, cc * 128:(cc + 1) * 128],
                            xsl_ds[tp].ap()[r0:r0 + 128,
                                            dc * 128:(dc + 1) * 128]), stx, 16)
                M[("sxt", ge)] = val(stx)

        def zero_emit(ti):
            tp = ti % 2

            @blk.sync
            def _(sy):
                sy.wait_ge(s_v, M["lo_consts"])
                if ti >= 2:
                    sy.wait_ge(s_t, ge_w1free[(ti - 1) * NE])
                for r in range(SLOTS // 128):
                    inc(sy.dma_start(out=xsl_ds[tp].ap()[r * 128:(r + 1) * 128, :],
                                     in_=zz_sb[:]), s_zz[tp], 16)
                M[("zz", ti)] = val(s_zz[tp])

        def idx_emit(ti):
            tp = ti % 2
            tb = ti % 2

            @blk.vector
            def _(v):
                if ti >= 2:
                    for cc in range(NCH):
                        v.wait_ge(s_yg[cc % 2], M[("gath", ti - 2, cc)])
                v.tensor_tensor_scan(cnt_fm[:], indT[:], zeroT[:],
                                     zeroT[:, 0:1], ALU.add, ALU.add)
                v.drain()
                v.tensor_sub(indT[:], cnt_fm[:], indT[:])
                v.drain()
                v.tensor_scalar_add(indT[:], indT[:], basec_s[:])
                v.drain()
                v.tensor_scalar_min(indT[:], indT[:], capc_s[:])
                v.drain()
                for c in range(NCH):
                    for rb in range(4):
                        v.transpose(stm[32 * rb:32 * rb + 32, 0:32],
                                    indT[0:32, c * 128 + 32 * rb:c * 128 + 32 * rb + 32])
                    v.drain()
                    v.tensor_mul(tsa[:], stm[:, 0:8], eq4[:, c, :])
                    v.tensor_sub(tsb[:], ind4[:, c, :], eq4[:, c, :])
                    v.drain()
                    v.reduce_sum(s12f[:, 0:1], tsa[:], AX.X)
                    v.drain()
                    v.tensor_mul(tsa[:], stm[:, 0:8], tsb[:])
                    v.drain()
                    v.reduce_sum(s12f[:, 1:2], tsa[:], AX.X)
                    v.drain()
                    v.tensor_mul(tsa[:], c_sb[:, tb, c, :], eq4[:, c, :])
                    v.drain()
                    v.tensor_copy(i12[:, tp, c, 0:2], s12f[:])
                    v.reduce_sum(c12[:, tp, c, 0:1], tsa[:], AX.X)
                    v.drain()
                    v.tensor_mul(tsa[:], c_sb[:, tb, c, :], tsb[:])
                    v.drain()
                    v.reduce_sum(c12[:, tp, c, 1:2], tsa[:], AX.X)
                    v.drain()
                M[("idx", ti)] = inc(v.engine_nop(), s_v)

            @blk.gpsimd
            def _(g):
                g.wait_ge(s_v, M[("idx", ti)])
                g.wait_ge(s_zz[tp], M[("zz", ti)])
                g.wait_ge(s_xnd, M[("xnd", ti * NCH + NCH - 1)])
                for c in range(NCH):
                    if c >= 1:
                        g.wait_ge(s_scat[tp], M[("scat2", ti, c - 1)])
                    elif ti >= 2:
                        g.wait_ge(s_scat[tp], M[("scat", ti - 2)])
                    t0 = (ti * NCH + c) * 128
                    a = g.dma_start(out=xb_sb[:, c % 2, :],
                                    in_=xnd_d.ap()[t0:t0 + 128, :])
                    wv = inc(a, s_xb[c % 2], 16)
                    g.wait_ge(s_xb[c % 2], wv)
                    from concourse.bass_core_rust import IndirectOffsetOnAxis as IOA
                    inc(g.indirect_dma_start(
                        xsl_ds[tp].ap(), IOA(i12[:, tp, c, 0:1], 0),
                        xb_sb[:, c % 2, :], None), s_scat[tp], 16)
                    inc(g.indirect_dma_start(
                        xsl_ds[tp].ap(), IOA(i12[:, tp, c, 1:2], 0),
                        xb_sb[:, c % 2, :], None), s_scat[tp], 16)
                    M[("scat2", ti, c)] = val(s_scat[tp])
                M[("scat", ti)] = val(s_scat[tp])

        def expert_emit(ti, e):
            ge = ti * NE + e
            tb = ti % 2
            tp = ti % 2
            pe = e % 2

            @blk.tensor
            def _(t):
                t.wait_ge(s_w1, 32 * (ge + 1))
                t.wait_ge(s_sxt[pe], M[("sxt", ge)])
                for fc in range(NFC):
                    pp = (fc % 2) * 512
                    rk = 16 * ge + fc - 1
                    if rk > 0:
                        t.wait_ge(s_re, rk)
                    for dc in range(NDC):
                        last = t.matmul(ps_h[:, pp:pp + CAP],
                                        w1bf[:, dc, fc * 128:(fc + 1) * 128],
                                        sx_fm[:, pe, dc, :],
                                        start=(dc == 0), stop=(dc == NDC - 1))
                    M[("hps", ge, fc)] = inc(last, s_t)
                ge_w1free[ge + 1] = val(s_t)
                t.wait_ge(s_w2, 64 * (ge + 1))
                t.wait_ge(s_re, 16 * ge + 16)
                if e == 0:
                    t.wait_ge(s_v, val(s_v))
                for tc in range(CAP // 128):
                    if 2 * ge + tc - 1 > 0:
                        t.wait_ge(s_ye, 2 * ge + tc - 1)
                    ps = ps_ya if tc % 2 == 0 else ps_yb
                    for hf in range(2):
                        sl = slice(hf * 512, hf * 512 + 512)
                        for fc in range(NFC):
                            last = t.matmul(ps[:, sl],
                                            h_sb[:, fc, tc * 128:(tc + 1) * 128],
                                            w2bf[:, fc, sl],
                                            start=(fc == 0), stop=(fc == NFC - 1))
                    M[("yps", ge, tc)] = inc(last, s_t)
                ge_w2free[ge + 1] = val(s_t)

            @blk.scalar
            def _(s):
                for fc in range(NFC):
                    s.wait_ge(s_t, M[("hps", ge, fc)])
                    pp = (fc % 2) * 512
                    inc(s.activation(h_sb[:, fc, :], ps_h[:, pp:pp + CAP],
                                     AF.Relu), s_re)

            @blk.vector
            def _(v):
                for tc in range(CAP // 128):
                    v.wait_ge(s_t, M[("yps", ge, tc)])
                    if ("ysl", ge - 1, tc) in M:
                        ptp = ((ge - 1) // NE) % 2
                        v.wait_ge(s_ysl2[ptp * 2 + tc], M[("ysl", ge - 1, tc)])
                    ps = ps_ya if tc % 2 == 0 else ps_yb
                    M[("yev", ge, tc)] = inc(
                        v.tensor_copy(ysl_sb[:, tc, :], ps[:]), s_ye)

            @blk.sync
            def _(sy):
                for tc in range(CAP // 128):
                    sy.wait_ge(s_ye, M[("yev", ge, tc)])
                    r0 = e * CAP + tc * 128
                    M[("ysl", ge, tc)] = inc(
                        sy.dma_start(out=ysl_ds[tp].ap()[r0:r0 + 128, :],
                                     in_=ysl_sb[:, tc, :]), s_ysl2[tp * 2 + tc], 16)

            @blk.gpsimd
            def _(g):
                nge = ge + 1
                if nge >= NTILE * NE:
                    return
                ne_ = nge % NE
                g.wait_ge(s_t, ge_w1free[nge])
                w1r = w1_in.ap()[ne_].rearrange("(dc p) f -> p dc f", p=128)
                inc(g.dma_start(out=w1bf[:, 0:4, :], in_=w1r[:, 0:4, :]), s_w1, 16)
                inc(g.dma_start(out=w1bf[:, 4:8, :], in_=w1r[:, 4:8, :]), s_w1, 16)
                g.wait_ge(s_t, ge_w2free[nge])
                w2r = w2_in.ap()[ne_].rearrange("(fc p) d -> p fc d", p=128)
                for q in range(4):
                    inc(g.dma_start(out=w2bf[:, 4 * q:4 * q + 4, :],
                                    in_=w2r[:, 4 * q:4 * q + 4, :]), s_w2, 16)
            sx_prefetch(ti, e + 1)

        def final_emit(ti):
            tb = ti % 2
            tp = ti % 2
            toff = ti * TT
            from concourse.bass_core_rust import IndirectOffsetOnAxis as IOA
            for c in range(NCH):
                t0 = toff + c * 128

                @blk.gpsimd
                def _(g, c=c):
                    if c == 0:
                        for tc in range(2):
                            g.wait_ge(s_ysl2[tp * 2 + tc],
                                      M[("ysl", ti * NE + NE - 1, tc)])
                    if ("fin", ti, c - 2) in M:
                        g.wait_ge(s_v, M[("fin", ti, c - 2)])
                    elif ti >= 1 and ("fin", ti - 1, c + 2) in M:
                        g.wait_ge(s_v, M[("fin", ti - 1, c + 2)])
                    if ("fin", ti, c - 1) in M:
                        g.wait_ge(s_v, M[("fin", ti, c - 1)])
                    elif ti >= 1 and c == 0:
                        g.wait_ge(s_v, M[("fin", ti - 1, NCH - 1)])
                    inc(g.indirect_dma_start(
                        y1g[:], None,
                        ysl_ds[tp].ap(), IOA(i12[:, tp, c, 0:1], 0)), s_yg[c % 2], 16)
                    M[("gath", ti, c)] = inc(g.indirect_dma_start(
                        y2g[:], None,
                        ysl_ds[tp].ap(), IOA(i12[:, tp, c, 1:2], 0)), s_yg[c % 2], 16)

                @blk.tensor
                def _(t, c=c, t0=t0):
                    t.wait_ge(s_v, M[("scan", ti)])
                    if c < 2:
                        t.wait_ge(s_ye, val(s_ye))
                    else:
                        t.wait_ge(s_v, M[("fin", ti, c - 2)])
                    ps = ps_ya if c % 2 == 0 else ps_yb
                    last = None
                    for hf in range(2):
                        sl = slice(hf * 512, hf * 512 + 512)
                        last = t.matmul(ps[:, sl],
                                        st_bf[:, tb, c * 128:(c + 1) * 128],
                                        mtbf[:, sl], start=True, stop=True)
                    M[("ssm", ti, c)] = inc(last, s_t)

                @blk.vector
                def _(v, c=c):
                    v.wait_ge(s_t, M[("ssm", ti, c)])
                    v.wait_ge(s_yg[c % 2], M[("gath", ti, c)])
                    if ("store", c % 2) in M:
                        v.wait_ge(s_outs[c % 2], M[("store", c % 2)])
                    v.drain()
                    ps = ps_ya if c % 2 == 0 else ps_yb
                    v.scalar_tensor_tensor(outst[:, c % 2, :], y1g[:],
                                           c12[:, tp, c, 0:1], oacc[:, tb, c, :],
                                           ALU.mult, ALU.add)
                    v.drain()
                    v.scalar_tensor_tensor(outst[:, c % 2, :], y2g[:],
                                           c12[:, tp, c, 1:2], outst[:, c % 2, :],
                                           ALU.mult, ALU.add)
                    v.drain()
                    M[("fin", ti, c)] = inc(
                        v.tensor_add(outst[:, c % 2, :], outst[:, c % 2, :], ps[:]),
                        s_v)

                @blk.sync
                def _(sy, c=c, t0=t0):
                    sy.wait_ge(s_v, M[("fin", ti, c)])
                    M[("store", c % 2)] = inc(
                        sy.dma_start(out=out_d.ap()[t0:t0 + 128, :],
                                     in_=outst[:, c % 2, :]), s_outs[c % 2], 16)
                stores[0] += 1

        zero_emit(0)
        for c in range(NCH):
            s1VA(0, c)
            s1SB(0, c)
            s1VC(0, c)
            s1B1(0, c)
            s1B2(0, c)
        scan_emit(0)
        idx_emit(0)
        sx_prefetch(0, 0)
        for ti in range(NTILE):
            for e in range(NE):
                expert_emit(ti, e)
                if ti + 1 < NTILE:
                    if e == 0:
                        zero_emit(ti + 1)
                    if 2 <= e <= 5:
                        s1B1(ti + 1, e - 2)
                    if 3 <= e <= 6:
                        s1B2(ti + 1, e - 3)
                    if e <= 3:
                        s1VA(ti + 1, e)
                    if 1 <= e <= 4:
                        s1SB(ti + 1, e - 1)
                    if 1 <= e <= 4:
                        s1VC(ti + 1, e - 1)
                    if e == 7:
                        scan_emit(ti + 1)
                        idx_emit(ti + 1)
                        sx_prefetch(ti + 1, 0)
            final_emit(ti)

        @blk.sync
        def _(sy):
            sy.wait_ge(s_outs[0], val(s_outs[0]))
            sy.wait_ge(s_outs[1], val(s_outs[1]))

    return nc


_NC_CACHE = [None]


def kernel(**inputs):
    x = np.ascontiguousarray(np.asarray(inputs["L0_attn_x"], dtype=np.float32))
    W1 = np.ascontiguousarray(np.asarray(inputs["L0_ffn_W1"], dtype=np.float32))
    W2 = np.ascontiguousarray(np.asarray(inputs["L0_ffn_W2"], dtype=np.float32))
    Wg = np.asarray(inputs["L0_ffn_Wg"], dtype=np.float32)
    W_in = np.ascontiguousarray(np.asarray(inputs["L0_attn_W_in"], dtype=np.float32))
    A = np.asarray(inputs["L0_attn_A"], dtype=np.float32)
    B = np.asarray(inputs["L0_attn_B_ssm"], dtype=np.float32)
    C = np.asarray(inputs["L0_attn_C_ssm"], dtype=np.float32)
    W_out = np.asarray(inputs["L0_attn_W_out"], dtype=np.float32)
    b_in = np.asarray(inputs["L0_attn_b_in"], dtype=np.float32)
    b_out = np.asarray(inputs["L0_attn_b_out"], dtype=np.float32)
    norm_w = np.asarray(inputs["L0_attn_norm_w"], dtype=np.float32)
    norm_b = np.asarray(inputs["L0_attn_norm_b"], dtype=np.float32)

    AB = np.concatenate([A, B], axis=1)
    ab_l = np.ascontiguousarray(AB.reshape(16, 128, 32).transpose(1, 0, 2))
    c_l = np.ascontiguousarray(C.reshape(16, 128, 16).transpose(1, 0, 2))
    woutt = np.ascontiguousarray(W_out.T)
    wg_l = np.ascontiguousarray(Wg.reshape(8, 128, 8).transpose(1, 0, 2))
    bin_l = np.ascontiguousarray(b_in.reshape(16, 128).T)
    nw_l = np.ascontiguousarray(norm_w.reshape(8, 128).T)
    nb_l = np.ascontiguousarray(norm_b.reshape(8, 128).T)
    bout_l = np.ascontiguousarray(b_out.reshape(1, 1024))
    basec = np.zeros((32, 1), np.float32)
    capc = np.zeros((32, 1), np.float32)
    basec[0:8, 0] = np.arange(8, dtype=np.float32) * 256.0
    capc[0:8, 0] = basec[0:8, 0] + 255.0

    if _NC_CACHE[0] is None:
        _NC_CACHE[0] = _build()
    nc = _NC_CACHE[0]

    in_maps = []
    for i in range(8):
        xi = np.ascontiguousarray(x[2 * i:2 * i + 2].reshape(T, D))
        in_maps.append({
            "x": xi, "w1": W1, "w2": W2, "wg": wg_l, "win": W_in,
            "ab": ab_l, "cm": c_l, "woutt": woutt, "binc": bin_l,
            "nwc": nw_l, "nbc": nb_l, "bout": bout_l,
            "basec": basec, "capc": capc,
        })
    res = run_bass_kernel_spmd(nc, in_maps, core_ids=list(range(8)))
    out = np.stack([np.asarray(r["out"]) for r in res.results])
    return np.ascontiguousarray(out.reshape(16, 2048, 1024)).astype(np.float32, copy=False)


# version 18
# speedup vs baseline: 1.1990x; 1.1990x over previous
import numpy as np
import concourse.bass as bass
import concourse.mybir as mybir
from concourse.bass_utils import run_bass_kernel_spmd
from contextlib import ExitStack

F32 = mybir.dt.float32
BF16 = mybir.dt.bfloat16
AF = mybir.ActivationFunctionType
ALU = mybir.AluOpType
AX = mybir.AxisListType

D = 1024
DI = 2048
NS = 16
T = 4096
TT = 512
NTILE = T // TT
NCH = TT // 128
NE = 8
NDC = D // 128
NFC = DI // 128
NEC = DI // 128
CAP = 256
SLOTS = NE * CAP
EPS = 1e-5
R1024 = 1.0 / 1024.0
EPSK = 1024.0 * EPS
K2 = 1024.0 * EPS * EPS
HL1024 = 3.4657359027997265


def _build():
    nc = bass.Bass("TRN2", target_bir_lowering=False)

    x_in = nc.dram_tensor("x", [T, D], F32, kind="ExternalInput")
    w1_in = nc.dram_tensor("w1", [NE, D, DI], F32, kind="ExternalInput")
    w2_in = nc.dram_tensor("w2", [NE, DI, D], F32, kind="ExternalInput")
    wg_in = nc.dram_tensor("wg", [128, NDC, NE], F32, kind="ExternalInput")
    win_in = nc.dram_tensor("win", [DI, D], F32, kind="ExternalInput")
    ab_in = nc.dram_tensor("ab", [128, NEC, 32], F32, kind="ExternalInput")
    c_in = nc.dram_tensor("cm", [128, NEC, NS], F32, kind="ExternalInput")
    woutt_in = nc.dram_tensor("woutt", [DI, D], F32, kind="ExternalInput")
    bin_in = nc.dram_tensor("binc", [128, NEC], F32, kind="ExternalInput")
    nw_in = nc.dram_tensor("nwc", [128, NDC], F32, kind="ExternalInput")
    nb_in = nc.dram_tensor("nbc", [128, NDC], F32, kind="ExternalInput")
    bout_in = nc.dram_tensor("bout", [1, D], F32, kind="ExternalInput")
    basec_in = nc.dram_tensor("basec", [32, 1], F32, kind="ExternalInput")
    capc_in = nc.dram_tensor("capc", [32, 1], F32, kind="ExternalInput")
    identf_in = nc.dram_tensor("identf", [128, 128], F32, kind="ExternalInput")
    out_d = nc.dram_tensor("out", [T, D], F32, kind="ExternalOutput")
    xnd_d = nc.dram_tensor("xnd", [T, D], BF16)
    xsl_ds = [nc.dram_tensor("xsl%d" % j, [SLOTS, D], BF16) for j in range(2)]
    ysl_ds = [nc.dram_tensor("ysl%d" % j, [SLOTS, D], BF16) for j in range(2)]

    cnt = {}

    def inc(ins, sem, n=1):
        ins.then_inc(sem, n)
        k = id(sem)
        cnt[k] = cnt.get(k, 0) + n
        return cnt[k]

    def val(sem):
        return cnt.get(id(sem), 0)

    with ExitStack() as ctx:
        E = ctx.enter_context
        xnhi_fm = E(nc.sbuf_tensor("xnhi_fm", [128, 3, NDC, 128], BF16, is_transpose_dest=True))
        xnlo_fm = E(nc.sbuf_tensor("xnlo_fm", [128, NDC, 128], BF16, is_transpose_dest=True))
        w1bf = E(nc.sbuf_tensor("w1bf", [128, NDC, DI], BF16))
        w2bf = E(nc.sbuf_tensor("w2bf", [128, NFC, D], BF16))
        h_sb = E(nc.sbuf_tensor("h_sb", [128, NFC, CAP], BF16))
        sx_fm = E(nc.sbuf_tensor("sx_fm", [128, 2, NDC, CAP], BF16))
        xg_sb = E(nc.sbuf_tensor("xg_sb", [128, 2, D], BF16))
        identb = E(nc.sbuf_tensor("identb", [128, 128], BF16))
        ysl_sb = E(nc.sbuf_tensor("ysl_sb", [128, 2, D], BF16))
        y1g = E(nc.sbuf_tensor("y1g", [128, D], BF16))
        y2g = E(nc.sbuf_tensor("y2g", [128, D], BF16))
        zz_sb = E(nc.sbuf_tensor("zz_sb", [128, D], BF16))
        ind32 = E(nc.sbuf_tensor("ind32", [128, 32], F32))
        indT = E(nc.sbuf_tensor("indT", [32, TT], F32))
        zeroT = E(nc.sbuf_tensor("zeroT", [32, TT], F32))
        cnt_fm = E(nc.sbuf_tensor("cnt_fm", [32, TT], F32))
        stm = E(nc.sbuf_tensor("stm", [128, 32], F32))
        eq4 = E(nc.sbuf_tensor("eq4", [128, NCH, NE], F32))
        ind4 = E(nc.sbuf_tensor("ind4", [128, NCH, NE], F32))
        tsa = E(nc.sbuf_tensor("tsa", [128, NE], F32))
        tsb = E(nc.sbuf_tensor("tsb", [128, NE], F32))
        s12f = E(nc.sbuf_tensor("s12f", [128, 2], F32))
        i12 = E(nc.sbuf_tensor("i12", [128, 2, NCH, 2], mybir.dt.int32))
        c12 = E(nc.sbuf_tensor("c12", [128, 2, NCH, 2], F32))
        basec_s = E(nc.sbuf_tensor("basec_s", [32, 1], F32))
        capc_s = E(nc.sbuf_tensor("capc_s", [32, 1], F32))
        oacc = E(nc.sbuf_tensor("oacc", [128, 2, NCH, D], F32))
        xst = E(nc.sbuf_tensor("xst", [128, 3, D], F32))
        outst = E(nc.sbuf_tensor("outst", [128, 2, D], F32))
        xn_tm = E(nc.sbuf_tensor("xn_tm", [128, D], F32))
        xc_s = E(nc.sbuf_tensor("xc_s", [128, 3, D], F32))
        sq_s = E(nc.sbuf_tensor("sq_s", [128, D], F32))
        xnhi_tm = E(nc.sbuf_tensor("xnhi_tm", [128, D], BF16))
        xnlo_tm = E(nc.sbuf_tensor("xnlo_tm", [128, D], BF16))
        auf_a = E(nc.sbuf_tensor("auf_a", [32, TT], F32))
        auf_u = E(nc.sbuf_tensor("auf_u", [32, TT], F32))
        st_f = E(nc.sbuf_tensor("st_f", [NS, TT], F32))
        st_bf = E(nc.sbuf_tensor("st_bf", [NS, 2, TT], BF16))
        carry_s = E(nc.sbuf_tensor("carry_s", [NS, 1], F32))
        au_tm = E(nc.sbuf_tensor("au_tm", [128, 128], F32))
        apre = E(nc.sbuf_tensor("apre", [128, 2, 32], F32))
        term = E(nc.sbuf_tensor("term", [128, 32], F32))
        sn1 = E(nc.sbuf_tensor("sn1", [128, NS], F32))
        sn2 = E(nc.sbuf_tensor("sn2", [128, NS], F32))
        lg_sb = E(nc.sbuf_tensor("lg_sb", [128, 2, NE], F32))
        eq_sb = E(nc.sbuf_tensor("eq_sb", [128, NE], F32))
        msk_sb = E(nc.sbuf_tensor("msk_sb", [128, NE], F32))
        ind_sb = E(nc.sbuf_tensor("ind_sb", [128, NE], F32))
        e_sb = E(nc.sbuf_tensor("e_sb", [128, NE], F32))
        c_sb = E(nc.sbuf_tensor("c_sb", [128, 2, NCH, NE], F32))
        sum_s = E(nc.sbuf_tensor("sum_s", [128, 1], F32))
        mu_s = E(nc.sbuf_tensor("mu_s", [128, 1], F32))
        tq_s = E(nc.sbuf_tensor("tq_s", [128, 1], F32))
        q_s = E(nc.sbuf_tensor("q_s", [128, 1], F32))
        vs_s = E(nc.sbuf_tensor("vs_s", [128, 3], F32))
        sxc_s = E(nc.sbuf_tensor("sxc_s", [128, 4], F32))
        w_s = E(nc.sbuf_tensor("w_s", [128, 3], F32))
        rs_s = E(nc.sbuf_tensor("rs_s", [128, 3], F32))
        u1_s = E(nc.sbuf_tensor("u1_s", [128, 1], F32))
        hu1_s = E(nc.sbuf_tensor("hu1_s", [128, 1], F32))
        u2_s = E(nc.sbuf_tensor("u2_s", [128, 1], F32))
        mu2_s = E(nc.sbuf_tensor("mu2_s", [128, 1], F32))
        rs2_c = E(nc.sbuf_tensor("rs2_c", [128, NCH], F32))
        rm2_c = E(nc.sbuf_tensor("rm2_c", [128, NCH], F32))
        m1_s = E(nc.sbuf_tensor("m1_s", [128, 2], F32))
        nm1_s = E(nc.sbuf_tensor("nm1_s", [128, 2], F32))
        m2_s = E(nc.sbuf_tensor("m2_s", [128, 2], F32))
        e2_s = E(nc.sbuf_tensor("e2_s", [128, 1], F32))
        den_s = E(nc.sbuf_tensor("den_s", [128, 1], F32))
        rden_s = E(nc.sbuf_tensor("rden_s", [128, 1], F32))
        epsk_s = E(nc.sbuf_tensor("epsk_s", [128, 1], F32))
        wgbf = E(nc.sbuf_tensor("wgbf", [128, NDC, NE], BF16))

        abhi = E(nc.sbuf_tensor("abhi", [128, NEC, 32], BF16))
        ablo = E(nc.sbuf_tensor("ablo", [128, NEC, 32], BF16))

        chi = E(nc.sbuf_tensor("chi", [128, NEC, NS], BF16))
        clo = E(nc.sbuf_tensor("clo", [128, NEC, NS], BF16))

        binhi = E(nc.sbuf_tensor("binhi", [128, NEC], BF16))
        binlo = E(nc.sbuf_tensor("binlo", [128, NEC], BF16))
        nw_f = E(nc.sbuf_tensor("nw_f", [128, NDC], F32))
        ab_f = xst[:, 0, 0:512].rearrange("p (a b) -> p a b", b=32)
        c_f = xst[:, 0, 512:768].rearrange("p (a b) -> p a b", b=16)
        bin_f = xst[:, 1, 0:16]
        nb_f = xst[:, 1, 16:24]

        nbhi = E(nc.sbuf_tensor("nbhi", [128, NDC], BF16))
        nblo = E(nc.sbuf_tensor("nblo", [128, NDC], BF16))
        bout_rep = E(nc.sbuf_tensor("bout_rep", [128, D], BF16))
        onesr_bf = E(nc.sbuf_tensor("onesr_bf", [1, 128], BF16))
        onec_bf = E(nc.sbuf_tensor("onec_bf", [128, 1], BF16))
        vwhi = E(nc.sbuf_tensor("vwhi", [1, 32], BF16))
        vwlo = E(nc.sbuf_tensor("vwlo", [1, 32], BF16))
        cbhi = E(nc.sbuf_tensor("cbhi", [1, 32], BF16))
        cblo = E(nc.sbuf_tensor("cblo", [1, 32], BF16))
        cab_f = E(nc.sbuf_tensor("cab_f", [128, NDC, 32], F32))
        cabhi = E(nc.sbuf_tensor("cabhi", [128, NDC, 32], BF16))
        cablo = E(nc.sbuf_tensor("cablo", [128, NDC, 32], BF16))
        cabw_f = E(nc.sbuf_tensor("cabw_f", [128, NDC, 32], F32))
        cabwhi = E(nc.sbuf_tensor("cabwhi", [128, NDC, 32], BF16))
        cabwlo = E(nc.sbuf_tensor("cabwlo", [128, NDC, 32], BF16))
        vwrow = E(nc.sbuf_tensor("vwrow", [1, 32], F32))
        cbrow = E(nc.sbuf_tensor("cbrow", [1, 32], F32))
        vwrep = E(nc.sbuf_tensor("vwrep", [128, 32], F32))
        cbrep = E(nc.sbuf_tensor("cbrep", [128, 32], F32))
        mtbf = E(nc.sbuf_tensor("mtbf", [NS, D], BF16))
        ps_h = E(nc.psum_tensor("ps_h", [128, 1024], F32))
        ps_ya = E(nc.psum_tensor("ps_ya", [128, 1024], F32))
        ps_yb = E(nc.psum_tensor("ps_yb", [128, 1024], F32))
        ps_misc = E(nc.psum_tensor("ps_misc", [128, 512], F32))
        ps_trx = E(nc.psum_tensor("ps_trx", [128, 1024], BF16))
        ps_ap = ps_misc[:, 256:288]
        ps_lg = ps_misc[:, 128:136]
        s_c = E(nc.semaphore("s_c"))
        s_xs = [E(nc.semaphore("s_x%d" % j)) for j in range(4)]
        s_tps = [E(nc.semaphore("s_tp%d" % j)) for j in range(8)]
        s_outs = [E(nc.semaphore("s_out%d" % j)) for j in range(2)]
        s_wg = E(nc.semaphore("s_wg"))
        s_w1 = E(nc.semaphore("s_w1"))
        s_w2 = E(nc.semaphore("s_w2"))
        s_t = E(nc.semaphore("s_t"))
        s_v = E(nc.semaphore("s_v"))
        s_sc = E(nc.semaphore("s_sc"))
        s_ye = E(nc.semaphore("s_ye"))
        s_re = E(nc.semaphore("s_re"))
        s_zz = [E(nc.semaphore("s_zz%d" % j)) for j in range(2)]
        s_scat = [E(nc.semaphore("s_scat%d" % j)) for j in range(2)]
        s_xg = E(nc.semaphore("s_xg"))
        s_sxc = E(nc.semaphore("s_sxc"))
        s_ysl2 = [E(nc.semaphore("s_ysl%d" % j)) for j in range(4)]
        s_yg = [E(nc.semaphore("s_yg%d" % j)) for j in range(2)]
        s_xb = [E(nc.semaphore("s_xb%d" % j)) for j in range(2)]
        s_xnd = E(nc.semaphore("s_xnd"))

        M = {}
        LATE = {"lgcp": 0, "m6": 0, "apmm": 0}
        blk = E(nc.Block())

        @blk.sync
        def _(sy):
            pass
            inc(sy.dma_start(out=nw_f[:], in_=nw_in.ap()), s_c, 16)
            inc(sy.dma_start(out=nb_f, in_=nb_in.ap()), s_c, 16)
            inc(sy.dma_start(out=bin_f, in_=bin_in.ap()), s_c, 16)
            inc(sy.dma_start(out=ab_f, in_=ab_in.ap()), s_c, 16)
            inc(sy.dma_start(out=c_f, in_=c_in.ap()), s_c, 16)
            inc(sy.dma_start(out=basec_s[:], in_=basec_in.ap()), s_c, 16)
            inc(sy.dma_start(out=capc_s[:], in_=capc_in.ap()), s_c, 16)
            M["consts"] = val(s_c)

        @blk.gpsimd
        def _(g):
            inc(g.dma_start(out=bout_rep[:],
                            in_=bout_in.ap()[0:1, :].partition_broadcast(128)), s_wg, 16)
            inc(g.dma_start(out=wgbf[:], in_=wg_in.ap()), s_wg, 16)
            inc(g.dma_start(out=identb[:], in_=identf_in.ap()), s_wg, 16)
            w1r = w1_in.ap()[0].rearrange("(dc p) f -> p dc f", p=128)
            inc(g.dma_start(out=w1bf[:, 0:4, :], in_=w1r[:, 0:4, :]), s_w1, 16)
            inc(g.dma_start(out=w1bf[:, 4:8, :], in_=w1r[:, 4:8, :]), s_w1, 16)
            w2r = w2_in.ap()[0].rearrange("(fc p) d -> p fc d", p=128)
            for q in range(4):
                inc(g.dma_start(out=w2bf[:, 4 * q:4 * q + 4, :],
                                in_=w2r[:, 4 * q:4 * q + 4, :]), s_w2, 16)

        @blk.vector
        def _(v):
            v.memset(onec_bf[:], 1.0)
            v.memset(onesr_bf[:], 1.0)
            v.memset(au_tm[:, 16:32], 0.0)
            v.memset(au_tm[:, 48:64], 0.0)
            v.memset(epsk_s[:], EPSK)
            v.memset(zz_sb[:], 0.0)
            v.memset(zeroT[:], 0.0)
            v.memset(ind32[:, 8:32], 0.0)

        @blk.scalar
        def _(s):
            s.wait_ge(s_c, M["consts"])
            s.copy(abhi[:], ab_f)
            s.copy(chi[:], c_f)
            s.copy(binhi[:], bin_f)
            inc(s.copy(nbhi[:], nb_f), s_sc)
            M["hi_consts"] = val(s_sc)

        @blk.vector
        def _(v):
            v.wait_ge(s_sc, M["hi_consts"])
            v.tensor_sub(ablo[:], ab_f, abhi[:])
            v.tensor_sub(clo[:], c_f, chi[:])
            v.tensor_sub(binlo[:], bin_f, binhi[:])
            inc(v.tensor_sub(nblo[:], nb_f, nbhi[:]), s_v)
            M["lo_consts"] = val(s_v)

        caps = [ps_h[:, 0:32], ps_h[:, 512:544],
                ps_ya[:, 0:32], ps_ya[:, 512:544],
                ps_yb[:, 0:32], ps_yb[:, 512:544],
                ps_misc[:, 0:32], ps_misc[:, 128:160]]
        for ec in range(NEC):
            @blk.sync
            def _(sy, ec=ec):
                if ec >= 2:
                    sy.wait_ge(s_sc, M[("whi", ec - 2)])
                    sy.wait_ge(s_v, M[("wlo", ec - 2)])
                else:
                    sy.wait_ge(s_sc, M["hi_consts"])
                    sy.wait_ge(s_v, M["lo_consts"])
                a = sy.dma_start(out=xst[:, ec % 2, :],
                                 in_=win_in.ap()[ec * 128:(ec + 1) * 128, :])
                M[("wld", ec)] = inc(a, s_xs[ec % 2], 16)

            @blk.scalar
            def _(s, ec=ec):
                s.wait_ge(s_xs[ec % 2], M[("wld", ec)])
                if ec >= 1:
                    s.wait_ge(s_t, M[("cabmm", ec - 1)])
                M[("whi", ec)] = inc(s.copy(xnhi_tm[:], xst[:, ec % 2, :]), s_sc)

            @blk.vector
            def _(v, ec=ec):
                v.wait_ge(s_sc, M[("whi", ec)])
                M[("wlo", ec)] = inc(
                    v.tensor_sub(xnlo_tm[:], xst[:, ec % 2, :], xnhi_tm[:]), s_v)

            @blk.tensor
            def _(t, ec=ec):
                t.wait_ge(s_v, M[("wlo", ec)])
                if ec == 0:
                    t.wait_ge(s_sc, M["hi_consts"])
                    t.wait_ge(s_v, M["lo_consts"])
                last = None
                for dc in range(NDC):
                    for p, (lh, rh) in enumerate([
                            (xnhi_tm, abhi), (xnlo_tm, abhi), (xnhi_tm, ablo)]):
                        last = t.matmul(caps[dc], lh[:, dc * 128:(dc + 1) * 128],
                                        rh[:, ec, :],
                                        start=(ec == 0 and p == 0 and dc != 7),
                                        stop=(ec == NEC - 1 and p == 2
                                              and dc != 6))
                M[("cabmm", ec)] = inc(last, s_t)

        @blk.vector
        def _(v):
            v.wait_ge(s_t, M[("cabmm", NEC - 1)])
            for dc in range(NDC):
                v.tensor_copy(cab_f[:, dc, :], caps[dc])
            v.drain()
            last = None
            for dc in range(NDC):
                last = v.tensor_scalar(cabw_f[:, dc, :], cab_f[:, dc, :],
                                       nw_f[:, dc:dc + 1], None, ALU.mult)
            M["cab_f"] = inc(last, s_v)

        @blk.scalar
        def _(s):
            s.wait_ge(s_v, M["cab_f"])
            s.copy(cabhi[:], cab_f[:])
            M["cabhi"] = inc(s.copy(cabwhi[:], cabw_f[:]), s_sc)

        @blk.vector
        def _(v):
            v.wait_ge(s_sc, M["cabhi"])
            v.tensor_sub(cablo[:], cab_f[:], cabhi[:])
            M["cablo"] = inc(v.tensor_sub(cabwlo[:], cabw_f[:], cabwhi[:]), s_v)

        @blk.tensor
        def _(t):
            t.wait_ge(s_v, M["cablo"])
            for dc in range(NDC):
                for rh in (cabwhi, cabwlo):
                    t.matmul(ps_ap[0:1, :], onec_bf[:], rh[:, dc, :],
                             start=(dc == 0 and rh is cabwhi), stop=False)
            last2 = None
            for dc in range(NDC):
                for lh, rh in [(nbhi, cabhi), (nblo, cabhi), (nbhi, cablo)]:
                    last2 = t.matmul(ps_misc[0:1, 0:32], lh[:, dc:dc + 1],
                                     rh[:, dc, :], start=False, stop=False)
            for ec in range(NEC):
                for p, (lh, rh) in enumerate([(binhi, abhi), (binlo, abhi),
                                              (binhi, ablo)]):
                    last2 = t.matmul(ps_misc[0:1, 0:32], lh[:, ec:ec + 1],
                                     rh[:, ec, :], start=False,
                                     stop=(ec == NEC - 1 and p == 2))
            M["vwcb"] = inc(last2, s_t)

        @blk.vector
        def _(v):
            v.wait_ge(s_t, M["vwcb"])
            v.tensor_copy(vwrow[:], ps_ap[0:1, :])
            M["rows"] = inc(v.tensor_copy(cbrow[:], ps_misc[0:1, 0:32]), s_v)

        @blk.scalar
        def _(s):
            s.wait_ge(s_v, M["rows"])
            s.copy(vwhi[:], vwrow[:])
            M["rephi"] = inc(s.copy(cbhi[:], cbrow[:]), s_sc)

        @blk.vector
        def _(v):
            v.wait_ge(s_sc, M["rephi"])
            v.tensor_sub(vwlo[:], vwrow[:], vwhi[:])
            M["replo"] = inc(v.tensor_sub(cblo[:], cbrow[:], cbhi[:]), s_v)

        @blk.tensor
        def _(t):
            t.wait_ge(s_v, M["replo"])
            t.matmul(ps_ap[:], onesr_bf[:], vwhi[:], start=True, stop=False)
            M["rep1"] = inc(t.matmul(ps_ap[:], onesr_bf[:], vwlo[:],
                                     start=False, stop=True), s_t)

        @blk.vector
        def _(v):
            v.wait_ge(s_t, M["rep1"])
            M["vwrep"] = inc(v.tensor_copy(vwrep[:], ps_ap[:]), s_v)

        @blk.tensor
        def _(t):
            t.wait_ge(s_v, M["vwrep"])
            t.matmul(ps_ap[:], onesr_bf[:], cbhi[:], start=True, stop=False)
            M["rep2"] = inc(t.matmul(ps_ap[:], onesr_bf[:], cblo[:],
                                     start=False, stop=True), s_t)

        @blk.vector
        def _(v):
            v.wait_ge(s_t, M["rep2"])
            M["cbrep"] = inc(v.tensor_copy(cbrep[:], ps_ap[:]), s_v)

        for ec in range(NEC):
            @blk.sync
            def _(sy, ec=ec):
                sy.wait_ge(s_sc, M[("mwhi", ec - 2)] if ec >= 2 else M[("whi", NEC - 1)])
                sy.wait_ge(s_v, M[("mwlo", ec - 2)] if ec >= 2 else M[("wlo", NEC - 1)])
                a = sy.dma_start(out=xst[:, ec % 2, :],
                                 in_=woutt_in.ap()[ec * 128:(ec + 1) * 128, :])
                M[("mwld", ec)] = inc(a, s_xs[ec % 2], 16)

            @blk.scalar
            def _(s, ec=ec):
                s.wait_ge(s_xs[ec % 2], M[("mwld", ec)])
                s.wait_ge(s_t, M[("mtmm", ec - 1)] if ec >= 1 else M[("cabmm", NEC - 1)])
                M[("mwhi", ec)] = inc(s.copy(xnhi_tm[:], xst[:, ec % 2, :]), s_sc)

            @blk.vector
            def _(v, ec=ec):
                v.wait_ge(s_sc, M[("mwhi", ec)])
                M[("mwlo", ec)] = inc(
                    v.tensor_sub(xnlo_tm[:], xst[:, ec % 2, :], xnhi_tm[:]), s_v)

            @blk.tensor
            def _(t, ec=ec):
                t.wait_ge(s_v, M[("mwlo", ec)])
                if ec == 0:
                    t.wait_ge(s_v, M["cbrep"])
                last = None
                for hf in range(2):
                    sl = slice(hf * 512, hf * 512 + 512)
                    for p, (lh, rh) in enumerate([(chi, xnhi_tm), (clo, xnhi_tm),
                                                  (chi, xnlo_tm)]):
                        last = t.matmul(ps_yb[0:NS, sl], lh[:, ec, :], rh[:, sl],
                                        start=(ec == 0 and p == 0),
                                        stop=(ec == NEC - 1 and p == 2))
                M[("mtmm", ec)] = inc(last, s_t)

        @blk.vector
        def _(v):
            v.wait_ge(s_t, M[("mtmm", NEC - 1)])
            M["mtbf"] = inc(v.tensor_copy(mtbf[:], ps_yb[0:NS, :]), s_v)

        ge_w1free = {}
        ge_w2free = {}
        xfree = {}
        stores = [0]

        def s1VA(ti, c):
            g = ti * NCH + c
            gp = g % 2
            g3 = g % 3
            g4 = g % 4
            xs = g % 3
            t0 = g * 128

            @blk.sync
            def _(sy):
                if g >= 3:
                    sy.wait_ge(s_v, xfree[g - 3])
                else:
                    sy.wait_ge(s_sc, M[("mwhi", NEC - 1)])
                    sy.wait_ge(s_v, M[("mwlo", NEC - 1)])
                a = sy.dma_start(out=xst[:, xs, :], in_=x_in.ap()[t0:t0 + 128, :])
                M[("x", g)] = inc(a, s_xs[xs], 16)

            @blk.vector
            def _(v):
                v.wait_ge(s_xs[xs], M[("x", g)])
                xt = xst[:, xs, :]
                v.reduce_sum(sum_s[:], xt, AX.X)
                v.drain()
                v.tensor_scalar_mul(mu_s[:], sum_s[:], R1024)
                v.drain()
                v.tensor_scalar_sub(xc_s[:, g3, :], xt, mu_s[:])
                v.drain()
                v.tensor_mul(sq_s[:], xc_s[:, g3, :], xc_s[:, g3, :])
                v.reduce_sum(sxc_s[:, g4:g4 + 1], xc_s[:, g3, :], AX.X)
                v.drain()
                v.reduce_sum(vs_s[:, g3:g3 + 1], sq_s[:], AX.X)
                v.tensor_mul(tq_s[:], sxc_s[:, g4:g4 + 1], sxc_s[:, g4:g4 + 1])
                v.drain()
                v.scalar_tensor_tensor(q_s[:], tq_s[:], -R1024, vs_s[:, g3:g3 + 1],
                                       ALU.mult, ALU.add)
                v.drain()
                v.scalar_tensor_tensor(w_s[:, g3:g3 + 1], vs_s[:, g3:g3 + 1], EPS,
                                       q_s[:], ALU.mult, ALU.add)
                v.drain()
                M[("m1", g)] = inc(
                    v.tensor_scalar_add(w_s[:, g3:g3 + 1], w_s[:, g3:g3 + 1], K2),
                    s_v)

        def s1SB(ti, c):
            g = ti * NCH + c
            g3 = g % 3

            @blk.scalar
            def _(s):
                s.wait_ge(s_v, M[("m1", g)])
                s.activation(u1_s[:], vs_s[:, g3:g3 + 1], AF.Ln, bias=epsk_s[:])
                s.drain()
                s.mul(hu1_s[:], u1_s[:], 0.5)
                s.activation(rs_s[:, g3:g3 + 1], u1_s[:], AF.Exp, scale=-0.5)
                s.drain()
                s.mul(rs_s[:, g3:g3 + 1], rs_s[:, g3:g3 + 1], 32.0)
                s.drain()
                s.activation(u2_s[:], w_s[:, g3:g3 + 1], AF.Ln)
                s.drain()
                M[("m2", g)] = inc(
                    s.activation(rs2_c[:, c:c + 1], u2_s[:], AF.Exp,
                                 scale=-0.5, bias=hu1_s[:]), s_sc)

        def s1VC(ti, c):
            g = ti * NCH + c
            gp = g % 2
            g3 = g % 3
            g4 = g % 4
            xs = g % 3
            tb = ti % 2

            @blk.vector
            def _(v):
                v.wait_ge(s_sc, M[("m2", g)])
                if g >= 1:
                    v.wait_ge(s_tps[(g - 1) % 8], M[("tpd", g - 1)])
                    v.wait_ge(s_xnd, M[("xnd", g - 1)])
                else:
                    v.wait_ge(s_t, M[("mtmm", NEC - 1)])
                    v.wait_ge(s_c, M["consts"])
                    v.wait_ge(s_wg, 48)
                v.tensor_scalar_mul(xn_tm[:], xc_s[:, g3, :], rs_s[:, g3:g3 + 1])
                v.drain()
                v.tensor_copy(xnhi_tm[:], xn_tm[:])
                v.drain()
                v.tensor_sub(xnlo_tm[:], xn_tm[:], xnhi_tm[:])
                v.tensor_scalar_mul(mu2_s[:], sxc_s[:, g4:g4 + 1], R1024)
                v.drain()
                v.tensor_mul(mu2_s[:], mu2_s[:], rs_s[:, g3:g3 + 1])
                v.drain()
                v.tensor_mul(rm2_c[:, c:c + 1], mu2_s[:], rs2_c[:, c:c + 1])
                v.tensor_add(oacc[:, tb, c, :], xst[:, xs, :], xn_tm[:])
                v.drain()
                ii = v.tensor_add(oacc[:, tb, c, :], oacc[:, tb, c, :], bout_rep[:])
                xfree[g] = inc(ii, s_v)

            @blk.sync
            def _(sy):
                sy.wait_ge(s_v, xfree[g])
                if g >= 1:
                    sy.wait_ge(s_t, LATE["apmm"])
                stp = s_tps[g % 8]
                for dc in range(NDC):
                    inc(sy.dma_start_transpose(
                        xnhi_fm[:, g % 3, dc, :],
                        xnhi_tm[:, dc * 128:(dc + 1) * 128]), stp, 16)
                for dc in range(NDC):
                    inc(sy.dma_start_transpose(
                        xnlo_fm[:, dc, :],
                        xnlo_tm[:, dc * 128:(dc + 1) * 128]), stp, 16)
                M[("tpd", g)] = val(stp)
                M[("xnd", g)] = inc(
                    sy.dma_start(out=xnd_d.ap()[g * 128:(g + 1) * 128, :],
                                 in_=xnhi_tm[:]), s_xnd, 16)

        def s1B1(ti, c):
            g = ti * NCH + c
            tb = ti % 2
            pc = c % 2

            @blk.tensor
            def _(t):
                if g == 0:
                    t.wait_ge(s_wg, 48)
                    t.wait_ge(s_sc, M["cabhi"])
                    t.wait_ge(s_v, M["cablo"])
                    t.wait_ge(s_v, M["mtbf"])
                    t.wait_ge(s_v, M["cbrep"])
                if LATE["m6"]:
                    t.wait_ge(s_v, LATE["m6"])
                t.wait_ge(s_tps[g % 8], M[("tpd", g)])
                last = None
                for dc in range(NDC):
                    last = t.matmul(ps_lg, xnhi_fm[:, g % 3, dc, :],
                                    wgbf[:, dc, :],
                                    start=(dc == 0), stop=(dc == NDC - 1))
                M[("lgmm", g)] = inc(last, s_t)

            @blk.vector
            def _(v):
                if g >= 2:
                    v.wait_ge(s_sc, M[("m10", g - 2)])
                v.wait_ge(s_t, M[("lgmm", g)])
                M[("lgcp", g)] = inc(v.tensor_copy(lg_sb[:, pc, :], ps_lg), s_v)
                LATE["lgcp"] = M[("lgcp", g)]

            @blk.tensor
            def _(t):
                t.wait_ge(s_v, M[("lgcp", g)])
                last = None
                for p in range(3):
                    for dc in range(NDC):
                        lh = (xnhi_fm[:, g % 3, dc, :]
                              if p != 1 else xnlo_fm[:, dc, :])
                        rh = cabwhi[:, dc, :] if p != 2 else cabwlo[:, dc, :]
                        last = t.matmul(ps_ap[:], lh, rh,
                                        start=(p == 0 and dc == 0),
                                        stop=(p == 2 and dc == NDC - 1))
                M[("apmm", g)] = inc(last, s_t)
                LATE["apmm"] = M[("apmm", g)]

            @blk.vector
            def _(v):
                v.wait_ge(s_t, M[("apmm", g)])
                v.scalar_tensor_tensor(term[:], vwrep[:], rm2_c[:, c:c + 1],
                                       cbrep[:], ALU.mult, ALU.subtract)
                v.drain()
                v.scalar_tensor_tensor(apre[:, pc, :], ps_ap[:],
                                        rs2_c[:, c:c + 1], term[:],
                                        ALU.mult, ALU.subtract)
                v.drain()
                M[("m6", g)] = inc(
                    v.tensor_scalar_max(apre[:, pc, 0:NS], apre[:, pc, 0:NS],
                                        -30.0), s_v)
                LATE["m6"] = M[("m6", g)]
                v.reduce_max(m1_s[:, pc:pc + 1], lg_sb[:, pc, :], AX.X)
                v.reduce_max(nm1_s[:, pc:pc + 1], lg_sb[:, pc, :], AX.X, negate=True)
                v.drain()
                v.tensor_scalar(eq4[:, c, :], lg_sb[:, pc, :], m1_s[:, pc:pc + 1],
                                None, ALU.is_equal)
                v.drain()
                v.scalar_tensor_tensor(msk_sb[:], eq4[:, c, :], -1e30, lg_sb[:, pc, :],
                                       ALU.mult, ALU.add)
                v.drain()
                M[("m9", g)] = inc(
                    v.reduce_max(m2_s[:, pc:pc + 1], msk_sb[:], AX.X), s_v)

        def s1B2(ti, c):
            g = ti * NCH + c
            tb = ti % 2
            pc = c % 2

            @blk.scalar
            def _(s):
                if g >= 1:
                    s.wait_ge(s_v, M[("router", g - 1)])
                s.wait_ge(s_v, M[("m6", g)])
                M[("m7", g)] = inc(
                    s.activation(sn1[:], apre[:, pc, 0:NS], AF.Exp, scale=-1.0),
                    s_sc)
                s.wait_ge(s_v, M[("m9", g)])
                s.activation(e_sb[:], lg_sb[:, pc, :], AF.Exp,
                             bias=nm1_s[:, pc:pc + 1])
                M[("m10", g)] = inc(
                    s.activation(e2_s[:], m2_s[:, pc:pc + 1], AF.Exp,
                                 bias=nm1_s[:, pc:pc + 1]), s_sc)

            @blk.vector
            def _(v):
                v.wait_ge(s_sc, M[("m7", g)])
                v.tensor_copy(au_tm[:, 32:48], apre[:, pc, 16:32])
                v.tensor_scalar_add(sn2[:], sn1[:], 1.0)
                v.drain()
                v.reciprocal(au_tm[:, 0:16], sn2[:])
                v.drain()
                for rb in range(4):
                    for dst, cbk in ((auf_a, 0), (auf_u, 1)):
                        v.transpose(
                            dst[0:32, c * 128 + 32 * rb:c * 128 + 32 * rb + 32],
                            au_tm[32 * rb:32 * rb + 32, 32 * cbk:32 * cbk + 32])
                v.drain()
                v.wait_ge(s_sc, M[("m10", g)])
                v.tensor_scalar_add(den_s[:], e2_s[:], 1.0)
                v.drain()
                v.reciprocal(rden_s[:], den_s[:])
                v.tensor_scalar(ind4[:, c, :], lg_sb[:, pc, :], m2_s[:, pc:pc + 1],
                                None, ALU.is_ge)
                v.drain()
                v.tensor_copy(ind32[:, 0:8], ind4[:, c, :])
                v.drain()
                for rb in range(4):
                    v.transpose(indT[0:32, c * 128 + 32 * rb:c * 128 + 32 * rb + 32],
                                ind32[32 * rb:32 * rb + 32, 0:32])
                v.drain()
                M[("router", g)] = inc(
                    v.scalar_tensor_tensor(c_sb[:, tb, c, :], ind4[:, c, :], rden_s[:],
                                           e_sb[:], ALU.mult, ALU.mult), s_v)

        def scan_emit(ti):
            tb = ti % 2

            @blk.vector
            def _(v):
                if ti % 4 == 0:
                    v.memset(carry_s[:], 0.0)
                v.drain()
                v.tensor_tensor_scan(st_f[:], auf_a[0:NS, :], auf_u[0:NS, :],
                                     carry_s[:], ALU.mult, ALU.add)
                v.drain()
                v.tensor_copy(carry_s[:], st_f[:, TT - 1:TT])
                M[("scan", ti)] = inc(v.tensor_copy(st_bf[:, tb, :], st_f[:]), s_v)

        def sx_prefetch(ti, e):
            if e >= NE:
                return
            ge = ti * NE + e
            tp = ti % 2

            @blk.gpsimd
            def _(g):
                g.wait_ge(s_scat[tp], M[("scat", ti)])
                if ge >= 1:
                    g.wait_ge(s_t, M[("trp2", ge - 1)])
                for cc in range(CAP // 128):
                    r0 = e * CAP + cc * 128
                    inc(g.dma_start(out=xg_sb[:, cc, :],
                                    in_=xsl_ds[tp].ap()[r0:r0 + 128, :]),
                        s_xg, 16)
                M[("xg", ge)] = val(s_xg)

        def trx_batch(t, ge, cc):
            last = None
            for dc in range(NDC):
                last = t.transpose(ps_trx[:, dc * 128:(dc + 1) * 128],
                                   xg_sb[:, cc, dc * 128:(dc + 1) * 128],
                                   identb[:])
            M[("trp%d" % (cc + 1), ge)] = inc(last, s_t)

        def trx_copies(s, ge, pe):
            s.wait_ge(s_t, M[("trp1", ge)])
            if ge >= 2:
                s.wait_ge(s_t, M[("hps", ge - 2, NFC - 1)])
            for dc in range(NDC):
                inc(s.copy(sx_fm[:, pe, dc, 0:128],
                           ps_trx[:, dc * 128:(dc + 1) * 128]), s_sxc)
            s.wait_ge(s_t, M[("trp2", ge)])
            for dc in range(NDC):
                inc(s.copy(sx_fm[:, pe, dc, 128:256],
                           ps_trx[:, dc * 128:(dc + 1) * 128]), s_sxc)

        def zero_emit(ti):
            tp = ti % 2

            @blk.sync
            def _(sy):
                sy.wait_ge(s_v, M["lo_consts"])
                if ti >= 2:
                    sy.wait_ge(s_t, ge_w1free[(ti - 1) * NE])
                for r in range(SLOTS // 128):
                    inc(sy.dma_start(out=xsl_ds[tp].ap()[r * 128:(r + 1) * 128, :],
                                     in_=zz_sb[:]), s_zz[tp], 16)
                M[("zz", ti)] = val(s_zz[tp])

        def idx_emit(ti):
            tp = ti % 2
            tb = ti % 2

            @blk.vector
            def _(v):
                if ti >= 2:
                    for cc in range(NCH):
                        v.wait_ge(s_yg[cc % 2], M[("gath", ti - 2, cc)])
                v.tensor_tensor_scan(cnt_fm[:], indT[:], zeroT[:],
                                     zeroT[:, 0:1], ALU.add, ALU.add)
                v.drain()
                v.tensor_sub(indT[:], cnt_fm[:], indT[:])
                v.drain()
                v.tensor_scalar_add(indT[:], indT[:], basec_s[:])
                v.drain()
                v.tensor_scalar_min(indT[:], indT[:], capc_s[:])
                v.drain()
                for c in range(NCH):
                    for rb in range(4):
                        v.transpose(stm[32 * rb:32 * rb + 32, 0:32],
                                    indT[0:32, c * 128 + 32 * rb:c * 128 + 32 * rb + 32])
                    v.drain()
                    v.tensor_mul(tsa[:], stm[:, 0:8], eq4[:, c, :])
                    v.tensor_sub(tsb[:], ind4[:, c, :], eq4[:, c, :])
                    v.drain()
                    v.reduce_sum(s12f[:, 0:1], tsa[:], AX.X)
                    v.drain()
                    v.tensor_mul(tsa[:], stm[:, 0:8], tsb[:])
                    v.drain()
                    v.reduce_sum(s12f[:, 1:2], tsa[:], AX.X)
                    v.drain()
                    v.tensor_mul(tsa[:], c_sb[:, tb, c, :], eq4[:, c, :])
                    v.drain()
                    v.tensor_copy(i12[:, tp, c, 0:2], s12f[:])
                    v.reduce_sum(c12[:, tp, c, 0:1], tsa[:], AX.X)
                    v.drain()
                    v.tensor_mul(tsa[:], c_sb[:, tb, c, :], tsb[:])
                    v.drain()
                    v.reduce_sum(c12[:, tp, c, 1:2], tsa[:], AX.X)
                    v.drain()
                M[("idx", ti)] = inc(v.engine_nop(), s_v)

            @blk.gpsimd
            def _(g):
                g.wait_ge(s_v, M[("idx", ti)])
                g.wait_ge(s_zz[tp], M[("zz", ti)])
                g.wait_ge(s_xnd, M[("xnd", ti * NCH + NCH - 1)])
                if ti >= 1:
                    g.wait_ge(s_t, M[("trp2", ti * NE - 1)])
                from concourse.bass_core_rust import IndirectOffsetOnAxis as IOA
                for c in range(NCH):
                    if c >= 1:
                        g.wait_ge(s_scat[tp], M[("scat2", ti, c - 1)])
                    elif ti >= 2:
                        g.wait_ge(s_scat[tp], M[("scat", ti - 2)])
                    t0 = (ti * NCH + c) * 128
                    a = g.dma_start(out=xg_sb[:, c % 2, :],
                                    in_=xnd_d.ap()[t0:t0 + 128, :])
                    wv = inc(a, s_xb[c % 2], 16)
                    g.wait_ge(s_xb[c % 2], wv)
                    inc(g.indirect_dma_start(
                        xsl_ds[tp].ap(), IOA(i12[:, tp, c, 0:1], 0),
                        xg_sb[:, c % 2, :], None), s_scat[tp], 16)
                    inc(g.indirect_dma_start(
                        xsl_ds[tp].ap(), IOA(i12[:, tp, c, 1:2], 0),
                        xg_sb[:, c % 2, :], None), s_scat[tp], 16)
                    M[("scat2", ti, c)] = val(s_scat[tp])
                M[("scat", ti)] = val(s_scat[tp])

        def expert_emit(ti, e):
            ge = ti * NE + e
            tb = ti % 2
            tp = ti % 2
            pe = e % 2
            sx_prefetch(ti, e + 1)

            nxt = ge + 1 if ge + 1 < NTILE * NE else None
            npe = (e + 1) % 2

            @blk.tensor
            def _(t):
                t.wait_ge(s_w1, 32 * (ge + 1))
                t.wait_ge(s_sxc, 16 * ge + 16)
                for fc in range(NFC):
                    pp = (fc % 2) * 512
                    rk = 16 * ge + fc - 1
                    if rk > 0:
                        t.wait_ge(s_re, rk)
                    for dc in range(NDC):
                        last = t.matmul(ps_h[:, pp:pp + CAP],
                                        w1bf[:, dc, fc * 128:(fc + 1) * 128],
                                        sx_fm[:, pe, dc, :],
                                        start=(dc == 0), stop=(dc == NDC - 1))
                    M[("hps", ge, fc)] = inc(last, s_t)
                ge_w1free[ge + 1] = val(s_t)
                t.wait_ge(s_w2, 64 * (ge + 1))
                t.wait_ge(s_re, 16 * ge + 16)
                if e == 0:
                    t.wait_ge(s_v, val(s_v))
                if nxt is not None:
                    t.wait_ge(s_xg, M[("xg", nxt)])
                    trx_batch(t, nxt, 0)
                for tc in range(CAP // 128):
                    if 2 * ge + tc - 1 > 0:
                        t.wait_ge(s_ye, 2 * ge + tc - 1)
                    if tc == 1 and nxt is not None:
                        t.wait_ge(s_sxc, 16 * nxt + 8)
                        trx_batch(t, nxt, 1)
                    ps = ps_ya if tc % 2 == 0 else ps_yb
                    for hf in range(2):
                        sl = slice(hf * 512, hf * 512 + 512)
                        for fc in range(NFC):
                            last = t.matmul(ps[:, sl],
                                            h_sb[:, fc, tc * 128:(tc + 1) * 128],
                                            w2bf[:, fc, sl],
                                            start=(fc == 0), stop=(fc == NFC - 1))
                    M[("yps", ge, tc)] = inc(last, s_t)
                ge_w2free[ge + 1] = val(s_t)

            @blk.scalar
            def _(s):
                for fc in range(NFC):
                    s.wait_ge(s_t, M[("hps", ge, fc)])
                    pp = (fc % 2) * 512
                    inc(s.activation(h_sb[:, fc, :], ps_h[:, pp:pp + CAP],
                                     AF.Relu), s_re)
                if nxt is not None:
                    trx_copies(s, nxt, npe)

            @blk.vector
            def _(v):
                for tc in range(CAP // 128):
                    v.wait_ge(s_t, M[("yps", ge, tc)])
                    if ("ysl", ge - 1, tc) in M:
                        ptp = ((ge - 1) // NE) % 2
                        v.wait_ge(s_ysl2[ptp * 2 + tc], M[("ysl", ge - 1, tc)])
                    ps = ps_ya if tc % 2 == 0 else ps_yb
                    M[("yev", ge, tc)] = inc(
                        v.tensor_copy(ysl_sb[:, tc, :], ps[:]), s_ye)

            @blk.sync
            def _(sy):
                for tc in range(CAP // 128):
                    sy.wait_ge(s_ye, M[("yev", ge, tc)])
                    r0 = e * CAP + tc * 128
                    M[("ysl", ge, tc)] = inc(
                        sy.dma_start(out=ysl_ds[tp].ap()[r0:r0 + 128, :],
                                     in_=ysl_sb[:, tc, :]), s_ysl2[tp * 2 + tc], 16)

            @blk.gpsimd
            def _(g):
                nge = ge + 1
                if nge >= NTILE * NE:
                    return
                ne_ = nge % NE
                g.wait_ge(s_t, ge_w1free[nge])
                w1r = w1_in.ap()[ne_].rearrange("(dc p) f -> p dc f", p=128)
                inc(g.dma_start(out=w1bf[:, 0:4, :], in_=w1r[:, 0:4, :]), s_w1, 16)
                inc(g.dma_start(out=w1bf[:, 4:8, :], in_=w1r[:, 4:8, :]), s_w1, 16)
                g.wait_ge(s_t, ge_w2free[nge])
                w2r = w2_in.ap()[ne_].rearrange("(fc p) d -> p fc d", p=128)
                for q in range(4):
                    inc(g.dma_start(out=w2bf[:, 4 * q:4 * q + 4, :],
                                    in_=w2r[:, 4 * q:4 * q + 4, :]), s_w2, 16)

        def final_emit(ti):
            tb = ti % 2
            tp = ti % 2
            toff = ti * TT
            from concourse.bass_core_rust import IndirectOffsetOnAxis as IOA
            for c in range(NCH):
                t0 = toff + c * 128

                @blk.gpsimd
                def _(g, c=c):
                    if c == 0:
                        for tc in range(2):
                            g.wait_ge(s_ysl2[tp * 2 + tc],
                                      M[("ysl", ti * NE + NE - 1, tc)])
                    if ("fin", ti, c - 2) in M:
                        g.wait_ge(s_v, M[("fin", ti, c - 2)])
                    elif ti >= 1 and ("fin", ti - 1, c + 2) in M:
                        g.wait_ge(s_v, M[("fin", ti - 1, c + 2)])
                    if ("fin", ti, c - 1) in M:
                        g.wait_ge(s_v, M[("fin", ti, c - 1)])
                    elif ti >= 1 and c == 0:
                        g.wait_ge(s_v, M[("fin", ti - 1, NCH - 1)])
                    inc(g.indirect_dma_start(
                        y1g[:], None,
                        ysl_ds[tp].ap(), IOA(i12[:, tp, c, 0:1], 0)), s_yg[c % 2], 16)
                    M[("gath", ti, c)] = inc(g.indirect_dma_start(
                        y2g[:], None,
                        ysl_ds[tp].ap(), IOA(i12[:, tp, c, 1:2], 0)), s_yg[c % 2], 16)

                @blk.tensor
                def _(t, c=c, t0=t0):
                    t.wait_ge(s_v, M[("scan", ti)])
                    if c < 2:
                        t.wait_ge(s_ye, val(s_ye))
                    else:
                        t.wait_ge(s_v, M[("fin", ti, c - 2)])
                    ps = ps_ya if c % 2 == 0 else ps_yb
                    last = None
                    for hf in range(2):
                        sl = slice(hf * 512, hf * 512 + 512)
                        last = t.matmul(ps[:, sl],
                                        st_bf[:, tb, c * 128:(c + 1) * 128],
                                        mtbf[:, sl], start=True, stop=True)
                    M[("ssm", ti, c)] = inc(last, s_t)

                @blk.vector
                def _(v, c=c):
                    v.wait_ge(s_t, M[("ssm", ti, c)])
                    v.wait_ge(s_yg[c % 2], M[("gath", ti, c)])
                    if ("store", c % 2) in M:
                        v.wait_ge(s_outs[c % 2], M[("store", c % 2)])
                    v.drain()
                    ps = ps_ya if c % 2 == 0 else ps_yb
                    v.scalar_tensor_tensor(outst[:, c % 2, :], y1g[:],
                                           c12[:, tp, c, 0:1], oacc[:, tb, c, :],
                                           ALU.mult, ALU.add)
                    v.drain()
                    v.scalar_tensor_tensor(outst[:, c % 2, :], y2g[:],
                                           c12[:, tp, c, 1:2], outst[:, c % 2, :],
                                           ALU.mult, ALU.add)
                    v.drain()
                    M[("fin", ti, c)] = inc(
                        v.tensor_add(outst[:, c % 2, :], outst[:, c % 2, :], ps[:]),
                        s_v)

                @blk.sync
                def _(sy, c=c, t0=t0):
                    sy.wait_ge(s_v, M[("fin", ti, c)])
                    M[("store", c % 2)] = inc(
                        sy.dma_start(out=out_d.ap()[t0:t0 + 128, :],
                                     in_=outst[:, c % 2, :]), s_outs[c % 2], 16)
                stores[0] += 1

        zero_emit(0)
        for c in range(NCH):
            s1VA(0, c)
            s1SB(0, c)
            s1VC(0, c)
            s1B1(0, c)
            s1B2(0, c)
        scan_emit(0)
        idx_emit(0)
        sx_prefetch(0, 0)

        @blk.tensor
        def _(t):
            t.wait_ge(s_wg, 48)
            t.wait_ge(s_xg, M[("xg", 0)])
            trx_batch(t, 0, 0)
            t.wait_ge(s_sxc, 8)
            trx_batch(t, 0, 1)

        @blk.scalar
        def _(s):
            trx_copies(s, 0, 0)

        for ti in range(NTILE):
            for e in range(NE):
                expert_emit(ti, e)
                if ti + 1 < NTILE:
                    if e == 0:
                        zero_emit(ti + 1)
                    if 2 <= e <= 5:
                        s1B1(ti + 1, e - 2)
                    if 3 <= e <= 6:
                        s1B2(ti + 1, e - 3)
                    if e <= 3:
                        s1VA(ti + 1, e)
                    if 1 <= e <= 4:
                        s1SB(ti + 1, e - 1)
                    if 1 <= e <= 4:
                        s1VC(ti + 1, e - 1)
                    if e == 6:
                        idx_emit(ti + 1)
                        sx_prefetch(ti + 1, 0)
                    if e == 7:
                        scan_emit(ti + 1)
            final_emit(ti)

        @blk.sync
        def _(sy):
            sy.wait_ge(s_outs[0], val(s_outs[0]))
            sy.wait_ge(s_outs[1], val(s_outs[1]))

    return nc


_NC_CACHE = [None]


def kernel(**inputs):
    x = np.ascontiguousarray(np.asarray(inputs["L0_attn_x"], dtype=np.float32))
    W1 = np.ascontiguousarray(np.asarray(inputs["L0_ffn_W1"], dtype=np.float32))
    W2 = np.ascontiguousarray(np.asarray(inputs["L0_ffn_W2"], dtype=np.float32))
    Wg = np.asarray(inputs["L0_ffn_Wg"], dtype=np.float32)
    W_in = np.ascontiguousarray(np.asarray(inputs["L0_attn_W_in"], dtype=np.float32))
    A = np.asarray(inputs["L0_attn_A"], dtype=np.float32)
    B = np.asarray(inputs["L0_attn_B_ssm"], dtype=np.float32)
    C = np.asarray(inputs["L0_attn_C_ssm"], dtype=np.float32)
    W_out = np.asarray(inputs["L0_attn_W_out"], dtype=np.float32)
    b_in = np.asarray(inputs["L0_attn_b_in"], dtype=np.float32)
    b_out = np.asarray(inputs["L0_attn_b_out"], dtype=np.float32)
    norm_w = np.asarray(inputs["L0_attn_norm_w"], dtype=np.float32)
    norm_b = np.asarray(inputs["L0_attn_norm_b"], dtype=np.float32)

    AB = np.concatenate([A, B], axis=1)
    ab_l = np.ascontiguousarray(AB.reshape(16, 128, 32).transpose(1, 0, 2))
    c_l = np.ascontiguousarray(C.reshape(16, 128, 16).transpose(1, 0, 2))
    woutt = np.ascontiguousarray(W_out.T)
    wg_l = np.ascontiguousarray(Wg.reshape(8, 128, 8).transpose(1, 0, 2))
    bin_l = np.ascontiguousarray(b_in.reshape(16, 128).T)
    nw_l = np.ascontiguousarray(norm_w.reshape(8, 128).T)
    nb_l = np.ascontiguousarray(norm_b.reshape(8, 128).T)
    bout_l = np.ascontiguousarray(b_out.reshape(1, 1024))
    basec = np.zeros((32, 1), np.float32)
    capc = np.zeros((32, 1), np.float32)
    basec[0:8, 0] = np.arange(8, dtype=np.float32) * 256.0
    capc[0:8, 0] = basec[0:8, 0] + 255.0
    identf = np.eye(128, dtype=np.float32)

    if _NC_CACHE[0] is None:
        _NC_CACHE[0] = _build()
    nc = _NC_CACHE[0]

    in_maps = []
    for i in range(8):
        xi = np.ascontiguousarray(x[2 * i:2 * i + 2].reshape(T, D))
        in_maps.append({
            "x": xi, "w1": W1, "w2": W2, "wg": wg_l, "win": W_in,
            "ab": ab_l, "cm": c_l, "woutt": woutt, "binc": bin_l,
            "nwc": nw_l, "nbc": nb_l, "bout": bout_l,
            "basec": basec, "capc": capc, "identf": identf,
        })
    res = run_bass_kernel_spmd(nc, in_maps, core_ids=list(range(8)))
    out = np.stack([np.asarray(r["out"]) for r in res.results])
    return np.ascontiguousarray(out.reshape(16, 2048, 1024)).astype(np.float32, copy=False)


# version 19
# speedup vs baseline: 1.3345x; 1.1130x over previous
import numpy as np
import concourse.bass as bass
import concourse.mybir as mybir
from concourse.bass_utils import run_bass_kernel_spmd
from contextlib import ExitStack

F32 = mybir.dt.float32
BF16 = mybir.dt.bfloat16
AF = mybir.ActivationFunctionType
ALU = mybir.AluOpType
AX = mybir.AxisListType

D = 1024
DI = 2048
NS = 16
T = 4096
TT = 512
NTILE = T // TT
NCH = TT // 128
NE = 8
NDC = D // 128
NFC = DI // 128
NEC = DI // 128
CAP = 256
SLOTS = NE * CAP
EPS = 1e-5
R1024 = 1.0 / 1024.0
EPSK = 1024.0 * EPS
K2 = 1024.0 * EPS * EPS
HL1024 = 3.4657359027997265


def _build():
    nc = bass.Bass("TRN2", target_bir_lowering=False)

    x_in = nc.dram_tensor("x", [T, D], F32, kind="ExternalInput")
    w1_in = nc.dram_tensor("w1", [NE, D, DI], F32, kind="ExternalInput")
    w2_in = nc.dram_tensor("w2", [NE, DI, D], F32, kind="ExternalInput")
    wg_in = nc.dram_tensor("wg", [128, NDC, NE], F32, kind="ExternalInput")
    win_in = nc.dram_tensor("win", [DI, D], F32, kind="ExternalInput")
    ab_in = nc.dram_tensor("ab", [128, NEC, 32], F32, kind="ExternalInput")
    c_in = nc.dram_tensor("cm", [128, NEC, NS], F32, kind="ExternalInput")
    woutt_in = nc.dram_tensor("woutt", [DI, D], F32, kind="ExternalInput")
    bin_in = nc.dram_tensor("binc", [128, NEC], F32, kind="ExternalInput")
    nw_in = nc.dram_tensor("nwc", [128, NDC], F32, kind="ExternalInput")
    nb_in = nc.dram_tensor("nbc", [128, NDC], F32, kind="ExternalInput")
    bout_in = nc.dram_tensor("bout", [1, D], F32, kind="ExternalInput")
    basec_in = nc.dram_tensor("basec", [32, 1], F32, kind="ExternalInput")
    capc_in = nc.dram_tensor("capc", [32, 1], F32, kind="ExternalInput")
    identf_in = nc.dram_tensor("identf", [128, 128], F32, kind="ExternalInput")
    out_d = nc.dram_tensor("out", [T, D], F32, kind="ExternalOutput")
    xnd_d = nc.dram_tensor("xnd", [T, D], BF16)
    xsl_ds = [nc.dram_tensor("xsl%d" % j, [SLOTS, D], BF16) for j in range(2)]
    ysl_ds = [nc.dram_tensor("ysl%d" % j, [SLOTS, D], BF16) for j in range(2)]
    w1c_d = nc.dram_tensor("w1c", [NE, 128, NDC, DI], BF16)
    w2c_d = nc.dram_tensor("w2c", [NE, 128, NFC, D], BF16)

    cnt = {}

    def inc(ins, sem, n=1):
        ins.then_inc(sem, n)
        k = id(sem)
        cnt[k] = cnt.get(k, 0) + n
        return cnt[k]

    def val(sem):
        return cnt.get(id(sem), 0)

    with ExitStack() as ctx:
        E = ctx.enter_context
        xnhi_fm = E(nc.sbuf_tensor("xnhi_fm", [128, 3, NDC, 128], BF16, is_transpose_dest=True))
        xnlo_fm = E(nc.sbuf_tensor("xnlo_fm", [128, NDC, 128], BF16, is_transpose_dest=True))
        w1bf = E(nc.sbuf_tensor("w1bf", [128, NDC, DI], BF16))
        w2bf = E(nc.sbuf_tensor("w2bf", [128, NFC, D], BF16))
        h_sb = E(nc.sbuf_tensor("h_sb", [128, NFC, CAP], BF16))
        sx_fm = E(nc.sbuf_tensor("sx_fm", [128, 2, NDC, CAP], BF16))
        xg_sb = E(nc.sbuf_tensor("xg_sb", [128, 2, D], BF16))
        identb = E(nc.sbuf_tensor("identb", [128, 128], BF16))
        ysl_sb = E(nc.sbuf_tensor("ysl_sb", [128, 2, D], BF16))
        y1g = E(nc.sbuf_tensor("y1g", [128, D], BF16))
        y2g = E(nc.sbuf_tensor("y2g", [128, D], BF16))
        zz_sb = E(nc.sbuf_tensor("zz_sb", [128, D], BF16))
        ind32 = E(nc.sbuf_tensor("ind32", [128, 32], F32))
        indT = E(nc.sbuf_tensor("indT", [32, TT], F32))
        zeroT = E(nc.sbuf_tensor("zeroT", [32, TT], F32))
        cnt_fm = E(nc.sbuf_tensor("cnt_fm", [32, TT], F32))
        stm = E(nc.sbuf_tensor("stm", [128, 32], F32))
        eq4 = E(nc.sbuf_tensor("eq4", [128, NCH, NE], F32))
        ind4 = E(nc.sbuf_tensor("ind4", [128, NCH, NE], F32))
        tsa = E(nc.sbuf_tensor("tsa", [128, NE], F32))
        tsb = E(nc.sbuf_tensor("tsb", [128, NE], F32))
        s12f = E(nc.sbuf_tensor("s12f", [128, 2], F32))
        i12 = E(nc.sbuf_tensor("i12", [128, 2, NCH, 2], mybir.dt.int32))
        c12 = E(nc.sbuf_tensor("c12", [128, 2, NCH, 2], F32))
        basec_s = E(nc.sbuf_tensor("basec_s", [32, 1], F32))
        capc_s = E(nc.sbuf_tensor("capc_s", [32, 1], F32))
        oacc = E(nc.sbuf_tensor("oacc", [128, 2, NCH, D], F32))
        xst = E(nc.sbuf_tensor("xst", [128, 3, D], F32))
        outst = E(nc.sbuf_tensor("outst", [128, 2, D], F32))
        xn_tm = E(nc.sbuf_tensor("xn_tm", [128, D], F32))
        xc_s = E(nc.sbuf_tensor("xc_s", [128, 3, D], F32))
        sq_s = E(nc.sbuf_tensor("sq_s", [128, D], F32))
        xnhi_tm = E(nc.sbuf_tensor("xnhi_tm", [128, D], BF16))
        xnlo_tm = E(nc.sbuf_tensor("xnlo_tm", [128, D], BF16))
        auf_a = E(nc.sbuf_tensor("auf_a", [32, TT], F32))
        auf_u = E(nc.sbuf_tensor("auf_u", [32, TT], F32))
        st_f = E(nc.sbuf_tensor("st_f", [NS, TT], F32))
        st_bf = E(nc.sbuf_tensor("st_bf", [NS, 2, TT], BF16))
        carry_s = E(nc.sbuf_tensor("carry_s", [NS, 1], F32))
        au_tm = E(nc.sbuf_tensor("au_tm", [128, 128], F32))
        apre = E(nc.sbuf_tensor("apre", [128, 2, 32], F32))
        term = E(nc.sbuf_tensor("term", [128, 32], F32))
        sn1 = E(nc.sbuf_tensor("sn1", [128, NS], F32))
        sn2 = E(nc.sbuf_tensor("sn2", [128, NS], F32))
        lg_sb = E(nc.sbuf_tensor("lg_sb", [128, 2, NE], F32))
        eq_sb = E(nc.sbuf_tensor("eq_sb", [128, NE], F32))
        msk_sb = E(nc.sbuf_tensor("msk_sb", [128, NE], F32))
        ind_sb = E(nc.sbuf_tensor("ind_sb", [128, NE], F32))
        e_sb = E(nc.sbuf_tensor("e_sb", [128, NE], F32))
        c_sb = E(nc.sbuf_tensor("c_sb", [128, 2, NCH, NE], F32))
        sum_s = E(nc.sbuf_tensor("sum_s", [128, 1], F32))
        mu_s = E(nc.sbuf_tensor("mu_s", [128, 1], F32))
        tq_s = E(nc.sbuf_tensor("tq_s", [128, 1], F32))
        q_s = E(nc.sbuf_tensor("q_s", [128, 1], F32))
        vs_s = E(nc.sbuf_tensor("vs_s", [128, 3], F32))
        sxc_s = E(nc.sbuf_tensor("sxc_s", [128, 4], F32))
        w_s = E(nc.sbuf_tensor("w_s", [128, 3], F32))
        rs_s = E(nc.sbuf_tensor("rs_s", [128, 3], F32))
        u1_s = E(nc.sbuf_tensor("u1_s", [128, 1], F32))
        hu1_s = E(nc.sbuf_tensor("hu1_s", [128, 1], F32))
        u2_s = E(nc.sbuf_tensor("u2_s", [128, 1], F32))
        mu2_s = E(nc.sbuf_tensor("mu2_s", [128, 1], F32))
        rs2_c = E(nc.sbuf_tensor("rs2_c", [128, NCH], F32))
        rm2_c = E(nc.sbuf_tensor("rm2_c", [128, NCH], F32))
        m1_s = E(nc.sbuf_tensor("m1_s", [128, 2], F32))
        nm1_s = E(nc.sbuf_tensor("nm1_s", [128, 2], F32))
        m2_s = E(nc.sbuf_tensor("m2_s", [128, 2], F32))
        e2_s = E(nc.sbuf_tensor("e2_s", [128, 1], F32))
        den_s = E(nc.sbuf_tensor("den_s", [128, 1], F32))
        rden_s = E(nc.sbuf_tensor("rden_s", [128, 1], F32))
        epsk_s = E(nc.sbuf_tensor("epsk_s", [128, 1], F32))
        wgbf = E(nc.sbuf_tensor("wgbf", [128, NDC, NE], BF16))

        abhi = E(nc.sbuf_tensor("abhi", [128, NEC, 32], BF16))
        ablo = E(nc.sbuf_tensor("ablo", [128, NEC, 32], BF16))

        chi = E(nc.sbuf_tensor("chi", [128, NEC, NS], BF16))
        clo = E(nc.sbuf_tensor("clo", [128, NEC, NS], BF16))

        binhi = E(nc.sbuf_tensor("binhi", [128, NEC], BF16))
        binlo = E(nc.sbuf_tensor("binlo", [128, NEC], BF16))
        nw_f = E(nc.sbuf_tensor("nw_f", [128, NDC], F32))
        ab_f = xst[:, 0, 0:512].rearrange("p (a b) -> p a b", b=32)
        c_f = xst[:, 0, 512:768].rearrange("p (a b) -> p a b", b=16)
        bin_f = xst[:, 1, 0:16]
        nb_f = xst[:, 1, 16:24]

        nbhi = E(nc.sbuf_tensor("nbhi", [128, NDC], BF16))
        nblo = E(nc.sbuf_tensor("nblo", [128, NDC], BF16))
        bout_rep = E(nc.sbuf_tensor("bout_rep", [128, D], BF16))
        onesr_bf = E(nc.sbuf_tensor("onesr_bf", [1, 128], BF16))
        onec_bf = E(nc.sbuf_tensor("onec_bf", [128, 1], BF16))
        vwhi = E(nc.sbuf_tensor("vwhi", [1, 32], BF16))
        vwlo = E(nc.sbuf_tensor("vwlo", [1, 32], BF16))
        cbhi = E(nc.sbuf_tensor("cbhi", [1, 32], BF16))
        cblo = E(nc.sbuf_tensor("cblo", [1, 32], BF16))
        cab_f = E(nc.sbuf_tensor("cab_f", [128, NDC, 32], F32))
        cabhi = E(nc.sbuf_tensor("cabhi", [128, NDC, 32], BF16))
        cablo = E(nc.sbuf_tensor("cablo", [128, NDC, 32], BF16))
        cabw_f = E(nc.sbuf_tensor("cabw_f", [128, NDC, 32], F32))
        cabwhi = E(nc.sbuf_tensor("cabwhi", [128, NDC, 32], BF16))
        cabwlo = E(nc.sbuf_tensor("cabwlo", [128, NDC, 32], BF16))
        vwrow = E(nc.sbuf_tensor("vwrow", [1, 32], F32))
        cbrow = E(nc.sbuf_tensor("cbrow", [1, 32], F32))
        vwrep = E(nc.sbuf_tensor("vwrep", [128, 32], F32))
        cbrep = E(nc.sbuf_tensor("cbrep", [128, 32], F32))
        mtbf = E(nc.sbuf_tensor("mtbf", [NS, D], BF16))
        ps_h = E(nc.psum_tensor("ps_h", [128, 1024], F32))
        ps_ya = E(nc.psum_tensor("ps_ya", [128, 1024], F32))
        ps_yb = E(nc.psum_tensor("ps_yb", [128, 1024], F32))
        ps_misc = E(nc.psum_tensor("ps_misc", [128, 512], F32))
        ps_trx = E(nc.psum_tensor("ps_trx", [128, 1024], BF16))
        ps_ap = ps_misc[:, 256:288]
        ps_lg = ps_misc[:, 128:136]
        s_c = E(nc.semaphore("s_c"))
        s_xs = [E(nc.semaphore("s_x%d" % j)) for j in range(4)]
        s_tps = [E(nc.semaphore("s_tp%d" % j)) for j in range(8)]
        s_outs = [E(nc.semaphore("s_out%d" % j)) for j in range(2)]
        s_wg = E(nc.semaphore("s_wg"))
        s_w1 = E(nc.semaphore("s_w1"))
        s_w2 = E(nc.semaphore("s_w2"))
        s_t = E(nc.semaphore("s_t"))
        s_v = E(nc.semaphore("s_v"))
        s_sc = E(nc.semaphore("s_sc"))
        s_ye = E(nc.semaphore("s_ye"))
        s_re = E(nc.semaphore("s_re"))
        s_zz = [E(nc.semaphore("s_zz%d" % j)) for j in range(2)]
        s_scat = [E(nc.semaphore("s_scat%d" % j)) for j in range(2)]
        s_xg = E(nc.semaphore("s_xg"))
        s_sxc = E(nc.semaphore("s_sxc"))
        s_wc1 = E(nc.semaphore("s_wc1"))
        s_wc2 = E(nc.semaphore("s_wc2"))
        s_ysl2 = [E(nc.semaphore("s_ysl%d" % j)) for j in range(4)]
        s_yg = [E(nc.semaphore("s_yg%d" % j)) for j in range(2)]
        s_xb = [E(nc.semaphore("s_xb%d" % j)) for j in range(2)]
        s_xnd = E(nc.semaphore("s_xnd"))

        M = {}
        LATE = {"lgcp": 0, "m6": 0, "apmm": 0}
        blk = E(nc.Block())

        @blk.sync
        def _(sy):
            pass
            inc(sy.dma_start(out=nw_f[:], in_=nw_in.ap()), s_c, 16)
            inc(sy.dma_start(out=nb_f, in_=nb_in.ap()), s_c, 16)
            inc(sy.dma_start(out=bin_f, in_=bin_in.ap()), s_c, 16)
            inc(sy.dma_start(out=ab_f, in_=ab_in.ap()), s_c, 16)
            inc(sy.dma_start(out=c_f, in_=c_in.ap()), s_c, 16)
            inc(sy.dma_start(out=basec_s[:], in_=basec_in.ap()), s_c, 16)
            inc(sy.dma_start(out=capc_s[:], in_=capc_in.ap()), s_c, 16)
            M["consts"] = val(s_c)

        @blk.gpsimd
        def _(g):
            inc(g.dma_start(out=bout_rep[:],
                            in_=bout_in.ap()[0:1, :].partition_broadcast(128)), s_wg, 16)
            inc(g.dma_start(out=wgbf[:], in_=wg_in.ap()), s_wg, 16)
            inc(g.dma_start(out=identb[:], in_=identf_in.ap()), s_wg, 16)
            w1r = w1_in.ap()[0].rearrange("(dc p) f -> p dc f", p=128)
            inc(g.dma_start(out=w1bf[:, 0:4, :], in_=w1r[:, 0:4, :]), s_w1, 16)
            inc(g.dma_start(out=w1bf[:, 4:8, :], in_=w1r[:, 4:8, :]), s_w1, 16)
            w2r = w2_in.ap()[0].rearrange("(fc p) d -> p fc d", p=128)
            for q in range(4):
                inc(g.dma_start(out=w2bf[:, 4 * q:4 * q + 4, :],
                                in_=w2r[:, 4 * q:4 * q + 4, :]), s_w2, 16)
            g.wait_ge(s_w1, 32)
            inc(g.dma_start(out=w1c_d.ap()[0][:, 0:4, :],
                            in_=w1bf[:, 0:4, :]), s_wc1, 16)
            inc(g.dma_start(out=w1c_d.ap()[0][:, 4:8, :],
                            in_=w1bf[:, 4:8, :]), s_wc1, 16)
            g.wait_ge(s_w2, 64)
            for q in range(4):
                inc(g.dma_start(out=w2c_d.ap()[0][:, 4 * q:4 * q + 4, :],
                                in_=w2bf[:, 4 * q:4 * q + 4, :]), s_wc2, 16)

        @blk.vector
        def _(v):
            v.memset(onec_bf[:], 1.0)
            v.memset(onesr_bf[:], 1.0)
            v.memset(au_tm[:, 16:32], 0.0)
            v.memset(au_tm[:, 48:64], 0.0)
            v.memset(epsk_s[:], EPSK)
            v.memset(zz_sb[:], 0.0)
            v.memset(zeroT[:], 0.0)
            v.memset(ind32[:, 8:32], 0.0)

        @blk.scalar
        def _(s):
            s.wait_ge(s_c, M["consts"])
            s.copy(abhi[:], ab_f)
            s.copy(chi[:], c_f)
            s.copy(binhi[:], bin_f)
            inc(s.copy(nbhi[:], nb_f), s_sc)
            M["hi_consts"] = val(s_sc)

        @blk.vector
        def _(v):
            v.wait_ge(s_sc, M["hi_consts"])
            v.tensor_sub(ablo[:], ab_f, abhi[:])
            v.tensor_sub(clo[:], c_f, chi[:])
            v.tensor_sub(binlo[:], bin_f, binhi[:])
            inc(v.tensor_sub(nblo[:], nb_f, nbhi[:]), s_v)
            M["lo_consts"] = val(s_v)

        caps = [ps_h[:, 0:32], ps_h[:, 512:544],
                ps_ya[:, 0:32], ps_ya[:, 512:544],
                ps_yb[:, 0:32], ps_yb[:, 512:544],
                ps_misc[:, 0:32], ps_misc[:, 128:160]]
        for ec in range(NEC):
            @blk.sync
            def _(sy, ec=ec):
                if ec >= 2:
                    sy.wait_ge(s_sc, M[("whi", ec - 2)])
                    sy.wait_ge(s_v, M[("wlo", ec - 2)])
                else:
                    sy.wait_ge(s_sc, M["hi_consts"])
                    sy.wait_ge(s_v, M["lo_consts"])
                a = sy.dma_start(out=xst[:, ec % 2, :],
                                 in_=win_in.ap()[ec * 128:(ec + 1) * 128, :])
                M[("wld", ec)] = inc(a, s_xs[ec % 2], 16)

            @blk.scalar
            def _(s, ec=ec):
                s.wait_ge(s_xs[ec % 2], M[("wld", ec)])
                if ec >= 1:
                    s.wait_ge(s_t, M[("cabmm", ec - 1)])
                M[("whi", ec)] = inc(s.copy(xnhi_tm[:], xst[:, ec % 2, :]), s_sc)

            @blk.vector
            def _(v, ec=ec):
                v.wait_ge(s_sc, M[("whi", ec)])
                M[("wlo", ec)] = inc(
                    v.tensor_sub(xnlo_tm[:], xst[:, ec % 2, :], xnhi_tm[:]), s_v)

            @blk.tensor
            def _(t, ec=ec):
                t.wait_ge(s_v, M[("wlo", ec)])
                if ec == 0:
                    t.wait_ge(s_sc, M["hi_consts"])
                    t.wait_ge(s_v, M["lo_consts"])
                last = None
                for dc in range(NDC):
                    for p, (lh, rh) in enumerate([
                            (xnhi_tm, abhi), (xnlo_tm, abhi), (xnhi_tm, ablo)]):
                        last = t.matmul(caps[dc], lh[:, dc * 128:(dc + 1) * 128],
                                        rh[:, ec, :],
                                        start=(ec == 0 and p == 0 and dc != 7),
                                        stop=(ec == NEC - 1 and p == 2
                                              and dc != 6))
                M[("cabmm", ec)] = inc(last, s_t)

        @blk.vector
        def _(v):
            v.wait_ge(s_t, M[("cabmm", NEC - 1)])
            for dc in range(NDC):
                v.tensor_copy(cab_f[:, dc, :], caps[dc])
            v.drain()
            last = None
            for dc in range(NDC):
                last = v.tensor_scalar(cabw_f[:, dc, :], cab_f[:, dc, :],
                                       nw_f[:, dc:dc + 1], None, ALU.mult)
            M["cab_f"] = inc(last, s_v)

        @blk.scalar
        def _(s):
            s.wait_ge(s_v, M["cab_f"])
            s.copy(cabhi[:], cab_f[:])
            M["cabhi"] = inc(s.copy(cabwhi[:], cabw_f[:]), s_sc)

        @blk.vector
        def _(v):
            v.wait_ge(s_sc, M["cabhi"])
            v.tensor_sub(cablo[:], cab_f[:], cabhi[:])
            M["cablo"] = inc(v.tensor_sub(cabwlo[:], cabw_f[:], cabwhi[:]), s_v)

        @blk.tensor
        def _(t):
            t.wait_ge(s_v, M["cablo"])
            for dc in range(NDC):
                for rh in (cabwhi, cabwlo):
                    t.matmul(ps_ap[0:1, :], onec_bf[:], rh[:, dc, :],
                             start=(dc == 0 and rh is cabwhi), stop=False)
            last2 = None
            for dc in range(NDC):
                for lh, rh in [(nbhi, cabhi), (nblo, cabhi), (nbhi, cablo)]:
                    last2 = t.matmul(ps_misc[0:1, 0:32], lh[:, dc:dc + 1],
                                     rh[:, dc, :], start=False, stop=False)
            for ec in range(NEC):
                for p, (lh, rh) in enumerate([(binhi, abhi), (binlo, abhi),
                                              (binhi, ablo)]):
                    last2 = t.matmul(ps_misc[0:1, 0:32], lh[:, ec:ec + 1],
                                     rh[:, ec, :], start=False,
                                     stop=(ec == NEC - 1 and p == 2))
            M["vwcb"] = inc(last2, s_t)

        @blk.vector
        def _(v):
            v.wait_ge(s_t, M["vwcb"])
            v.tensor_copy(vwrow[:], ps_ap[0:1, :])
            M["rows"] = inc(v.tensor_copy(cbrow[:], ps_misc[0:1, 0:32]), s_v)

        @blk.scalar
        def _(s):
            s.wait_ge(s_v, M["rows"])
            s.copy(vwhi[:], vwrow[:])
            M["rephi"] = inc(s.copy(cbhi[:], cbrow[:]), s_sc)

        @blk.vector
        def _(v):
            v.wait_ge(s_sc, M["rephi"])
            v.tensor_sub(vwlo[:], vwrow[:], vwhi[:])
            M["replo"] = inc(v.tensor_sub(cblo[:], cbrow[:], cbhi[:]), s_v)

        @blk.tensor
        def _(t):
            t.wait_ge(s_v, M["replo"])
            t.matmul(ps_ap[:], onesr_bf[:], vwhi[:], start=True, stop=False)
            M["rep1"] = inc(t.matmul(ps_ap[:], onesr_bf[:], vwlo[:],
                                     start=False, stop=True), s_t)

        @blk.vector
        def _(v):
            v.wait_ge(s_t, M["rep1"])
            M["vwrep"] = inc(v.tensor_copy(vwrep[:], ps_ap[:]), s_v)

        @blk.tensor
        def _(t):
            t.wait_ge(s_v, M["vwrep"])
            t.matmul(ps_ap[:], onesr_bf[:], cbhi[:], start=True, stop=False)
            M["rep2"] = inc(t.matmul(ps_ap[:], onesr_bf[:], cblo[:],
                                     start=False, stop=True), s_t)

        @blk.vector
        def _(v):
            v.wait_ge(s_t, M["rep2"])
            M["cbrep"] = inc(v.tensor_copy(cbrep[:], ps_ap[:]), s_v)

        for ec in range(NEC):
            @blk.sync
            def _(sy, ec=ec):
                sy.wait_ge(s_sc, M[("mwhi", ec - 2)] if ec >= 2 else M[("whi", NEC - 1)])
                sy.wait_ge(s_v, M[("mwlo", ec - 2)] if ec >= 2 else M[("wlo", NEC - 1)])
                a = sy.dma_start(out=xst[:, ec % 2, :],
                                 in_=woutt_in.ap()[ec * 128:(ec + 1) * 128, :])
                M[("mwld", ec)] = inc(a, s_xs[ec % 2], 16)

            @blk.scalar
            def _(s, ec=ec):
                s.wait_ge(s_xs[ec % 2], M[("mwld", ec)])
                s.wait_ge(s_t, M[("mtmm", ec - 1)] if ec >= 1 else M[("cabmm", NEC - 1)])
                M[("mwhi", ec)] = inc(s.copy(xnhi_tm[:], xst[:, ec % 2, :]), s_sc)

            @blk.vector
            def _(v, ec=ec):
                v.wait_ge(s_sc, M[("mwhi", ec)])
                M[("mwlo", ec)] = inc(
                    v.tensor_sub(xnlo_tm[:], xst[:, ec % 2, :], xnhi_tm[:]), s_v)

            @blk.tensor
            def _(t, ec=ec):
                t.wait_ge(s_v, M[("mwlo", ec)])
                if ec == 0:
                    t.wait_ge(s_v, M["cbrep"])
                last = None
                for hf in range(2):
                    sl = slice(hf * 512, hf * 512 + 512)
                    for p, (lh, rh) in enumerate([(chi, xnhi_tm), (clo, xnhi_tm),
                                                  (chi, xnlo_tm)]):
                        last = t.matmul(ps_yb[0:NS, sl], lh[:, ec, :], rh[:, sl],
                                        start=(ec == 0 and p == 0),
                                        stop=(ec == NEC - 1 and p == 2))
                M[("mtmm", ec)] = inc(last, s_t)

        @blk.vector
        def _(v):
            v.wait_ge(s_t, M[("mtmm", NEC - 1)])
            M["mtbf"] = inc(v.tensor_copy(mtbf[:], ps_yb[0:NS, :]), s_v)

        ge_w1free = {}
        ge_w2free = {}
        xfree = {}
        stores = [0]

        def s1VA(ti, c):
            g = ti * NCH + c
            gp = g % 2
            g3 = g % 3
            g4 = g % 4
            xs = g % 3
            t0 = g * 128

            @blk.sync
            def _(sy):
                if g >= 3:
                    sy.wait_ge(s_v, xfree[g - 3])
                else:
                    sy.wait_ge(s_sc, M[("mwhi", NEC - 1)])
                    sy.wait_ge(s_v, M[("mwlo", NEC - 1)])
                a = sy.dma_start(out=xst[:, xs, :], in_=x_in.ap()[t0:t0 + 128, :])
                M[("x", g)] = inc(a, s_xs[xs], 16)

            @blk.vector
            def _(v):
                v.wait_ge(s_xs[xs], M[("x", g)])
                xt = xst[:, xs, :]
                v.reduce_sum(sum_s[:], xt, AX.X)
                v.drain()
                v.tensor_scalar_mul(mu_s[:], sum_s[:], R1024)
                v.drain()
                v.tensor_scalar_sub(xc_s[:, g3, :], xt, mu_s[:])
                v.drain()
                v.tensor_mul(sq_s[:], xc_s[:, g3, :], xc_s[:, g3, :])
                v.reduce_sum(sxc_s[:, g4:g4 + 1], xc_s[:, g3, :], AX.X)
                v.drain()
                v.reduce_sum(vs_s[:, g3:g3 + 1], sq_s[:], AX.X)
                v.tensor_mul(tq_s[:], sxc_s[:, g4:g4 + 1], sxc_s[:, g4:g4 + 1])
                v.drain()
                v.scalar_tensor_tensor(q_s[:], tq_s[:], -R1024, vs_s[:, g3:g3 + 1],
                                       ALU.mult, ALU.add)
                v.drain()
                v.scalar_tensor_tensor(w_s[:, g3:g3 + 1], vs_s[:, g3:g3 + 1], EPS,
                                       q_s[:], ALU.mult, ALU.add)
                v.drain()
                M[("m1", g)] = inc(
                    v.tensor_scalar_add(w_s[:, g3:g3 + 1], w_s[:, g3:g3 + 1], K2),
                    s_v)

        def s1SB(ti, c):
            g = ti * NCH + c
            g3 = g % 3

            @blk.scalar
            def _(s):
                s.wait_ge(s_v, M[("m1", g)])
                s.activation(u1_s[:], vs_s[:, g3:g3 + 1], AF.Ln, bias=epsk_s[:])
                s.drain()
                s.mul(hu1_s[:], u1_s[:], 0.5)
                s.activation(rs_s[:, g3:g3 + 1], u1_s[:], AF.Exp, scale=-0.5)
                s.drain()
                s.mul(rs_s[:, g3:g3 + 1], rs_s[:, g3:g3 + 1], 32.0)
                s.drain()
                s.activation(u2_s[:], w_s[:, g3:g3 + 1], AF.Ln)
                s.drain()
                M[("m2", g)] = inc(
                    s.activation(rs2_c[:, c:c + 1], u2_s[:], AF.Exp,
                                 scale=-0.5, bias=hu1_s[:]), s_sc)

        def s1VC(ti, c):
            g = ti * NCH + c
            gp = g % 2
            g3 = g % 3
            g4 = g % 4
            xs = g % 3
            tb = ti % 2

            @blk.vector
            def _(v):
                v.wait_ge(s_sc, M[("m2", g)])
                if g >= 1:
                    v.wait_ge(s_tps[(g - 1) % 8], M[("tpd", g - 1)])
                    v.wait_ge(s_xnd, M[("xnd", g - 1)])
                else:
                    v.wait_ge(s_t, M[("mtmm", NEC - 1)])
                    v.wait_ge(s_c, M["consts"])
                    v.wait_ge(s_wg, 48)
                v.tensor_scalar_mul(xn_tm[:], xc_s[:, g3, :], rs_s[:, g3:g3 + 1])
                v.drain()
                v.tensor_copy(xnhi_tm[:], xn_tm[:])
                v.drain()
                v.tensor_sub(xnlo_tm[:], xn_tm[:], xnhi_tm[:])
                v.tensor_scalar_mul(mu2_s[:], sxc_s[:, g4:g4 + 1], R1024)
                v.drain()
                v.tensor_mul(mu2_s[:], mu2_s[:], rs_s[:, g3:g3 + 1])
                v.drain()
                v.tensor_mul(rm2_c[:, c:c + 1], mu2_s[:], rs2_c[:, c:c + 1])
                v.tensor_add(oacc[:, tb, c, :], xst[:, xs, :], xn_tm[:])
                v.drain()
                ii = v.tensor_add(oacc[:, tb, c, :], oacc[:, tb, c, :], bout_rep[:])
                xfree[g] = inc(ii, s_v)

            @blk.sync
            def _(sy):
                sy.wait_ge(s_v, xfree[g])
                if g >= 1:
                    sy.wait_ge(s_t, LATE["apmm"])
                stp = s_tps[g % 8]
                for dc in range(NDC):
                    inc(sy.dma_start_transpose(
                        xnhi_fm[:, g % 3, dc, :],
                        xnhi_tm[:, dc * 128:(dc + 1) * 128]), stp, 16)
                for dc in range(NDC):
                    inc(sy.dma_start_transpose(
                        xnlo_fm[:, dc, :],
                        xnlo_tm[:, dc * 128:(dc + 1) * 128]), stp, 16)
                M[("tpd", g)] = val(stp)
                M[("xnd", g)] = inc(
                    sy.dma_start(out=xnd_d.ap()[g * 128:(g + 1) * 128, :],
                                 in_=xnhi_tm[:]), s_xnd, 16)

        def s1B1(ti, c):
            g = ti * NCH + c
            tb = ti % 2
            pc = c % 2

            @blk.tensor
            def _(t):
                if g == 0:
                    t.wait_ge(s_wg, 48)
                    t.wait_ge(s_sc, M["cabhi"])
                    t.wait_ge(s_v, M["cablo"])
                    t.wait_ge(s_v, M["mtbf"])
                    t.wait_ge(s_v, M["cbrep"])
                if LATE["m6"]:
                    t.wait_ge(s_v, LATE["m6"])
                t.wait_ge(s_tps[g % 8], M[("tpd", g)])
                last = None
                for dc in range(NDC):
                    last = t.matmul(ps_lg, xnhi_fm[:, g % 3, dc, :],
                                    wgbf[:, dc, :],
                                    start=(dc == 0), stop=(dc == NDC - 1))
                M[("lgmm", g)] = inc(last, s_t)

            @blk.vector
            def _(v):
                if g >= 2:
                    v.wait_ge(s_sc, M[("m10", g - 2)])
                v.wait_ge(s_t, M[("lgmm", g)])
                M[("lgcp", g)] = inc(v.tensor_copy(lg_sb[:, pc, :], ps_lg), s_v)
                LATE["lgcp"] = M[("lgcp", g)]

            @blk.tensor
            def _(t):
                t.wait_ge(s_v, M[("lgcp", g)])
                last = None
                for p in range(3):
                    for dc in range(NDC):
                        lh = (xnhi_fm[:, g % 3, dc, :]
                              if p != 1 else xnlo_fm[:, dc, :])
                        rh = cabwhi[:, dc, :] if p != 2 else cabwlo[:, dc, :]
                        last = t.matmul(ps_ap[:], lh, rh,
                                        start=(p == 0 and dc == 0),
                                        stop=(p == 2 and dc == NDC - 1))
                M[("apmm", g)] = inc(last, s_t)
                LATE["apmm"] = M[("apmm", g)]

            @blk.vector
            def _(v):
                v.wait_ge(s_t, M[("apmm", g)])
                v.scalar_tensor_tensor(term[:], vwrep[:], rm2_c[:, c:c + 1],
                                       cbrep[:], ALU.mult, ALU.subtract)
                v.drain()
                v.scalar_tensor_tensor(apre[:, pc, :], ps_ap[:],
                                        rs2_c[:, c:c + 1], term[:],
                                        ALU.mult, ALU.subtract)
                v.drain()
                M[("m6", g)] = inc(
                    v.tensor_scalar_max(apre[:, pc, 0:NS], apre[:, pc, 0:NS],
                                        -30.0), s_v)
                LATE["m6"] = M[("m6", g)]
                v.reduce_max(m1_s[:, pc:pc + 1], lg_sb[:, pc, :], AX.X)
                v.reduce_max(nm1_s[:, pc:pc + 1], lg_sb[:, pc, :], AX.X, negate=True)
                v.drain()
                v.tensor_scalar(eq4[:, c, :], lg_sb[:, pc, :], m1_s[:, pc:pc + 1],
                                None, ALU.is_equal)
                v.drain()
                v.scalar_tensor_tensor(msk_sb[:], eq4[:, c, :], -1e30, lg_sb[:, pc, :],
                                       ALU.mult, ALU.add)
                v.drain()
                M[("m9", g)] = inc(
                    v.reduce_max(m2_s[:, pc:pc + 1], msk_sb[:], AX.X), s_v)

        def s1B2(ti, c):
            g = ti * NCH + c
            tb = ti % 2
            pc = c % 2

            @blk.scalar
            def _(s):
                if g >= 1:
                    s.wait_ge(s_v, M[("router", g - 1)])
                s.wait_ge(s_v, M[("m6", g)])
                M[("m7", g)] = inc(
                    s.activation(sn1[:], apre[:, pc, 0:NS], AF.Exp, scale=-1.0),
                    s_sc)
                s.wait_ge(s_v, M[("m9", g)])
                s.activation(e_sb[:], lg_sb[:, pc, :], AF.Exp,
                             bias=nm1_s[:, pc:pc + 1])
                M[("m10", g)] = inc(
                    s.activation(e2_s[:], m2_s[:, pc:pc + 1], AF.Exp,
                                 bias=nm1_s[:, pc:pc + 1]), s_sc)

            @blk.vector
            def _(v):
                v.wait_ge(s_sc, M[("m7", g)])
                v.tensor_copy(au_tm[:, 32:48], apre[:, pc, 16:32])
                v.tensor_scalar_add(sn2[:], sn1[:], 1.0)
                v.drain()
                v.reciprocal(au_tm[:, 0:16], sn2[:])
                v.drain()
                for rb in range(4):
                    for dst, cbk in ((auf_a, 0), (auf_u, 1)):
                        v.transpose(
                            dst[0:32, c * 128 + 32 * rb:c * 128 + 32 * rb + 32],
                            au_tm[32 * rb:32 * rb + 32, 32 * cbk:32 * cbk + 32])
                v.drain()
                v.wait_ge(s_sc, M[("m10", g)])
                v.tensor_scalar_add(den_s[:], e2_s[:], 1.0)
                v.drain()
                v.reciprocal(rden_s[:], den_s[:])
                v.tensor_scalar(ind4[:, c, :], lg_sb[:, pc, :], m2_s[:, pc:pc + 1],
                                None, ALU.is_ge)
                v.drain()
                v.tensor_copy(ind32[:, 0:8], ind4[:, c, :])
                v.drain()
                for rb in range(4):
                    v.transpose(indT[0:32, c * 128 + 32 * rb:c * 128 + 32 * rb + 32],
                                ind32[32 * rb:32 * rb + 32, 0:32])
                v.drain()
                M[("router", g)] = inc(
                    v.scalar_tensor_tensor(c_sb[:, tb, c, :], ind4[:, c, :], rden_s[:],
                                           e_sb[:], ALU.mult, ALU.mult), s_v)

        def scan_emit(ti):
            tb = ti % 2

            @blk.vector
            def _(v):
                if ti % 4 == 0:
                    v.memset(carry_s[:], 0.0)
                v.drain()
                v.tensor_tensor_scan(st_f[:], auf_a[0:NS, :], auf_u[0:NS, :],
                                     carry_s[:], ALU.mult, ALU.add)
                v.drain()
                v.tensor_copy(carry_s[:], st_f[:, TT - 1:TT])
                M[("scan", ti)] = inc(v.tensor_copy(st_bf[:, tb, :], st_f[:]), s_v)

        def sx_prefetch(ti, e):
            if e >= NE:
                return
            ge = ti * NE + e
            tp = ti % 2

            @blk.gpsimd
            def _(g):
                g.wait_ge(s_scat[tp], M[("scat", ti)])
                if ge >= 1:
                    g.wait_ge(s_t, M[("trp2", ge - 1)])
                for cc in range(CAP // 128):
                    r0 = e * CAP + cc * 128
                    inc(g.dma_start(out=xg_sb[:, cc, :],
                                    in_=xsl_ds[tp].ap()[r0:r0 + 128, :]),
                        s_xg, 16)
                M[("xg", ge)] = val(s_xg)

        def trx_batch(t, ge, cc):
            last = None
            for dc in range(NDC):
                last = t.transpose(ps_trx[:, dc * 128:(dc + 1) * 128],
                                   xg_sb[:, cc, dc * 128:(dc + 1) * 128],
                                   identb[:])
            M[("trp%d" % (cc + 1), ge)] = inc(last, s_t)

        def trx_copies(s, ge, pe):
            s.wait_ge(s_t, M[("trp1", ge)])
            if ge >= 2:
                s.wait_ge(s_t, M[("hps", ge - 2, NFC - 1)])
            for dc in range(NDC):
                inc(s.copy(sx_fm[:, pe, dc, 0:128],
                           ps_trx[:, dc * 128:(dc + 1) * 128]), s_sxc)
            s.wait_ge(s_t, M[("trp2", ge)])
            for dc in range(NDC):
                inc(s.copy(sx_fm[:, pe, dc, 128:256],
                           ps_trx[:, dc * 128:(dc + 1) * 128]), s_sxc)

        def zero_emit(ti):
            tp = ti % 2

            @blk.sync
            def _(sy):
                sy.wait_ge(s_v, M["lo_consts"])
                if ti >= 2:
                    sy.wait_ge(s_t, ge_w1free[(ti - 1) * NE])
                for r in range(SLOTS // 128):
                    inc(sy.dma_start(out=xsl_ds[tp].ap()[r * 128:(r + 1) * 128, :],
                                     in_=zz_sb[:]), s_zz[tp], 16)
                M[("zz", ti)] = val(s_zz[tp])

        def idx_emit(ti):
            tp = ti % 2
            tb = ti % 2

            @blk.vector
            def _(v):
                if ti >= 2:
                    for cc in range(NCH):
                        v.wait_ge(s_yg[cc % 2], M[("gath", ti - 2, cc)])
                v.tensor_tensor_scan(cnt_fm[:], indT[:], zeroT[:],
                                     zeroT[:, 0:1], ALU.add, ALU.add)
                v.drain()
                v.tensor_sub(indT[:], cnt_fm[:], indT[:])
                v.drain()
                v.tensor_scalar_add(indT[:], indT[:], basec_s[:])
                v.drain()
                v.tensor_scalar_min(indT[:], indT[:], capc_s[:])
                v.drain()
                for c in range(NCH):
                    for rb in range(4):
                        v.transpose(stm[32 * rb:32 * rb + 32, 0:32],
                                    indT[0:32, c * 128 + 32 * rb:c * 128 + 32 * rb + 32])
                    v.drain()
                    v.tensor_mul(tsa[:], stm[:, 0:8], eq4[:, c, :])
                    v.tensor_sub(tsb[:], ind4[:, c, :], eq4[:, c, :])
                    v.drain()
                    v.reduce_sum(s12f[:, 0:1], tsa[:], AX.X)
                    v.drain()
                    v.tensor_mul(tsa[:], stm[:, 0:8], tsb[:])
                    v.drain()
                    v.reduce_sum(s12f[:, 1:2], tsa[:], AX.X)
                    v.drain()
                    v.tensor_mul(tsa[:], c_sb[:, tb, c, :], eq4[:, c, :])
                    v.drain()
                    v.tensor_copy(i12[:, tp, c, 0:2], s12f[:])
                    v.reduce_sum(c12[:, tp, c, 0:1], tsa[:], AX.X)
                    v.drain()
                    v.tensor_mul(tsa[:], c_sb[:, tb, c, :], tsb[:])
                    v.drain()
                    v.reduce_sum(c12[:, tp, c, 1:2], tsa[:], AX.X)
                    v.drain()
                M[("idx", ti)] = inc(v.engine_nop(), s_v)

            @blk.gpsimd
            def _(g):
                g.wait_ge(s_v, M[("idx", ti)])
                g.wait_ge(s_zz[tp], M[("zz", ti)])
                g.wait_ge(s_xnd, M[("xnd", ti * NCH + NCH - 1)])
                if ti >= 1:
                    g.wait_ge(s_t, M[("trp2", ti * NE - 1)])
                from concourse.bass_core_rust import IndirectOffsetOnAxis as IOA
                for c in range(NCH):
                    if c >= 1:
                        g.wait_ge(s_scat[tp], M[("scat2", ti, c - 1)])
                    elif ti >= 2:
                        g.wait_ge(s_scat[tp], M[("scat", ti - 2)])
                    t0 = (ti * NCH + c) * 128
                    a = g.dma_start(out=xg_sb[:, c % 2, :],
                                    in_=xnd_d.ap()[t0:t0 + 128, :])
                    wv = inc(a, s_xb[c % 2], 16)
                    g.wait_ge(s_xb[c % 2], wv)
                    inc(g.indirect_dma_start(
                        xsl_ds[tp].ap(), IOA(i12[:, tp, c, 0:1], 0),
                        xg_sb[:, c % 2, :], None), s_scat[tp], 16)
                    inc(g.indirect_dma_start(
                        xsl_ds[tp].ap(), IOA(i12[:, tp, c, 1:2], 0),
                        xg_sb[:, c % 2, :], None), s_scat[tp], 16)
                    M[("scat2", ti, c)] = val(s_scat[tp])
                M[("scat", ti)] = val(s_scat[tp])

        def expert_emit(ti, e):
            ge = ti * NE + e
            tb = ti % 2
            tp = ti % 2
            pe = e % 2
            sx_prefetch(ti, e + 1)

            nxt = ge + 1 if ge + 1 < NTILE * NE else None
            npe = (e + 1) % 2

            @blk.tensor
            def _(t):
                t.wait_ge(s_w1, 32 * (ge + 1))
                t.wait_ge(s_sxc, 16 * ge + 16)
                for fc in range(NFC):
                    pp = (fc % 2) * 512
                    rk = 16 * ge + fc - 1
                    if rk > 0:
                        t.wait_ge(s_re, rk)
                    for dc in range(NDC):
                        last = t.matmul(ps_h[:, pp:pp + CAP],
                                        w1bf[:, dc, fc * 128:(fc + 1) * 128],
                                        sx_fm[:, pe, dc, :],
                                        start=(dc == 0), stop=(dc == NDC - 1))
                    M[("hps", ge, fc)] = inc(last, s_t)
                ge_w1free[ge + 1] = val(s_t)
                t.wait_ge(s_w2, 64 * (ge + 1))
                t.wait_ge(s_re, 16 * ge + 16)
                if e == 0:
                    t.wait_ge(s_v, val(s_v))
                if nxt is not None:
                    t.wait_ge(s_xg, M[("xg", nxt)])
                    trx_batch(t, nxt, 0)
                for tc in range(CAP // 128):
                    if 2 * ge + tc - 1 > 0:
                        t.wait_ge(s_ye, 2 * ge + tc - 1)
                    if tc == 1 and nxt is not None:
                        t.wait_ge(s_sxc, 16 * nxt + 8)
                        trx_batch(t, nxt, 1)
                    ps = ps_ya if tc % 2 == 0 else ps_yb
                    for hf in range(2):
                        sl = slice(hf * 512, hf * 512 + 512)
                        for fc in range(NFC):
                            last = t.matmul(ps[:, sl],
                                            h_sb[:, fc, tc * 128:(tc + 1) * 128],
                                            w2bf[:, fc, sl],
                                            start=(fc == 0), stop=(fc == NFC - 1))
                    M[("yps", ge, tc)] = inc(last, s_t)
                ge_w2free[ge + 1] = val(s_t)

            @blk.scalar
            def _(s):
                for fc in range(NFC):
                    s.wait_ge(s_t, M[("hps", ge, fc)])
                    pp = (fc % 2) * 512
                    inc(s.activation(h_sb[:, fc, :], ps_h[:, pp:pp + CAP],
                                     AF.Relu), s_re)
                if nxt is not None:
                    trx_copies(s, nxt, npe)

            @blk.vector
            def _(v):
                for tc in range(CAP // 128):
                    v.wait_ge(s_t, M[("yps", ge, tc)])
                    if ("ysl", ge - 1, tc) in M:
                        ptp = ((ge - 1) // NE) % 2
                        v.wait_ge(s_ysl2[ptp * 2 + tc], M[("ysl", ge - 1, tc)])
                    ps = ps_ya if tc % 2 == 0 else ps_yb
                    M[("yev", ge, tc)] = inc(
                        v.tensor_copy(ysl_sb[:, tc, :], ps[:]), s_ye)

            @blk.sync
            def _(sy):
                for tc in range(CAP // 128):
                    sy.wait_ge(s_ye, M[("yev", ge, tc)])
                    r0 = e * CAP + tc * 128
                    M[("ysl", ge, tc)] = inc(
                        sy.dma_start(out=ysl_ds[tp].ap()[r0:r0 + 128, :],
                                     in_=ysl_sb[:, tc, :]), s_ysl2[tp * 2 + tc], 16)

            @blk.gpsimd
            def _(g):
                nge = ge + 1
                if nge >= NTILE * NE:
                    return
                nti, ne_ = nge // NE, nge % NE
                g.wait_ge(s_t, ge_w1free[nge])
                if nti == 0:
                    g.wait_ge(s_wc1, 32 * nge)
                    w1r = w1_in.ap()[ne_].rearrange("(dc p) f -> p dc f", p=128)
                elif nge == NE:
                    g.wait_ge(s_wc1, 32 * NE)
                    w1r = w1c_d.ap()[ne_]
                else:
                    w1r = w1c_d.ap()[ne_]
                inc(g.dma_start(out=w1bf[:, 0:4, :], in_=w1r[:, 0:4, :]), s_w1, 16)
                inc(g.dma_start(out=w1bf[:, 4:8, :], in_=w1r[:, 4:8, :]), s_w1, 16)
                if nti == 0:
                    g.wait_ge(s_w1, 32 * (nge + 1))
                    inc(g.dma_start(out=w1c_d.ap()[ne_][:, 0:4, :],
                                    in_=w1bf[:, 0:4, :]), s_wc1, 16)
                    inc(g.dma_start(out=w1c_d.ap()[ne_][:, 4:8, :],
                                    in_=w1bf[:, 4:8, :]), s_wc1, 16)
                g.wait_ge(s_t, ge_w2free[nge])
                if nti == 0:
                    g.wait_ge(s_wc2, 64 * nge)
                    w2r = w2_in.ap()[ne_].rearrange("(fc p) d -> p fc d", p=128)
                elif nge == NE:
                    g.wait_ge(s_wc2, 64 * NE)
                    w2r = w2c_d.ap()[ne_]
                else:
                    w2r = w2c_d.ap()[ne_]
                for q in range(4):
                    inc(g.dma_start(out=w2bf[:, 4 * q:4 * q + 4, :],
                                    in_=w2r[:, 4 * q:4 * q + 4, :]), s_w2, 16)
                if nti == 0:
                    g.wait_ge(s_w2, 64 * (nge + 1))
                    for q in range(4):
                        inc(g.dma_start(out=w2c_d.ap()[ne_][:, 4 * q:4 * q + 4, :],
                                        in_=w2bf[:, 4 * q:4 * q + 4, :]), s_wc2, 16)

        def final_emit(ti):
            tb = ti % 2
            tp = ti % 2
            toff = ti * TT
            from concourse.bass_core_rust import IndirectOffsetOnAxis as IOA
            for c in range(NCH):
                t0 = toff + c * 128

                @blk.gpsimd
                def _(g, c=c):
                    if c == 0:
                        for tc in range(2):
                            g.wait_ge(s_ysl2[tp * 2 + tc],
                                      M[("ysl", ti * NE + NE - 1, tc)])
                    if ("fin", ti, c - 2) in M:
                        g.wait_ge(s_v, M[("fin", ti, c - 2)])
                    elif ti >= 1 and ("fin", ti - 1, c + 2) in M:
                        g.wait_ge(s_v, M[("fin", ti - 1, c + 2)])
                    if ("fin", ti, c - 1) in M:
                        g.wait_ge(s_v, M[("fin", ti, c - 1)])
                    elif ti >= 1 and c == 0:
                        g.wait_ge(s_v, M[("fin", ti - 1, NCH - 1)])
                    inc(g.indirect_dma_start(
                        y1g[:], None,
                        ysl_ds[tp].ap(), IOA(i12[:, tp, c, 0:1], 0)), s_yg[c % 2], 16)
                    M[("gath", ti, c)] = inc(g.indirect_dma_start(
                        y2g[:], None,
                        ysl_ds[tp].ap(), IOA(i12[:, tp, c, 1:2], 0)), s_yg[c % 2], 16)

                @blk.tensor
                def _(t, c=c, t0=t0):
                    t.wait_ge(s_v, M[("scan", ti)])
                    if c < 2:
                        t.wait_ge(s_ye, val(s_ye))
                    else:
                        t.wait_ge(s_v, M[("fin", ti, c - 2)])
                    ps = ps_ya if c % 2 == 0 else ps_yb
                    last = None
                    for hf in range(2):
                        sl = slice(hf * 512, hf * 512 + 512)
                        last = t.matmul(ps[:, sl],
                                        st_bf[:, tb, c * 128:(c + 1) * 128],
                                        mtbf[:, sl], start=True, stop=True)
                    M[("ssm", ti, c)] = inc(last, s_t)

                @blk.vector
                def _(v, c=c):
                    v.wait_ge(s_t, M[("ssm", ti, c)])
                    v.wait_ge(s_yg[c % 2], M[("gath", ti, c)])
                    if ("store", c % 2) in M:
                        v.wait_ge(s_outs[c % 2], M[("store", c % 2)])
                    v.drain()
                    ps = ps_ya if c % 2 == 0 else ps_yb
                    v.scalar_tensor_tensor(outst[:, c % 2, :], y1g[:],
                                           c12[:, tp, c, 0:1], oacc[:, tb, c, :],
                                           ALU.mult, ALU.add)
                    v.drain()
                    v.scalar_tensor_tensor(outst[:, c % 2, :], y2g[:],
                                           c12[:, tp, c, 1:2], outst[:, c % 2, :],
                                           ALU.mult, ALU.add)
                    v.drain()
                    M[("fin", ti, c)] = inc(
                        v.tensor_add(outst[:, c % 2, :], outst[:, c % 2, :], ps[:]),
                        s_v)

                @blk.sync
                def _(sy, c=c, t0=t0):
                    sy.wait_ge(s_v, M[("fin", ti, c)])
                    M[("store", c % 2)] = inc(
                        sy.dma_start(out=out_d.ap()[t0:t0 + 128, :],
                                     in_=outst[:, c % 2, :]), s_outs[c % 2], 16)
                stores[0] += 1

        zero_emit(0)
        for c in range(NCH):
            s1VA(0, c)
            s1SB(0, c)
            s1VC(0, c)
            s1B1(0, c)
            s1B2(0, c)
        scan_emit(0)
        idx_emit(0)
        sx_prefetch(0, 0)

        @blk.tensor
        def _(t):
            t.wait_ge(s_wg, 48)
            t.wait_ge(s_xg, M[("xg", 0)])
            trx_batch(t, 0, 0)
            t.wait_ge(s_sxc, 8)
            trx_batch(t, 0, 1)

        @blk.scalar
        def _(s):
            trx_copies(s, 0, 0)

        for ti in range(NTILE):
            for e in range(NE):
                expert_emit(ti, e)
                if ti + 1 < NTILE:
                    if e == 0:
                        zero_emit(ti + 1)
                    if 2 <= e <= 5:
                        s1B1(ti + 1, e - 2)
                    if 3 <= e <= 6:
                        s1B2(ti + 1, e - 3)
                    if e <= 3:
                        s1VA(ti + 1, e)
                    if 1 <= e <= 4:
                        s1SB(ti + 1, e - 1)
                    if 1 <= e <= 4:
                        s1VC(ti + 1, e - 1)
                    if e == 6:
                        idx_emit(ti + 1)
                        sx_prefetch(ti + 1, 0)
                    if e == 7:
                        scan_emit(ti + 1)
            final_emit(ti)

        @blk.sync
        def _(sy):
            sy.wait_ge(s_outs[0], val(s_outs[0]))
            sy.wait_ge(s_outs[1], val(s_outs[1]))

    return nc


_NC_CACHE = [None]


def kernel(**inputs):
    x = np.ascontiguousarray(np.asarray(inputs["L0_attn_x"], dtype=np.float32))
    W1 = np.ascontiguousarray(np.asarray(inputs["L0_ffn_W1"], dtype=np.float32))
    W2 = np.ascontiguousarray(np.asarray(inputs["L0_ffn_W2"], dtype=np.float32))
    Wg = np.asarray(inputs["L0_ffn_Wg"], dtype=np.float32)
    W_in = np.ascontiguousarray(np.asarray(inputs["L0_attn_W_in"], dtype=np.float32))
    A = np.asarray(inputs["L0_attn_A"], dtype=np.float32)
    B = np.asarray(inputs["L0_attn_B_ssm"], dtype=np.float32)
    C = np.asarray(inputs["L0_attn_C_ssm"], dtype=np.float32)
    W_out = np.asarray(inputs["L0_attn_W_out"], dtype=np.float32)
    b_in = np.asarray(inputs["L0_attn_b_in"], dtype=np.float32)
    b_out = np.asarray(inputs["L0_attn_b_out"], dtype=np.float32)
    norm_w = np.asarray(inputs["L0_attn_norm_w"], dtype=np.float32)
    norm_b = np.asarray(inputs["L0_attn_norm_b"], dtype=np.float32)

    AB = np.concatenate([A, B], axis=1)
    ab_l = np.ascontiguousarray(AB.reshape(16, 128, 32).transpose(1, 0, 2))
    c_l = np.ascontiguousarray(C.reshape(16, 128, 16).transpose(1, 0, 2))
    woutt = np.ascontiguousarray(W_out.T)
    wg_l = np.ascontiguousarray(Wg.reshape(8, 128, 8).transpose(1, 0, 2))
    bin_l = np.ascontiguousarray(b_in.reshape(16, 128).T)
    nw_l = np.ascontiguousarray(norm_w.reshape(8, 128).T)
    nb_l = np.ascontiguousarray(norm_b.reshape(8, 128).T)
    bout_l = np.ascontiguousarray(b_out.reshape(1, 1024))
    basec = np.zeros((32, 1), np.float32)
    capc = np.zeros((32, 1), np.float32)
    basec[0:8, 0] = np.arange(8, dtype=np.float32) * 256.0
    capc[0:8, 0] = basec[0:8, 0] + 255.0
    identf = np.eye(128, dtype=np.float32)

    if _NC_CACHE[0] is None:
        _NC_CACHE[0] = _build()
    nc = _NC_CACHE[0]

    in_maps = []
    for i in range(8):
        xi = np.ascontiguousarray(x[2 * i:2 * i + 2].reshape(T, D))
        in_maps.append({
            "x": xi, "w1": W1, "w2": W2, "wg": wg_l, "win": W_in,
            "ab": ab_l, "cm": c_l, "woutt": woutt, "binc": bin_l,
            "nwc": nw_l, "nbc": nb_l, "bout": bout_l,
            "basec": basec, "capc": capc, "identf": identf,
        })
    res = run_bass_kernel_spmd(nc, in_maps, core_ids=list(range(8)))
    out = np.stack([np.asarray(r["out"]) for r in res.results])
    return np.ascontiguousarray(out.reshape(16, 2048, 1024)).astype(np.float32, copy=False)


# version 20
# speedup vs baseline: 1.3979x; 1.0474x over previous
import numpy as np
import concourse.bass as bass
import concourse.mybir as mybir
from concourse.bass_utils import run_bass_kernel_spmd
from contextlib import ExitStack

F32 = mybir.dt.float32
BF16 = mybir.dt.bfloat16
AF = mybir.ActivationFunctionType
ALU = mybir.AluOpType
AX = mybir.AxisListType

D = 1024
DI = 2048
NS = 16
T = 4096
TT = 512
NTILE = T // TT
NCH = TT // 128
NE = 8
NDC = D // 128
NFC = DI // 128
NEC = DI // 128
CAP = 256
SLOTS = NE * CAP
EPS = 1e-5
R1024 = 1.0 / 1024.0
EPSK = 1024.0 * EPS
K2 = 1024.0 * EPS * EPS
HL1024 = 3.4657359027997265


def _build():
    nc = bass.Bass("TRN2", target_bir_lowering=False)

    x_in = nc.dram_tensor("x", [T, D], F32, kind="ExternalInput")
    w1_in = nc.dram_tensor("w1", [NE, D, DI], F32, kind="ExternalInput")
    w2_in = nc.dram_tensor("w2", [NE, DI, D], F32, kind="ExternalInput")
    wg_in = nc.dram_tensor("wg", [128, NDC, NE], F32, kind="ExternalInput")
    win_in = nc.dram_tensor("win", [DI, D], F32, kind="ExternalInput")
    ab_in = nc.dram_tensor("ab", [128, NEC, 32], F32, kind="ExternalInput")
    c_in = nc.dram_tensor("cm", [128, NEC, NS], F32, kind="ExternalInput")
    woutt_in = nc.dram_tensor("woutt", [DI, D], F32, kind="ExternalInput")
    bin_in = nc.dram_tensor("binc", [128, NEC], F32, kind="ExternalInput")
    nw_in = nc.dram_tensor("nwc", [128, NDC], F32, kind="ExternalInput")
    nb_in = nc.dram_tensor("nbc", [128, NDC], F32, kind="ExternalInput")
    bout_in = nc.dram_tensor("bout", [1, D], F32, kind="ExternalInput")
    basec_in = nc.dram_tensor("basec", [32, 1], F32, kind="ExternalInput")
    capc_in = nc.dram_tensor("capc", [32, 1], F32, kind="ExternalInput")
    identf_in = nc.dram_tensor("identf", [128, 128], F32, kind="ExternalInput")
    out_d = nc.dram_tensor("out", [T, D], F32, kind="ExternalOutput")
    xnd_d = nc.dram_tensor("xnd", [T, D], BF16)
    xsl_ds = [nc.dram_tensor("xsl%d" % j, [SLOTS, D], BF16) for j in range(2)]
    ysl_ds = [nc.dram_tensor("ysl%d" % j, [SLOTS, D], BF16) for j in range(2)]
    w1c_d = nc.dram_tensor("w1c", [NE, 128, NDC, DI], BF16)
    w2c_d = nc.dram_tensor("w2c", [NE, 128, NFC, D], BF16)

    cnt = {}

    def inc(ins, sem, n=1):
        ins.then_inc(sem, n)
        k = id(sem)
        cnt[k] = cnt.get(k, 0) + n
        return cnt[k]

    def val(sem):
        return cnt.get(id(sem), 0)

    with ExitStack() as ctx:
        E = ctx.enter_context
        xnhi_fm = E(nc.sbuf_tensor("xnhi_fm", [128, 3, NDC, 128], BF16, is_transpose_dest=True))
        xnlo_fm = E(nc.sbuf_tensor("xnlo_fm", [128, NDC, 128], BF16, is_transpose_dest=True))
        w1bf = E(nc.sbuf_tensor("w1bf", [128, NDC, DI], BF16))
        w2bf = E(nc.sbuf_tensor("w2bf", [128, NFC, D], BF16))
        h_sb = E(nc.sbuf_tensor("h_sb", [128, NFC, CAP], BF16))
        sx_fm = E(nc.sbuf_tensor("sx_fm", [128, 2, NDC, CAP], BF16))
        xg_sb = E(nc.sbuf_tensor("xg_sb", [128, 2, D], BF16))
        identb = E(nc.sbuf_tensor("identb", [128, 128], BF16))
        ysl_sb = E(nc.sbuf_tensor("ysl_sb", [128, 2, D], BF16))
        y1g = E(nc.sbuf_tensor("y1g", [128, D], BF16))
        y2g = E(nc.sbuf_tensor("y2g", [128, D], BF16))
        zz_sb = E(nc.sbuf_tensor("zz_sb", [128, D], BF16))
        ind32 = E(nc.sbuf_tensor("ind32", [128, 32], F32))
        indT = E(nc.sbuf_tensor("indT", [32, TT], F32))
        zeroT = E(nc.sbuf_tensor("zeroT", [32, TT], F32))
        cnt_fm = E(nc.sbuf_tensor("cnt_fm", [32, TT], F32))
        stm = E(nc.sbuf_tensor("stm", [128, 32], F32))
        eq4 = E(nc.sbuf_tensor("eq4", [128, NCH, NE], F32))
        ind4 = E(nc.sbuf_tensor("ind4", [128, NCH, NE], F32))
        tsa = E(nc.sbuf_tensor("tsa", [128, NE], F32))
        tsb = E(nc.sbuf_tensor("tsb", [128, NE], F32))
        s12f = E(nc.sbuf_tensor("s12f", [128, 2], F32))
        i12 = E(nc.sbuf_tensor("i12", [128, 2, NCH, 2], mybir.dt.int32))
        c12 = E(nc.sbuf_tensor("c12", [128, 2, NCH, 2], F32))
        basec_s = E(nc.sbuf_tensor("basec_s", [32, 1], F32))
        capc_s = E(nc.sbuf_tensor("capc_s", [32, 1], F32))
        oacc = E(nc.sbuf_tensor("oacc", [128, 2, NCH, D], F32))
        xst = E(nc.sbuf_tensor("xst", [128, 3, D], F32))
        outst = E(nc.sbuf_tensor("outst", [128, 2, D], F32))
        xn_tm = E(nc.sbuf_tensor("xn_tm", [128, D], F32))
        xc_s = E(nc.sbuf_tensor("xc_s", [128, 3, D], F32))
        sq_s = E(nc.sbuf_tensor("sq_s", [128, D], F32))
        xnhi_tm = E(nc.sbuf_tensor("xnhi_tm", [128, D], BF16))
        xnlo_tm = E(nc.sbuf_tensor("xnlo_tm", [128, D], BF16))
        auf_a = E(nc.sbuf_tensor("auf_a", [32, TT], F32))
        auf_u = E(nc.sbuf_tensor("auf_u", [32, TT], F32))
        st_f = E(nc.sbuf_tensor("st_f", [NS, TT], F32))
        st_bf = E(nc.sbuf_tensor("st_bf", [NS, 2, TT], BF16))
        carry_s = E(nc.sbuf_tensor("carry_s", [NS, 1], F32))
        au_tm = E(nc.sbuf_tensor("au_tm", [128, 128], F32))
        apre = E(nc.sbuf_tensor("apre", [128, 2, 32], F32))
        term = E(nc.sbuf_tensor("term", [128, 32], F32))
        sn1 = E(nc.sbuf_tensor("sn1", [128, NS], F32))
        sn2 = E(nc.sbuf_tensor("sn2", [128, NS], F32))
        lg_sb = E(nc.sbuf_tensor("lg_sb", [128, 2, NE], F32))
        eq_sb = E(nc.sbuf_tensor("eq_sb", [128, NE], F32))
        msk_sb = E(nc.sbuf_tensor("msk_sb", [128, NE], F32))
        ind_sb = E(nc.sbuf_tensor("ind_sb", [128, NE], F32))
        e_sb = E(nc.sbuf_tensor("e_sb", [128, NE], F32))
        c_sb = E(nc.sbuf_tensor("c_sb", [128, 2, NCH, NE], F32))
        sum_s = E(nc.sbuf_tensor("sum_s", [128, 1], F32))
        mu_s = E(nc.sbuf_tensor("mu_s", [128, 1], F32))
        tq_s = E(nc.sbuf_tensor("tq_s", [128, 1], F32))
        q_s = E(nc.sbuf_tensor("q_s", [128, 1], F32))
        vs_s = E(nc.sbuf_tensor("vs_s", [128, 3], F32))
        sxc_s = E(nc.sbuf_tensor("sxc_s", [128, 4], F32))
        w_s = E(nc.sbuf_tensor("w_s", [128, 3], F32))
        rs_s = E(nc.sbuf_tensor("rs_s", [128, 3], F32))
        u1_s = E(nc.sbuf_tensor("u1_s", [128, 1], F32))
        hu1_s = E(nc.sbuf_tensor("hu1_s", [128, 1], F32))
        u2_s = E(nc.sbuf_tensor("u2_s", [128, 1], F32))
        mu2_s = E(nc.sbuf_tensor("mu2_s", [128, 1], F32))
        rs2_c = E(nc.sbuf_tensor("rs2_c", [128, NCH], F32))
        rm2_c = E(nc.sbuf_tensor("rm2_c", [128, NCH], F32))
        m1_s = E(nc.sbuf_tensor("m1_s", [128, 2], F32))
        nm1_s = E(nc.sbuf_tensor("nm1_s", [128, 2], F32))
        m2_s = E(nc.sbuf_tensor("m2_s", [128, 2], F32))
        e2_s = E(nc.sbuf_tensor("e2_s", [128, 1], F32))
        den_s = E(nc.sbuf_tensor("den_s", [128, 1], F32))
        rden_s = E(nc.sbuf_tensor("rden_s", [128, 1], F32))
        epsk_s = E(nc.sbuf_tensor("epsk_s", [128, 1], F32))
        wgbf = E(nc.sbuf_tensor("wgbf", [128, NDC, NE], BF16))

        abhi = E(nc.sbuf_tensor("abhi", [128, NEC, 32], BF16))
        ablo = E(nc.sbuf_tensor("ablo", [128, NEC, 32], BF16))

        chi = E(nc.sbuf_tensor("chi", [128, NEC, NS], BF16))
        clo = E(nc.sbuf_tensor("clo", [128, NEC, NS], BF16))

        binhi = E(nc.sbuf_tensor("binhi", [128, NEC], BF16))
        binlo = E(nc.sbuf_tensor("binlo", [128, NEC], BF16))
        nw_f = E(nc.sbuf_tensor("nw_f", [128, NDC], F32))
        ab_f = xst[:, 0, 0:512].rearrange("p (a b) -> p a b", b=32)
        c_f = xst[:, 0, 512:768].rearrange("p (a b) -> p a b", b=16)
        bin_f = xst[:, 1, 0:16]
        nb_f = xst[:, 1, 16:24]

        nbhi = E(nc.sbuf_tensor("nbhi", [128, NDC], BF16))
        nblo = E(nc.sbuf_tensor("nblo", [128, NDC], BF16))
        bout_rep = E(nc.sbuf_tensor("bout_rep", [128, D], BF16))
        onesr_bf = E(nc.sbuf_tensor("onesr_bf", [1, 128], BF16))
        onec_bf = E(nc.sbuf_tensor("onec_bf", [128, 1], BF16))
        vwhi = E(nc.sbuf_tensor("vwhi", [1, 32], BF16))
        vwlo = E(nc.sbuf_tensor("vwlo", [1, 32], BF16))
        cbhi = E(nc.sbuf_tensor("cbhi", [1, 32], BF16))
        cblo = E(nc.sbuf_tensor("cblo", [1, 32], BF16))
        cab_f = E(nc.sbuf_tensor("cab_f", [128, NDC, 32], F32))
        cabhi = E(nc.sbuf_tensor("cabhi", [128, NDC, 32], BF16))
        cablo = E(nc.sbuf_tensor("cablo", [128, NDC, 32], BF16))
        cabw_f = E(nc.sbuf_tensor("cabw_f", [128, NDC, 32], F32))
        cabwhi = E(nc.sbuf_tensor("cabwhi", [128, NDC, 32], BF16))
        cabwlo = E(nc.sbuf_tensor("cabwlo", [128, NDC, 32], BF16))
        vwrow = E(nc.sbuf_tensor("vwrow", [1, 32], F32))
        cbrow = E(nc.sbuf_tensor("cbrow", [1, 32], F32))
        vwrep = E(nc.sbuf_tensor("vwrep", [128, 32], F32))
        cbrep = E(nc.sbuf_tensor("cbrep", [128, 32], F32))
        mtbf = E(nc.sbuf_tensor("mtbf", [NS, D], BF16))
        ps_h = E(nc.psum_tensor("ps_h", [128, 1024], F32))
        ps_ya = E(nc.psum_tensor("ps_ya", [128, 1024], F32))
        ps_yb = E(nc.psum_tensor("ps_yb", [128, 1024], F32))
        ps_misc = E(nc.psum_tensor("ps_misc", [128, 512], F32))
        ps_trx = E(nc.psum_tensor("ps_trx", [128, 1024], BF16))
        ps_ap = ps_misc[:, 256:288]
        ps_lg = ps_misc[:, 128:136]
        s_c = E(nc.semaphore("s_c"))
        s_xs = [E(nc.semaphore("s_x%d" % j)) for j in range(4)]
        s_tps = [E(nc.semaphore("s_tp%d" % j)) for j in range(8)]
        s_outs = [E(nc.semaphore("s_out%d" % j)) for j in range(2)]
        s_wg = E(nc.semaphore("s_wg"))
        s_w1 = E(nc.semaphore("s_w1"))
        s_w2 = E(nc.semaphore("s_w2"))
        s_t = E(nc.semaphore("s_t"))
        s_v = E(nc.semaphore("s_v"))
        s_sc = E(nc.semaphore("s_sc"))
        s_ye = E(nc.semaphore("s_ye"))
        s_re = E(nc.semaphore("s_re"))
        s_zz = [E(nc.semaphore("s_zz%d" % j)) for j in range(2)]
        s_scat = [E(nc.semaphore("s_scat%d" % j)) for j in range(2)]
        s_xg = E(nc.semaphore("s_xg"))
        s_sxc = E(nc.semaphore("s_sxc"))
        s_wc1 = E(nc.semaphore("s_wc1"))
        s_wc2 = E(nc.semaphore("s_wc2"))
        s_ysl2 = [E(nc.semaphore("s_ysl%d" % j)) for j in range(4)]
        s_yg = [E(nc.semaphore("s_yg%d" % j)) for j in range(2)]
        s_xb = [E(nc.semaphore("s_xb%d" % j)) for j in range(2)]
        s_xnd = E(nc.semaphore("s_xnd"))

        M = {}
        LATE = {"lgcp": 0, "m6": 0, "apmm": 0}
        blk = E(nc.Block())

        @blk.sync
        def _(sy):
            pass
            inc(sy.dma_start(out=nw_f[:], in_=nw_in.ap()), s_c, 16)
            inc(sy.dma_start(out=nb_f, in_=nb_in.ap()), s_c, 16)
            inc(sy.dma_start(out=bin_f, in_=bin_in.ap()), s_c, 16)
            inc(sy.dma_start(out=ab_f, in_=ab_in.ap()), s_c, 16)
            inc(sy.dma_start(out=c_f, in_=c_in.ap()), s_c, 16)
            inc(sy.dma_start(out=basec_s[:], in_=basec_in.ap()), s_c, 16)
            inc(sy.dma_start(out=capc_s[:], in_=capc_in.ap()), s_c, 16)
            M["consts"] = val(s_c)

        @blk.gpsimd
        def _(g):
            inc(g.dma_start(out=bout_rep[:],
                            in_=bout_in.ap()[0:1, :].partition_broadcast(128)), s_wg, 16)
            inc(g.dma_start(out=wgbf[:], in_=wg_in.ap()), s_wg, 16)
            inc(g.dma_start(out=identb[:], in_=identf_in.ap()), s_wg, 16)
            w1r = w1_in.ap()[0].rearrange("(dc p) f -> p dc f", p=128)
            inc(g.dma_start(out=w1bf[:, 0:4, :], in_=w1r[:, 0:4, :]), s_w1, 16)
            inc(g.dma_start(out=w1bf[:, 4:8, :], in_=w1r[:, 4:8, :]), s_w1, 16)
            w2r = w2_in.ap()[0].rearrange("(fc p) d -> p fc d", p=128)
            for q in range(4):
                inc(g.dma_start(out=w2bf[:, 4 * q:4 * q + 4, :],
                                in_=w2r[:, 4 * q:4 * q + 4, :]), s_w2, 16)
            g.wait_ge(s_w1, 32)
            inc(g.dma_start(out=w1c_d.ap()[0][:, 0:4, :],
                            in_=w1bf[:, 0:4, :]), s_wc1, 16)
            inc(g.dma_start(out=w1c_d.ap()[0][:, 4:8, :],
                            in_=w1bf[:, 4:8, :]), s_wc1, 16)
            g.wait_ge(s_w2, 64)
            for q in range(4):
                inc(g.dma_start(out=w2c_d.ap()[0][:, 4 * q:4 * q + 4, :],
                                in_=w2bf[:, 4 * q:4 * q + 4, :]), s_wc2, 16)

        @blk.vector
        def _(v):
            v.memset(onec_bf[:], 1.0)
            v.memset(onesr_bf[:], 1.0)
            v.memset(au_tm[:, 16:32], 0.0)
            v.memset(au_tm[:, 48:64], 0.0)
            v.memset(epsk_s[:], EPSK)
            v.memset(zz_sb[:], 0.0)
            v.memset(zeroT[:], 0.0)
            v.memset(ind32[:, 8:32], 0.0)

        @blk.scalar
        def _(s):
            s.wait_ge(s_c, M["consts"])
            s.copy(abhi[:], ab_f)
            s.copy(chi[:], c_f)
            s.copy(binhi[:], bin_f)
            inc(s.copy(nbhi[:], nb_f), s_sc)
            M["hi_consts"] = val(s_sc)

        @blk.vector
        def _(v):
            v.wait_ge(s_sc, M["hi_consts"])
            v.tensor_sub(ablo[:], ab_f, abhi[:])
            v.tensor_sub(clo[:], c_f, chi[:])
            v.tensor_sub(binlo[:], bin_f, binhi[:])
            inc(v.tensor_sub(nblo[:], nb_f, nbhi[:]), s_v)
            M["lo_consts"] = val(s_v)

        caps = [ps_h[:, 0:32], ps_h[:, 512:544],
                ps_ya[:, 0:32], ps_ya[:, 512:544],
                ps_yb[:, 0:32], ps_yb[:, 512:544],
                ps_misc[:, 0:32], ps_misc[:, 128:160]]
        for ec in range(NEC):
            @blk.sync
            def _(sy, ec=ec):
                if ec >= 2:
                    sy.wait_ge(s_sc, M[("whi", ec - 2)])
                    sy.wait_ge(s_v, M[("wlo", ec - 2)])
                else:
                    sy.wait_ge(s_sc, M["hi_consts"])
                    sy.wait_ge(s_v, M["lo_consts"])
                a = sy.dma_start(out=xst[:, ec % 2, :],
                                 in_=win_in.ap()[ec * 128:(ec + 1) * 128, :])
                M[("wld", ec)] = inc(a, s_xs[ec % 2], 16)

            @blk.scalar
            def _(s, ec=ec):
                s.wait_ge(s_xs[ec % 2], M[("wld", ec)])
                if ec >= 1:
                    s.wait_ge(s_t, M[("cabmm", ec - 1)])
                M[("whi", ec)] = inc(s.copy(xnhi_tm[:], xst[:, ec % 2, :]), s_sc)

            @blk.vector
            def _(v, ec=ec):
                v.wait_ge(s_sc, M[("whi", ec)])
                M[("wlo", ec)] = inc(
                    v.tensor_sub(xnlo_tm[:], xst[:, ec % 2, :], xnhi_tm[:]), s_v)

            @blk.tensor
            def _(t, ec=ec):
                t.wait_ge(s_v, M[("wlo", ec)])
                if ec == 0:
                    t.wait_ge(s_sc, M["hi_consts"])
                    t.wait_ge(s_v, M["lo_consts"])
                last = None
                for dc in range(NDC):
                    for p, (lh, rh) in enumerate([
                            (xnhi_tm, abhi), (xnlo_tm, abhi), (xnhi_tm, ablo)]):
                        last = t.matmul(caps[dc], lh[:, dc * 128:(dc + 1) * 128],
                                        rh[:, ec, :],
                                        start=(ec == 0 and p == 0 and dc != 7),
                                        stop=(ec == NEC - 1 and p == 2
                                              and dc != 6))
                M[("cabmm", ec)] = inc(last, s_t)

        @blk.vector
        def _(v):
            v.wait_ge(s_t, M[("cabmm", NEC - 1)])
            for dc in range(NDC):
                v.tensor_copy(cab_f[:, dc, :], caps[dc])
            v.drain()
            last = None
            for dc in range(NDC):
                last = v.tensor_scalar(cabw_f[:, dc, :], cab_f[:, dc, :],
                                       nw_f[:, dc:dc + 1], None, ALU.mult)
            M["cab_f"] = inc(last, s_v)

        @blk.scalar
        def _(s):
            s.wait_ge(s_v, M["cab_f"])
            s.copy(cabhi[:], cab_f[:])
            M["cabhi"] = inc(s.copy(cabwhi[:], cabw_f[:]), s_sc)

        @blk.vector
        def _(v):
            v.wait_ge(s_sc, M["cabhi"])
            v.tensor_sub(cablo[:], cab_f[:], cabhi[:])
            M["cablo"] = inc(v.tensor_sub(cabwlo[:], cabw_f[:], cabwhi[:]), s_v)

        @blk.tensor
        def _(t):
            t.wait_ge(s_v, M["cablo"])
            for dc in range(NDC):
                for rh in (cabwhi, cabwlo):
                    t.matmul(ps_ap[0:1, :], onec_bf[:], rh[:, dc, :],
                             start=(dc == 0 and rh is cabwhi), stop=False)
            last2 = None
            for dc in range(NDC):
                for lh, rh in [(nbhi, cabhi), (nblo, cabhi), (nbhi, cablo)]:
                    last2 = t.matmul(ps_misc[0:1, 0:32], lh[:, dc:dc + 1],
                                     rh[:, dc, :], start=False, stop=False)
            for ec in range(NEC):
                for p, (lh, rh) in enumerate([(binhi, abhi), (binlo, abhi),
                                              (binhi, ablo)]):
                    last2 = t.matmul(ps_misc[0:1, 0:32], lh[:, ec:ec + 1],
                                     rh[:, ec, :], start=False,
                                     stop=(ec == NEC - 1 and p == 2))
            M["vwcb"] = inc(last2, s_t)

        @blk.vector
        def _(v):
            v.wait_ge(s_t, M["vwcb"])
            v.tensor_copy(vwrow[:], ps_ap[0:1, :])
            M["rows"] = inc(v.tensor_copy(cbrow[:], ps_misc[0:1, 0:32]), s_v)

        @blk.scalar
        def _(s):
            s.wait_ge(s_v, M["rows"])
            s.copy(vwhi[:], vwrow[:])
            M["rephi"] = inc(s.copy(cbhi[:], cbrow[:]), s_sc)

        @blk.vector
        def _(v):
            v.wait_ge(s_sc, M["rephi"])
            v.tensor_sub(vwlo[:], vwrow[:], vwhi[:])
            M["replo"] = inc(v.tensor_sub(cblo[:], cbrow[:], cbhi[:]), s_v)

        @blk.tensor
        def _(t):
            t.wait_ge(s_v, M["replo"])
            t.matmul(ps_ap[:], onesr_bf[:], vwhi[:], start=True, stop=False)
            M["rep1"] = inc(t.matmul(ps_ap[:], onesr_bf[:], vwlo[:],
                                     start=False, stop=True), s_t)

        @blk.vector
        def _(v):
            v.wait_ge(s_t, M["rep1"])
            M["vwrep"] = inc(v.tensor_copy(vwrep[:], ps_ap[:]), s_v)

        @blk.tensor
        def _(t):
            t.wait_ge(s_v, M["vwrep"])
            t.matmul(ps_ap[:], onesr_bf[:], cbhi[:], start=True, stop=False)
            M["rep2"] = inc(t.matmul(ps_ap[:], onesr_bf[:], cblo[:],
                                     start=False, stop=True), s_t)

        @blk.vector
        def _(v):
            v.wait_ge(s_t, M["rep2"])
            M["cbrep"] = inc(v.tensor_copy(cbrep[:], ps_ap[:]), s_v)

        for ec in range(NEC):
            @blk.sync
            def _(sy, ec=ec):
                sy.wait_ge(s_sc, M[("mwhi", ec - 2)] if ec >= 2 else M[("whi", NEC - 1)])
                sy.wait_ge(s_v, M[("mwlo", ec - 2)] if ec >= 2 else M[("wlo", NEC - 1)])
                a = sy.dma_start(out=xst[:, ec % 2, :],
                                 in_=woutt_in.ap()[ec * 128:(ec + 1) * 128, :])
                M[("mwld", ec)] = inc(a, s_xs[ec % 2], 16)

            @blk.scalar
            def _(s, ec=ec):
                s.wait_ge(s_xs[ec % 2], M[("mwld", ec)])
                s.wait_ge(s_t, M[("mtmm", ec - 1)] if ec >= 1 else M[("cabmm", NEC - 1)])
                M[("mwhi", ec)] = inc(s.copy(xnhi_tm[:], xst[:, ec % 2, :]), s_sc)

            @blk.vector
            def _(v, ec=ec):
                v.wait_ge(s_sc, M[("mwhi", ec)])
                M[("mwlo", ec)] = inc(
                    v.tensor_sub(xnlo_tm[:], xst[:, ec % 2, :], xnhi_tm[:]), s_v)

            @blk.tensor
            def _(t, ec=ec):
                t.wait_ge(s_v, M[("mwlo", ec)])
                if ec == 0:
                    t.wait_ge(s_v, M["cbrep"])
                last = None
                for hf in range(2):
                    sl = slice(hf * 512, hf * 512 + 512)
                    for p, (lh, rh) in enumerate([(chi, xnhi_tm), (clo, xnhi_tm),
                                                  (chi, xnlo_tm)]):
                        last = t.matmul(ps_yb[0:NS, sl], lh[:, ec, :], rh[:, sl],
                                        start=(ec == 0 and p == 0),
                                        stop=(ec == NEC - 1 and p == 2))
                M[("mtmm", ec)] = inc(last, s_t)

        @blk.vector
        def _(v):
            v.wait_ge(s_t, M[("mtmm", NEC - 1)])
            M["mtbf"] = inc(v.tensor_copy(mtbf[:], ps_yb[0:NS, :]), s_v)

        ge_w1free = {}
        ge_w2free = {}
        xfree = {}
        stores = [0]

        def s1VA(ti, c):
            g = ti * NCH + c
            gp = g % 2
            g3 = g % 3
            g4 = g % 4
            xs = g % 3
            t0 = g * 128

            @blk.sync
            def _(sy):
                if g >= 3:
                    sy.wait_ge(s_v, xfree[g - 3])
                else:
                    sy.wait_ge(s_sc, M[("mwhi", NEC - 1)])
                    sy.wait_ge(s_v, M[("mwlo", NEC - 1)])
                a = sy.dma_start(out=xst[:, xs, :], in_=x_in.ap()[t0:t0 + 128, :])
                M[("x", g)] = inc(a, s_xs[xs], 16)

            @blk.vector
            def _(v):
                v.wait_ge(s_xs[xs], M[("x", g)])
                xt = xst[:, xs, :]
                v.reduce_sum(sum_s[:], xt, AX.X)
                v.drain()
                v.tensor_scalar_mul(mu_s[:], sum_s[:], R1024)
                v.drain()
                v.tensor_scalar_sub(xc_s[:, g3, :], xt, mu_s[:])
                v.drain()
                v.tensor_mul(sq_s[:], xc_s[:, g3, :], xc_s[:, g3, :])
                v.reduce_sum(sxc_s[:, g4:g4 + 1], xc_s[:, g3, :], AX.X)
                v.drain()
                v.reduce_sum(vs_s[:, g3:g3 + 1], sq_s[:], AX.X)
                v.tensor_mul(tq_s[:], sxc_s[:, g4:g4 + 1], sxc_s[:, g4:g4 + 1])
                v.drain()
                v.scalar_tensor_tensor(q_s[:], tq_s[:], -R1024, vs_s[:, g3:g3 + 1],
                                       ALU.mult, ALU.add)
                v.drain()
                v.scalar_tensor_tensor(w_s[:, g3:g3 + 1], vs_s[:, g3:g3 + 1], EPS,
                                       q_s[:], ALU.mult, ALU.add)
                v.drain()
                M[("m1", g)] = inc(
                    v.tensor_scalar_add(w_s[:, g3:g3 + 1], w_s[:, g3:g3 + 1], K2),
                    s_v)

        def s1SB(ti, c):
            g = ti * NCH + c
            g3 = g % 3

            @blk.scalar
            def _(s):
                s.wait_ge(s_v, M[("m1", g)])
                s.activation(u1_s[:], vs_s[:, g3:g3 + 1], AF.Ln, bias=epsk_s[:])
                s.drain()
                s.mul(hu1_s[:], u1_s[:], 0.5)
                s.activation(rs_s[:, g3:g3 + 1], u1_s[:], AF.Exp, scale=-0.5)
                s.drain()
                s.mul(rs_s[:, g3:g3 + 1], rs_s[:, g3:g3 + 1], 32.0)
                s.drain()
                s.activation(u2_s[:], w_s[:, g3:g3 + 1], AF.Ln)
                s.drain()
                M[("m2", g)] = inc(
                    s.activation(rs2_c[:, c:c + 1], u2_s[:], AF.Exp,
                                 scale=-0.5, bias=hu1_s[:]), s_sc)

        def s1VC(ti, c):
            g = ti * NCH + c
            gp = g % 2
            g3 = g % 3
            g4 = g % 4
            xs = g % 3
            tb = ti % 2

            @blk.vector
            def _(v):
                v.wait_ge(s_sc, M[("m2", g)])
                if g >= 1:
                    v.wait_ge(s_tps[(g - 1) % 8], M[("tpd", g - 1)])
                    v.wait_ge(s_xnd, M[("xnd", g - 1)])
                else:
                    v.wait_ge(s_t, M[("mtmm", NEC - 1)])
                    v.wait_ge(s_c, M["consts"])
                    v.wait_ge(s_wg, 48)
                v.tensor_scalar_mul(xn_tm[:], xc_s[:, g3, :], rs_s[:, g3:g3 + 1])
                v.drain()
                v.tensor_copy(xnhi_tm[:], xn_tm[:])
                v.drain()
                v.tensor_sub(xnlo_tm[:], xn_tm[:], xnhi_tm[:])
                v.tensor_scalar_mul(mu2_s[:], sxc_s[:, g4:g4 + 1], R1024)
                v.drain()
                v.tensor_mul(mu2_s[:], mu2_s[:], rs_s[:, g3:g3 + 1])
                v.drain()
                v.tensor_mul(rm2_c[:, c:c + 1], mu2_s[:], rs2_c[:, c:c + 1])
                v.tensor_add(oacc[:, tb, c, :], xst[:, xs, :], xn_tm[:])
                v.drain()
                ii = v.tensor_add(oacc[:, tb, c, :], oacc[:, tb, c, :], bout_rep[:])
                xfree[g] = inc(ii, s_v)

            @blk.sync
            def _(sy):
                sy.wait_ge(s_v, xfree[g])
                if g >= 1:
                    sy.wait_ge(s_t, LATE["apmm"])
                stp = s_tps[g % 8]
                for dc in range(NDC):
                    inc(sy.dma_start_transpose(
                        xnhi_fm[:, g % 3, dc, :],
                        xnhi_tm[:, dc * 128:(dc + 1) * 128]), stp, 16)
                for dc in range(NDC):
                    inc(sy.dma_start_transpose(
                        xnlo_fm[:, dc, :],
                        xnlo_tm[:, dc * 128:(dc + 1) * 128]), stp, 16)
                M[("tpd", g)] = val(stp)
                M[("xnd", g)] = inc(
                    sy.dma_start(out=xnd_d.ap()[g * 128:(g + 1) * 128, :],
                                 in_=xnhi_tm[:]), s_xnd, 16)

        def s1B1(ti, c):
            g = ti * NCH + c
            tb = ti % 2
            pc = c % 2

            @blk.tensor
            def _(t):
                if g == 0:
                    t.wait_ge(s_wg, 48)
                    t.wait_ge(s_sc, M["cabhi"])
                    t.wait_ge(s_v, M["cablo"])
                    t.wait_ge(s_v, M["mtbf"])
                    t.wait_ge(s_v, M["cbrep"])
                if LATE["m6"]:
                    t.wait_ge(s_v, LATE["m6"])
                t.wait_ge(s_tps[g % 8], M[("tpd", g)])
                last = None
                for dc in range(NDC):
                    last = t.matmul(ps_lg, xnhi_fm[:, g % 3, dc, :],
                                    wgbf[:, dc, :],
                                    start=(dc == 0), stop=(dc == NDC - 1))
                M[("lgmm", g)] = inc(last, s_t)

            @blk.vector
            def _(v):
                if g >= 2:
                    v.wait_ge(s_sc, M[("m10", g - 2)])
                v.wait_ge(s_t, M[("lgmm", g)])
                M[("lgcp", g)] = inc(v.tensor_copy(lg_sb[:, pc, :], ps_lg), s_v)
                LATE["lgcp"] = M[("lgcp", g)]

            @blk.tensor
            def _(t):
                t.wait_ge(s_v, M[("lgcp", g)])
                last = None
                for p in range(3):
                    for dc in range(NDC):
                        lh = (xnhi_fm[:, g % 3, dc, :]
                              if p != 1 else xnlo_fm[:, dc, :])
                        rh = cabwhi[:, dc, :] if p != 2 else cabwlo[:, dc, :]
                        last = t.matmul(ps_ap[:], lh, rh,
                                        start=(p == 0 and dc == 0),
                                        stop=(p == 2 and dc == NDC - 1))
                M[("apmm", g)] = inc(last, s_t)
                LATE["apmm"] = M[("apmm", g)]

            @blk.vector
            def _(v):
                v.wait_ge(s_t, M[("apmm", g)])
                v.scalar_tensor_tensor(term[:], vwrep[:], rm2_c[:, c:c + 1],
                                       cbrep[:], ALU.mult, ALU.subtract)
                v.drain()
                v.scalar_tensor_tensor(apre[:, pc, :], ps_ap[:],
                                        rs2_c[:, c:c + 1], term[:],
                                        ALU.mult, ALU.subtract)
                v.drain()
                M[("m6", g)] = inc(
                    v.tensor_scalar_max(apre[:, pc, 0:NS], apre[:, pc, 0:NS],
                                        -30.0), s_v)
                LATE["m6"] = M[("m6", g)]
                v.reduce_max(m1_s[:, pc:pc + 1], lg_sb[:, pc, :], AX.X)
                v.reduce_max(nm1_s[:, pc:pc + 1], lg_sb[:, pc, :], AX.X, negate=True)
                v.drain()
                v.tensor_scalar(eq4[:, c, :], lg_sb[:, pc, :], m1_s[:, pc:pc + 1],
                                None, ALU.is_equal)
                v.drain()
                v.scalar_tensor_tensor(msk_sb[:], eq4[:, c, :], -1e30, lg_sb[:, pc, :],
                                       ALU.mult, ALU.add)
                v.drain()
                M[("m9", g)] = inc(
                    v.reduce_max(m2_s[:, pc:pc + 1], msk_sb[:], AX.X), s_v)

        def s1B2(ti, c):
            g = ti * NCH + c
            tb = ti % 2
            pc = c % 2

            @blk.scalar
            def _(s):
                if g >= 1:
                    s.wait_ge(s_v, M[("router", g - 1)])
                s.wait_ge(s_v, M[("m6", g)])
                M[("m7", g)] = inc(
                    s.activation(sn1[:], apre[:, pc, 0:NS], AF.Exp, scale=-1.0),
                    s_sc)
                s.wait_ge(s_v, M[("m9", g)])
                s.activation(e_sb[:], lg_sb[:, pc, :], AF.Exp,
                             bias=nm1_s[:, pc:pc + 1])
                M[("m10", g)] = inc(
                    s.activation(e2_s[:], m2_s[:, pc:pc + 1], AF.Exp,
                                 bias=nm1_s[:, pc:pc + 1]), s_sc)

            @blk.vector
            def _(v):
                v.wait_ge(s_sc, M[("m7", g)])
                v.tensor_copy(au_tm[:, 32:48], apre[:, pc, 16:32])
                v.tensor_scalar_add(sn2[:], sn1[:], 1.0)
                v.drain()
                v.reciprocal(au_tm[:, 0:16], sn2[:])
                v.drain()
                for rb in range(4):
                    for dst, cbk in ((auf_a, 0), (auf_u, 1)):
                        v.transpose(
                            dst[0:32, c * 128 + 32 * rb:c * 128 + 32 * rb + 32],
                            au_tm[32 * rb:32 * rb + 32, 32 * cbk:32 * cbk + 32])
                v.drain()
                v.wait_ge(s_sc, M[("m10", g)])
                v.tensor_scalar_add(den_s[:], e2_s[:], 1.0)
                v.drain()
                v.reciprocal(rden_s[:], den_s[:])
                v.tensor_scalar(ind4[:, c, :], lg_sb[:, pc, :], m2_s[:, pc:pc + 1],
                                None, ALU.is_ge)
                v.drain()
                v.tensor_copy(ind32[:, 0:8], ind4[:, c, :])
                v.drain()
                for rb in range(4):
                    v.transpose(indT[0:32, c * 128 + 32 * rb:c * 128 + 32 * rb + 32],
                                ind32[32 * rb:32 * rb + 32, 0:32])
                v.drain()
                M[("router", g)] = inc(
                    v.scalar_tensor_tensor(c_sb[:, tb, c, :], ind4[:, c, :], rden_s[:],
                                           e_sb[:], ALU.mult, ALU.mult), s_v)

        def scan_emit(ti):
            tb = ti % 2

            @blk.vector
            def _(v):
                if ti % 4 == 0:
                    v.memset(carry_s[:], 0.0)
                v.drain()
                v.tensor_tensor_scan(st_f[:], auf_a[0:NS, :], auf_u[0:NS, :],
                                     carry_s[:], ALU.mult, ALU.add)
                v.drain()
                v.tensor_copy(carry_s[:], st_f[:, TT - 1:TT])
                M[("scan", ti)] = inc(v.tensor_copy(st_bf[:, tb, :], st_f[:]), s_v)

        def sx_prefetch(ti, e):
            if e >= NE:
                return
            ge = ti * NE + e
            tp = ti % 2

            @blk.gpsimd
            def _(g):
                g.wait_ge(s_scat[tp], M[("scat", ti)])
                if ge >= 1:
                    g.wait_ge(s_t, M[("trp2", ge - 1)])
                for cc in range(CAP // 128):
                    r0 = e * CAP + cc * 128
                    inc(g.dma_start(out=xg_sb[:, cc, :],
                                    in_=xsl_ds[tp].ap()[r0:r0 + 128, :]),
                        s_xg, 16)
                M[("xg", ge)] = val(s_xg)

        def trx_batch(t, ge, cc):
            last = None
            for dc in range(NDC):
                last = t.transpose(ps_trx[:, dc * 128:(dc + 1) * 128],
                                   xg_sb[:, cc, dc * 128:(dc + 1) * 128],
                                   identb[:])
            M[("trp%d" % (cc + 1), ge)] = inc(last, s_t)

        def trx_copies(s, ge, pe):
            s.wait_ge(s_t, M[("trp1", ge)])
            if ge >= 2:
                s.wait_ge(s_t, M[("hps", ge - 2, NFC - 1)])
            for dc in range(NDC):
                inc(s.copy(sx_fm[:, pe, dc, 0:128],
                           ps_trx[:, dc * 128:(dc + 1) * 128]), s_sxc)
            s.wait_ge(s_t, M[("trp2", ge)])
            for dc in range(NDC):
                inc(s.copy(sx_fm[:, pe, dc, 128:256],
                           ps_trx[:, dc * 128:(dc + 1) * 128]), s_sxc)

        def zero_emit(ti):
            tp = ti % 2

            @blk.sync
            def _(sy):
                sy.wait_ge(s_v, M["lo_consts"])
                if ti >= 2:
                    sy.wait_ge(s_t, ge_w1free[(ti - 1) * NE])
                for r in range(SLOTS // 128):
                    inc(sy.dma_start(out=xsl_ds[tp].ap()[r * 128:(r + 1) * 128, :],
                                     in_=zz_sb[:]), s_zz[tp], 16)
                M[("zz", ti)] = val(s_zz[tp])

        def idx_emit(ti):
            tp = ti % 2
            tb = ti % 2

            @blk.vector
            def _(v):
                if ti >= 2:
                    for cc in range(NCH):
                        v.wait_ge(s_yg[cc % 2], M[("gath", ti - 2, cc)])
                v.tensor_tensor_scan(cnt_fm[:], indT[:], zeroT[:],
                                     zeroT[:, 0:1], ALU.add, ALU.add)
                v.drain()
                v.tensor_sub(indT[:], cnt_fm[:], indT[:])
                v.drain()
                v.tensor_scalar_add(indT[:], indT[:], basec_s[:])
                v.drain()
                v.tensor_scalar_min(indT[:], indT[:], capc_s[:])
                v.drain()
                for c in range(NCH):
                    for rb in range(4):
                        v.transpose(stm[32 * rb:32 * rb + 32, 0:32],
                                    indT[0:32, c * 128 + 32 * rb:c * 128 + 32 * rb + 32])
                    v.drain()
                    v.tensor_mul(tsa[:], stm[:, 0:8], eq4[:, c, :])
                    v.tensor_sub(tsb[:], ind4[:, c, :], eq4[:, c, :])
                    v.drain()
                    v.reduce_sum(s12f[:, 0:1], tsa[:], AX.X)
                    v.drain()
                    v.tensor_mul(tsa[:], stm[:, 0:8], tsb[:])
                    v.drain()
                    v.reduce_sum(s12f[:, 1:2], tsa[:], AX.X)
                    v.drain()
                    v.tensor_mul(tsa[:], c_sb[:, tb, c, :], eq4[:, c, :])
                    v.drain()
                    v.tensor_copy(i12[:, tp, c, 0:2], s12f[:])
                    v.reduce_sum(c12[:, tp, c, 0:1], tsa[:], AX.X)
                    v.drain()
                    v.tensor_mul(tsa[:], c_sb[:, tb, c, :], tsb[:])
                    v.drain()
                    v.reduce_sum(c12[:, tp, c, 1:2], tsa[:], AX.X)
                    v.drain()
                M[("idx", ti)] = inc(v.engine_nop(), s_v)

            @blk.gpsimd
            def _(g):
                g.wait_ge(s_v, M[("idx", ti)])
                g.wait_ge(s_zz[tp], M[("zz", ti)])
                g.wait_ge(s_xnd, M[("xnd", ti * NCH + NCH - 1)])
                if ti >= 1:
                    g.wait_ge(s_t, M[("trp2", ti * NE - 1)])
                from concourse.bass_core_rust import IndirectOffsetOnAxis as IOA
                for cp in range(NCH // 2):
                    if cp >= 1:
                        g.wait_ge(s_scat[tp], M[("scat2", ti, cp - 1)])
                    elif ti >= 2:
                        g.wait_ge(s_scat[tp], M[("scat", ti - 2)])
                    wv = None
                    for c in (2 * cp, 2 * cp + 1):
                        t0 = (ti * NCH + c) * 128
                        a = g.dma_start(out=xg_sb[:, c % 2, :],
                                        in_=xnd_d.ap()[t0:t0 + 128, :])
                        wv = inc(a, s_xb[0], 16)
                    g.wait_ge(s_xb[0], wv)
                    for c in (2 * cp, 2 * cp + 1):
                        inc(g.indirect_dma_start(
                            xsl_ds[tp].ap(), IOA(i12[:, tp, c, 0:1], 0),
                            xg_sb[:, c % 2, :], None), s_scat[tp], 16)
                        inc(g.indirect_dma_start(
                            xsl_ds[tp].ap(), IOA(i12[:, tp, c, 1:2], 0),
                            xg_sb[:, c % 2, :], None), s_scat[tp], 16)
                    M[("scat2", ti, cp)] = val(s_scat[tp])
                M[("scat", ti)] = val(s_scat[tp])

        def expert_emit(ti, e):
            ge = ti * NE + e
            tb = ti % 2
            tp = ti % 2
            pe = e % 2
            sx_prefetch(ti, e + 1)

            nxt = ge + 1 if ge + 1 < NTILE * NE else None
            npe = (e + 1) % 2

            @blk.tensor
            def _(t):
                t.wait_ge(s_w1, 32 * (ge + 1))
                t.wait_ge(s_sxc, 16 * ge + 16)
                for fc in range(NFC):
                    pp = (fc % 2) * 512
                    rk = 16 * ge + fc - 1
                    if rk > 0:
                        t.wait_ge(s_re, rk)
                    for dc in range(NDC):
                        last = t.matmul(ps_h[:, pp:pp + CAP],
                                        w1bf[:, dc, fc * 128:(fc + 1) * 128],
                                        sx_fm[:, pe, dc, :],
                                        start=(dc == 0), stop=(dc == NDC - 1))
                    M[("hps", ge, fc)] = inc(last, s_t)
                ge_w1free[ge + 1] = val(s_t)
                t.wait_ge(s_w2, 64 * (ge + 1))
                t.wait_ge(s_re, 16 * ge + 16)
                if e == 0:
                    t.wait_ge(s_v, val(s_v))
                cross = nxt is not None and e == NE - 1
                if nxt is not None and not cross:
                    t.wait_ge(s_xg, M[("xg", nxt)])
                    trx_batch(t, nxt, 0)
                for tc in range(CAP // 128):
                    if 2 * ge + tc - 1 > 0:
                        t.wait_ge(s_ye, 2 * ge + tc - 1)
                    if tc == 1 and nxt is not None and not cross:
                        t.wait_ge(s_sxc, 16 * nxt + 8)
                        trx_batch(t, nxt, 1)
                    ps = ps_ya if tc % 2 == 0 else ps_yb
                    for hf in range(2):
                        sl = slice(hf * 512, hf * 512 + 512)
                        for fc in range(NFC):
                            last = t.matmul(ps[:, sl],
                                            h_sb[:, fc, tc * 128:(tc + 1) * 128],
                                            w2bf[:, fc, sl],
                                            start=(fc == 0), stop=(fc == NFC - 1))
                    M[("yps", ge, tc)] = inc(last, s_t)
                    if tc == 0 and cross:
                        t.wait_ge(s_xg, M[("xg", nxt)])
                        trx_batch(t, nxt, 0)
                if cross:
                    t.wait_ge(s_sxc, 16 * nxt + 8)
                    trx_batch(t, nxt, 1)
                ge_w2free[ge + 1] = val(s_t)

            @blk.scalar
            def _(s):
                for fc in range(NFC):
                    s.wait_ge(s_t, M[("hps", ge, fc)])
                    pp = (fc % 2) * 512
                    inc(s.activation(h_sb[:, fc, :], ps_h[:, pp:pp + CAP],
                                     AF.Relu), s_re)
                if nxt is not None:
                    trx_copies(s, nxt, npe)

            @blk.vector
            def _(v):
                for tc in range(CAP // 128):
                    v.wait_ge(s_t, M[("yps", ge, tc)])
                    if ("ysl", ge - 1, tc) in M:
                        ptp = ((ge - 1) // NE) % 2
                        v.wait_ge(s_ysl2[ptp * 2 + tc], M[("ysl", ge - 1, tc)])
                    ps = ps_ya if tc % 2 == 0 else ps_yb
                    M[("yev", ge, tc)] = inc(
                        v.tensor_copy(ysl_sb[:, tc, :], ps[:]), s_ye)

            @blk.sync
            def _(sy):
                for tc in range(CAP // 128):
                    sy.wait_ge(s_ye, M[("yev", ge, tc)])
                    r0 = e * CAP + tc * 128
                    M[("ysl", ge, tc)] = inc(
                        sy.dma_start(out=ysl_ds[tp].ap()[r0:r0 + 128, :],
                                     in_=ysl_sb[:, tc, :]), s_ysl2[tp * 2 + tc], 16)

            @blk.gpsimd
            def _(g):
                nge = ge + 1
                if nge >= NTILE * NE:
                    return
                nti, ne_ = nge // NE, nge % NE
                g.wait_ge(s_t, ge_w1free[nge])
                if nti == 0:
                    g.wait_ge(s_wc1, 32 * nge)
                    w1r = w1_in.ap()[ne_].rearrange("(dc p) f -> p dc f", p=128)
                elif nge == NE:
                    g.wait_ge(s_wc1, 32 * NE)
                    w1r = w1c_d.ap()[ne_]
                else:
                    w1r = w1c_d.ap()[ne_]
                inc(g.dma_start(out=w1bf[:, 0:4, :], in_=w1r[:, 0:4, :]), s_w1, 16)
                inc(g.dma_start(out=w1bf[:, 4:8, :], in_=w1r[:, 4:8, :]), s_w1, 16)
                if nti == 0:
                    g.wait_ge(s_w1, 32 * (nge + 1))
                    inc(g.dma_start(out=w1c_d.ap()[ne_][:, 0:4, :],
                                    in_=w1bf[:, 0:4, :]), s_wc1, 16)
                    inc(g.dma_start(out=w1c_d.ap()[ne_][:, 4:8, :],
                                    in_=w1bf[:, 4:8, :]), s_wc1, 16)
                g.wait_ge(s_t, ge_w2free[nge])
                if nti == 0:
                    g.wait_ge(s_wc2, 64 * nge)
                    w2r = w2_in.ap()[ne_].rearrange("(fc p) d -> p fc d", p=128)
                elif nge == NE:
                    g.wait_ge(s_wc2, 64 * NE)
                    w2r = w2c_d.ap()[ne_]
                else:
                    w2r = w2c_d.ap()[ne_]
                for q in range(4):
                    inc(g.dma_start(out=w2bf[:, 4 * q:4 * q + 4, :],
                                    in_=w2r[:, 4 * q:4 * q + 4, :]), s_w2, 16)
                if nti == 0:
                    g.wait_ge(s_w2, 64 * (nge + 1))
                    for q in range(4):
                        inc(g.dma_start(out=w2c_d.ap()[ne_][:, 4 * q:4 * q + 4, :],
                                        in_=w2bf[:, 4 * q:4 * q + 4, :]), s_wc2, 16)

        def final_emit(ti):
            tb = ti % 2
            tp = ti % 2
            toff = ti * TT
            from concourse.bass_core_rust import IndirectOffsetOnAxis as IOA
            for c in range(NCH):
                t0 = toff + c * 128

                @blk.gpsimd
                def _(g, c=c):
                    if c == 0:
                        for tc in range(2):
                            g.wait_ge(s_ysl2[tp * 2 + tc],
                                      M[("ysl", ti * NE + NE - 1, tc)])
                    if ("fin", ti, c - 2) in M:
                        g.wait_ge(s_v, M[("fin", ti, c - 2)])
                    elif ti >= 1 and ("fin", ti - 1, c + 2) in M:
                        g.wait_ge(s_v, M[("fin", ti - 1, c + 2)])
                    if ("fin", ti, c - 1) in M:
                        g.wait_ge(s_v, M[("fin", ti, c - 1)])
                    elif ti >= 1 and c == 0:
                        g.wait_ge(s_v, M[("fin", ti - 1, NCH - 1)])
                    inc(g.indirect_dma_start(
                        y1g[:], None,
                        ysl_ds[tp].ap(), IOA(i12[:, tp, c, 0:1], 0)), s_yg[c % 2], 16)
                    M[("gath", ti, c)] = inc(g.indirect_dma_start(
                        y2g[:], None,
                        ysl_ds[tp].ap(), IOA(i12[:, tp, c, 1:2], 0)), s_yg[c % 2], 16)

                @blk.tensor
                def _(t, c=c, t0=t0):
                    t.wait_ge(s_v, M[("scan", ti)])
                    if c < 2:
                        t.wait_ge(s_ye, val(s_ye))
                    else:
                        t.wait_ge(s_v, M[("fin", ti, c - 2)])
                    ps = ps_ya if c % 2 == 0 else ps_yb
                    last = None
                    for hf in range(2):
                        sl = slice(hf * 512, hf * 512 + 512)
                        last = t.matmul(ps[:, sl],
                                        st_bf[:, tb, c * 128:(c + 1) * 128],
                                        mtbf[:, sl], start=True, stop=True)
                    M[("ssm", ti, c)] = inc(last, s_t)

                @blk.vector
                def _(v, c=c):
                    v.wait_ge(s_t, M[("ssm", ti, c)])
                    v.wait_ge(s_yg[c % 2], M[("gath", ti, c)])
                    if ("store", c % 2) in M:
                        v.wait_ge(s_outs[c % 2], M[("store", c % 2)])
                    v.drain()
                    ps = ps_ya if c % 2 == 0 else ps_yb
                    v.scalar_tensor_tensor(outst[:, c % 2, :], y1g[:],
                                           c12[:, tp, c, 0:1], oacc[:, tb, c, :],
                                           ALU.mult, ALU.add)
                    v.drain()
                    v.scalar_tensor_tensor(outst[:, c % 2, :], y2g[:],
                                           c12[:, tp, c, 1:2], outst[:, c % 2, :],
                                           ALU.mult, ALU.add)
                    v.drain()
                    M[("fin", ti, c)] = inc(
                        v.tensor_add(outst[:, c % 2, :], outst[:, c % 2, :], ps[:]),
                        s_v)

                @blk.sync
                def _(sy, c=c, t0=t0):
                    sy.wait_ge(s_v, M[("fin", ti, c)])
                    M[("store", c % 2)] = inc(
                        sy.dma_start(out=out_d.ap()[t0:t0 + 128, :],
                                     in_=outst[:, c % 2, :]), s_outs[c % 2], 16)
                stores[0] += 1

        zero_emit(0)
        for c in range(NCH):
            s1VA(0, c)
            s1SB(0, c)
            s1VC(0, c)
            s1B1(0, c)
            s1B2(0, c)
        scan_emit(0)
        idx_emit(0)
        sx_prefetch(0, 0)

        @blk.tensor
        def _(t):
            t.wait_ge(s_wg, 48)
            t.wait_ge(s_xg, M[("xg", 0)])
            trx_batch(t, 0, 0)
            t.wait_ge(s_sxc, 8)
            trx_batch(t, 0, 1)

        @blk.scalar
        def _(s):
            trx_copies(s, 0, 0)

        for ti in range(NTILE):
            for e in range(NE):
                expert_emit(ti, e)
                if ti + 1 < NTILE:
                    if e == 0:
                        zero_emit(ti + 1)
                    if 2 <= e <= 5:
                        s1B1(ti + 1, e - 2)
                    if 3 <= e <= 6:
                        s1B2(ti + 1, e - 3)
                    if e <= 3:
                        s1VA(ti + 1, e)
                    if 1 <= e <= 4:
                        s1SB(ti + 1, e - 1)
                    if 1 <= e <= 4:
                        s1VC(ti + 1, e - 1)
                    if e == 6:
                        idx_emit(ti + 1)
                        sx_prefetch(ti + 1, 0)
                    if e == 7:
                        scan_emit(ti + 1)
            final_emit(ti)

        @blk.sync
        def _(sy):
            sy.wait_ge(s_outs[0], val(s_outs[0]))
            sy.wait_ge(s_outs[1], val(s_outs[1]))

    return nc


_NC_CACHE = [None]


def kernel(**inputs):
    x = np.ascontiguousarray(np.asarray(inputs["L0_attn_x"], dtype=np.float32))
    W1 = np.ascontiguousarray(np.asarray(inputs["L0_ffn_W1"], dtype=np.float32))
    W2 = np.ascontiguousarray(np.asarray(inputs["L0_ffn_W2"], dtype=np.float32))
    Wg = np.asarray(inputs["L0_ffn_Wg"], dtype=np.float32)
    W_in = np.ascontiguousarray(np.asarray(inputs["L0_attn_W_in"], dtype=np.float32))
    A = np.asarray(inputs["L0_attn_A"], dtype=np.float32)
    B = np.asarray(inputs["L0_attn_B_ssm"], dtype=np.float32)
    C = np.asarray(inputs["L0_attn_C_ssm"], dtype=np.float32)
    W_out = np.asarray(inputs["L0_attn_W_out"], dtype=np.float32)
    b_in = np.asarray(inputs["L0_attn_b_in"], dtype=np.float32)
    b_out = np.asarray(inputs["L0_attn_b_out"], dtype=np.float32)
    norm_w = np.asarray(inputs["L0_attn_norm_w"], dtype=np.float32)
    norm_b = np.asarray(inputs["L0_attn_norm_b"], dtype=np.float32)

    AB = np.concatenate([A, B], axis=1)
    ab_l = np.ascontiguousarray(AB.reshape(16, 128, 32).transpose(1, 0, 2))
    c_l = np.ascontiguousarray(C.reshape(16, 128, 16).transpose(1, 0, 2))
    woutt = np.ascontiguousarray(W_out.T)
    wg_l = np.ascontiguousarray(Wg.reshape(8, 128, 8).transpose(1, 0, 2))
    bin_l = np.ascontiguousarray(b_in.reshape(16, 128).T)
    nw_l = np.ascontiguousarray(norm_w.reshape(8, 128).T)
    nb_l = np.ascontiguousarray(norm_b.reshape(8, 128).T)
    bout_l = np.ascontiguousarray(b_out.reshape(1, 1024))
    basec = np.zeros((32, 1), np.float32)
    capc = np.zeros((32, 1), np.float32)
    basec[0:8, 0] = np.arange(8, dtype=np.float32) * 256.0
    capc[0:8, 0] = basec[0:8, 0] + 255.0
    identf = np.eye(128, dtype=np.float32)

    if _NC_CACHE[0] is None:
        _NC_CACHE[0] = _build()
    nc = _NC_CACHE[0]

    in_maps = []
    for i in range(8):
        xi = np.ascontiguousarray(x[2 * i:2 * i + 2].reshape(T, D))
        in_maps.append({
            "x": xi, "w1": W1, "w2": W2, "wg": wg_l, "win": W_in,
            "ab": ab_l, "cm": c_l, "woutt": woutt, "binc": bin_l,
            "nwc": nw_l, "nbc": nb_l, "bout": bout_l,
            "basec": basec, "capc": capc, "identf": identf,
        })
    res = run_bass_kernel_spmd(nc, in_maps, core_ids=list(range(8)))
    out = np.stack([np.asarray(r["out"]) for r in res.results])
    return np.ascontiguousarray(out.reshape(16, 2048, 1024)).astype(np.float32, copy=False)


# version 21 (final)
# speedup vs baseline: 1.4048x; 1.0050x over previous
import numpy as np
import concourse.bass as bass
import concourse.mybir as mybir
from concourse.bass_utils import run_bass_kernel_spmd
from contextlib import ExitStack

F32 = mybir.dt.float32
BF16 = mybir.dt.bfloat16
AF = mybir.ActivationFunctionType
ALU = mybir.AluOpType
AX = mybir.AxisListType

D = 1024
DI = 2048
NS = 16
T = 4096
TT = 512
NTILE = T // TT
NCH = TT // 128
NE = 8
NDC = D // 128
NFC = DI // 128
NEC = DI // 128
CAP = 256
SLOTS = NE * CAP
EPS = 1e-5
R1024 = 1.0 / 1024.0
EPSK = 1024.0 * EPS
K2 = 1024.0 * EPS * EPS
HL1024 = 3.4657359027997265


def _build():
    nc = bass.Bass("TRN2", target_bir_lowering=False)

    x_in = nc.dram_tensor("x", [T, D], F32, kind="ExternalInput")
    w1_in = nc.dram_tensor("w1", [NE, D, DI], F32, kind="ExternalInput")
    w2_in = nc.dram_tensor("w2", [NE, DI, D], F32, kind="ExternalInput")
    wg_in = nc.dram_tensor("wg", [128, NDC, NE], F32, kind="ExternalInput")
    win_in = nc.dram_tensor("win", [DI, D], F32, kind="ExternalInput")
    ab_in = nc.dram_tensor("ab", [128, NEC, 32], F32, kind="ExternalInput")
    c_in = nc.dram_tensor("cm", [128, NEC, NS], F32, kind="ExternalInput")
    woutt_in = nc.dram_tensor("woutt", [DI, D], F32, kind="ExternalInput")
    bin_in = nc.dram_tensor("binc", [128, NEC], F32, kind="ExternalInput")
    nw_in = nc.dram_tensor("nwc", [128, NDC], F32, kind="ExternalInput")
    nb_in = nc.dram_tensor("nbc", [128, NDC], F32, kind="ExternalInput")
    bout_in = nc.dram_tensor("bout", [1, D], F32, kind="ExternalInput")
    basec_in = nc.dram_tensor("basec", [32, 1], F32, kind="ExternalInput")
    capc_in = nc.dram_tensor("capc", [32, 1], F32, kind="ExternalInput")
    identf_in = nc.dram_tensor("identf", [128, 128], F32, kind="ExternalInput")
    out_d = nc.dram_tensor("out", [T, D], F32, kind="ExternalOutput")
    xnd_d = nc.dram_tensor("xnd", [T, D], BF16)
    xsl_ds = [nc.dram_tensor("xsl%d" % j, [SLOTS, D], BF16) for j in range(2)]
    ysl_ds = [nc.dram_tensor("ysl%d" % j, [SLOTS, D], BF16) for j in range(2)]
    w1c_d = nc.dram_tensor("w1c", [NE, 128, NDC, DI], BF16)
    w2c_d = nc.dram_tensor("w2c", [NE, 128, NFC, D], BF16)

    cnt = {}

    def inc(ins, sem, n=1):
        ins.then_inc(sem, n)
        k = id(sem)
        cnt[k] = cnt.get(k, 0) + n
        return cnt[k]

    def val(sem):
        return cnt.get(id(sem), 0)

    with ExitStack() as ctx:
        E = ctx.enter_context
        xnhi_fm = E(nc.sbuf_tensor("xnhi_fm", [128, 3, NDC, 128], BF16, is_transpose_dest=True))
        xnlo_fm = E(nc.sbuf_tensor("xnlo_fm", [128, NDC, 128], BF16, is_transpose_dest=True))
        w1bf = E(nc.sbuf_tensor("w1bf", [128, NDC, DI], BF16))
        w2bf = E(nc.sbuf_tensor("w2bf", [128, NFC, D], BF16))
        h_sb = E(nc.sbuf_tensor("h_sb", [128, NFC, CAP], BF16))
        sx_fm = E(nc.sbuf_tensor("sx_fm", [128, 2, NDC, CAP], BF16))
        xg_sb = E(nc.sbuf_tensor("xg_sb", [128, 2, D], BF16))
        identb = E(nc.sbuf_tensor("identb", [128, 128], BF16))
        ysl_sb = E(nc.sbuf_tensor("ysl_sb", [128, 2, D], BF16))
        y1g = E(nc.sbuf_tensor("y1g", [128, D], BF16))
        y2g = E(nc.sbuf_tensor("y2g", [128, D], BF16))
        zz_sb = E(nc.sbuf_tensor("zz_sb", [128, D], BF16))
        ind32 = E(nc.sbuf_tensor("ind32", [128, 32], F32))
        indT = E(nc.sbuf_tensor("indT", [32, TT], F32))
        zeroT = E(nc.sbuf_tensor("zeroT", [32, TT], F32))
        cnt_fm = E(nc.sbuf_tensor("cnt_fm", [32, TT], F32))
        stm = E(nc.sbuf_tensor("stm", [128, 32], F32))
        eq4 = E(nc.sbuf_tensor("eq4", [128, NCH, NE], F32))
        ind4 = E(nc.sbuf_tensor("ind4", [128, NCH, NE], F32))
        tsa = E(nc.sbuf_tensor("tsa", [128, NE], F32))
        tsb = E(nc.sbuf_tensor("tsb", [128, NE], F32))
        s12f = E(nc.sbuf_tensor("s12f", [128, 2], F32))
        i12 = E(nc.sbuf_tensor("i12", [128, 2, NCH, 2], mybir.dt.int32))
        c12 = E(nc.sbuf_tensor("c12", [128, 2, NCH, 2], F32))
        basec_s = E(nc.sbuf_tensor("basec_s", [32, 1], F32))
        carryI = E(nc.sbuf_tensor("carryI", [32, 1], F32))
        capc_s = E(nc.sbuf_tensor("capc_s", [32, 1], F32))
        oacc = E(nc.sbuf_tensor("oacc", [128, 2, NCH, D], F32))
        xst = E(nc.sbuf_tensor("xst", [128, 3, D], F32))
        outst = E(nc.sbuf_tensor("outst", [128, 2, D], F32))
        xn_tm = E(nc.sbuf_tensor("xn_tm", [128, D], F32))
        xc_s = E(nc.sbuf_tensor("xc_s", [128, 3, D], F32))
        sq_s = E(nc.sbuf_tensor("sq_s", [128, D], F32))
        xnhi_tm = E(nc.sbuf_tensor("xnhi_tm", [128, D], BF16))
        xnlo_tm = E(nc.sbuf_tensor("xnlo_tm", [128, D], BF16))
        auf_a = E(nc.sbuf_tensor("auf_a", [32, TT], F32))
        auf_u = E(nc.sbuf_tensor("auf_u", [32, TT], F32))
        st_f = E(nc.sbuf_tensor("st_f", [NS, TT], F32))
        st_bf = E(nc.sbuf_tensor("st_bf", [NS, 2, TT], BF16))
        carry_s = E(nc.sbuf_tensor("carry_s", [NS, 1], F32))
        au_tm = E(nc.sbuf_tensor("au_tm", [128, 128], F32))
        apre = E(nc.sbuf_tensor("apre", [128, 2, 32], F32))
        term = E(nc.sbuf_tensor("term", [128, 32], F32))
        sn1 = E(nc.sbuf_tensor("sn1", [128, NS], F32))
        sn2 = E(nc.sbuf_tensor("sn2", [128, NS], F32))
        lg_sb = E(nc.sbuf_tensor("lg_sb", [128, 2, NE], F32))
        eq_sb = E(nc.sbuf_tensor("eq_sb", [128, NE], F32))
        msk_sb = E(nc.sbuf_tensor("msk_sb", [128, NE], F32))
        ind_sb = E(nc.sbuf_tensor("ind_sb", [128, NE], F32))
        e_sb = E(nc.sbuf_tensor("e_sb", [128, NE], F32))
        c_sb = E(nc.sbuf_tensor("c_sb", [128, 2, NCH, NE], F32))
        sum_s = E(nc.sbuf_tensor("sum_s", [128, 1], F32))
        mu_s = E(nc.sbuf_tensor("mu_s", [128, 1], F32))
        tq_s = E(nc.sbuf_tensor("tq_s", [128, 1], F32))
        q_s = E(nc.sbuf_tensor("q_s", [128, 1], F32))
        vs_s = E(nc.sbuf_tensor("vs_s", [128, 3], F32))
        sxc_s = E(nc.sbuf_tensor("sxc_s", [128, 4], F32))
        w_s = E(nc.sbuf_tensor("w_s", [128, 3], F32))
        rs_s = E(nc.sbuf_tensor("rs_s", [128, 3], F32))
        u1_s = E(nc.sbuf_tensor("u1_s", [128, 1], F32))
        hu1_s = E(nc.sbuf_tensor("hu1_s", [128, 1], F32))
        u2_s = E(nc.sbuf_tensor("u2_s", [128, 1], F32))
        mu2_s = E(nc.sbuf_tensor("mu2_s", [128, 1], F32))
        rs2_c = E(nc.sbuf_tensor("rs2_c", [128, NCH], F32))
        rm2_c = E(nc.sbuf_tensor("rm2_c", [128, NCH], F32))
        m1_s = E(nc.sbuf_tensor("m1_s", [128, 2], F32))
        nm1_s = E(nc.sbuf_tensor("nm1_s", [128, 2], F32))
        m2_s = E(nc.sbuf_tensor("m2_s", [128, 2], F32))
        e2_s = E(nc.sbuf_tensor("e2_s", [128, 1], F32))
        den_s = E(nc.sbuf_tensor("den_s", [128, 1], F32))
        rden_s = E(nc.sbuf_tensor("rden_s", [128, 1], F32))
        epsk_s = E(nc.sbuf_tensor("epsk_s", [128, 1], F32))
        wgbf = E(nc.sbuf_tensor("wgbf", [128, NDC, NE], BF16))

        abhi = E(nc.sbuf_tensor("abhi", [128, NEC, 32], BF16))
        ablo = E(nc.sbuf_tensor("ablo", [128, NEC, 32], BF16))

        chi = E(nc.sbuf_tensor("chi", [128, NEC, NS], BF16))
        clo = E(nc.sbuf_tensor("clo", [128, NEC, NS], BF16))

        binhi = E(nc.sbuf_tensor("binhi", [128, NEC], BF16))
        binlo = E(nc.sbuf_tensor("binlo", [128, NEC], BF16))
        nw_f = E(nc.sbuf_tensor("nw_f", [128, NDC], F32))
        ab_f = xst[:, 0, 0:512].rearrange("p (a b) -> p a b", b=32)
        c_f = xst[:, 0, 512:768].rearrange("p (a b) -> p a b", b=16)
        bin_f = xst[:, 1, 0:16]
        nb_f = xst[:, 1, 16:24]

        nbhi = E(nc.sbuf_tensor("nbhi", [128, NDC], BF16))
        nblo = E(nc.sbuf_tensor("nblo", [128, NDC], BF16))
        bout_rep = E(nc.sbuf_tensor("bout_rep", [128, D], BF16))
        onesr_bf = E(nc.sbuf_tensor("onesr_bf", [1, 128], BF16))
        onec_bf = E(nc.sbuf_tensor("onec_bf", [128, 1], BF16))
        vwhi = E(nc.sbuf_tensor("vwhi", [1, 32], BF16))
        vwlo = E(nc.sbuf_tensor("vwlo", [1, 32], BF16))
        cbhi = E(nc.sbuf_tensor("cbhi", [1, 32], BF16))
        cblo = E(nc.sbuf_tensor("cblo", [1, 32], BF16))
        cab_f = E(nc.sbuf_tensor("cab_f", [128, NDC, 32], F32))
        cabhi = E(nc.sbuf_tensor("cabhi", [128, NDC, 32], BF16))
        cablo = E(nc.sbuf_tensor("cablo", [128, NDC, 32], BF16))
        cabw_f = E(nc.sbuf_tensor("cabw_f", [128, NDC, 32], F32))
        cabwhi = E(nc.sbuf_tensor("cabwhi", [128, NDC, 32], BF16))
        cabwlo = E(nc.sbuf_tensor("cabwlo", [128, NDC, 32], BF16))
        vwrow = E(nc.sbuf_tensor("vwrow", [1, 32], F32))
        cbrow = E(nc.sbuf_tensor("cbrow", [1, 32], F32))
        vwrep = E(nc.sbuf_tensor("vwrep", [128, 32], F32))
        cbrep = E(nc.sbuf_tensor("cbrep", [128, 32], F32))
        mtbf = E(nc.sbuf_tensor("mtbf", [NS, D], BF16))
        ps_h = E(nc.psum_tensor("ps_h", [128, 1024], F32))
        ps_ya = E(nc.psum_tensor("ps_ya", [128, 1024], F32))
        ps_yb = E(nc.psum_tensor("ps_yb", [128, 1024], F32))
        ps_misc = E(nc.psum_tensor("ps_misc", [128, 512], F32))
        ps_trx = E(nc.psum_tensor("ps_trx", [128, 1024], BF16))
        ps_ap = ps_misc[:, 256:288]
        ps_lg = ps_misc[:, 128:136]
        s_c = E(nc.semaphore("s_c"))
        s_xs = [E(nc.semaphore("s_x%d" % j)) for j in range(4)]
        s_tps = [E(nc.semaphore("s_tp%d" % j)) for j in range(8)]
        s_outs = [E(nc.semaphore("s_out%d" % j)) for j in range(2)]
        s_wg = E(nc.semaphore("s_wg"))
        s_w1 = E(nc.semaphore("s_w1"))
        s_w2 = E(nc.semaphore("s_w2"))
        s_t = E(nc.semaphore("s_t"))
        s_v = E(nc.semaphore("s_v"))
        s_sc = E(nc.semaphore("s_sc"))
        s_ye = E(nc.semaphore("s_ye"))
        s_re = E(nc.semaphore("s_re"))
        s_zz = [E(nc.semaphore("s_zz%d" % j)) for j in range(2)]
        s_scat = [E(nc.semaphore("s_scat%d" % j)) for j in range(2)]
        s_xg = E(nc.semaphore("s_xg"))
        s_sxc = E(nc.semaphore("s_sxc"))
        s_wc1 = E(nc.semaphore("s_wc1"))
        s_wc2 = E(nc.semaphore("s_wc2"))
        s_ysl2 = [E(nc.semaphore("s_ysl%d" % j)) for j in range(4)]
        s_yg = [E(nc.semaphore("s_yg%d" % j)) for j in range(2)]
        s_xb = [E(nc.semaphore("s_xb%d" % j)) for j in range(2)]
        s_xnd = E(nc.semaphore("s_xnd"))

        M = {}
        LATE = {"lgcp": 0, "m6": 0, "apmm": 0}
        blk = E(nc.Block())

        @blk.sync
        def _(sy):
            pass
            inc(sy.dma_start(out=nw_f[:], in_=nw_in.ap()), s_c, 16)
            inc(sy.dma_start(out=nb_f, in_=nb_in.ap()), s_c, 16)
            inc(sy.dma_start(out=bin_f, in_=bin_in.ap()), s_c, 16)
            inc(sy.dma_start(out=ab_f, in_=ab_in.ap()), s_c, 16)
            inc(sy.dma_start(out=c_f, in_=c_in.ap()), s_c, 16)
            inc(sy.dma_start(out=basec_s[:], in_=basec_in.ap()), s_c, 16)
            inc(sy.dma_start(out=capc_s[:], in_=capc_in.ap()), s_c, 16)
            M["consts"] = val(s_c)

        @blk.gpsimd
        def _(g):
            inc(g.dma_start(out=bout_rep[:],
                            in_=bout_in.ap()[0:1, :].partition_broadcast(128)), s_wg, 16)
            inc(g.dma_start(out=wgbf[:], in_=wg_in.ap()), s_wg, 16)
            inc(g.dma_start(out=identb[:], in_=identf_in.ap()), s_wg, 16)
            w1r = w1_in.ap()[0].rearrange("(dc p) f -> p dc f", p=128)
            inc(g.dma_start(out=w1bf[:, 0:4, :], in_=w1r[:, 0:4, :]), s_w1, 16)
            inc(g.dma_start(out=w1bf[:, 4:8, :], in_=w1r[:, 4:8, :]), s_w1, 16)
            w2r = w2_in.ap()[0].rearrange("(fc p) d -> p fc d", p=128)
            for q in range(4):
                inc(g.dma_start(out=w2bf[:, 4 * q:4 * q + 4, :],
                                in_=w2r[:, 4 * q:4 * q + 4, :]), s_w2, 16)
            g.wait_ge(s_w1, 32)
            inc(g.dma_start(out=w1c_d.ap()[0][:, 0:4, :],
                            in_=w1bf[:, 0:4, :]), s_wc1, 16)
            inc(g.dma_start(out=w1c_d.ap()[0][:, 4:8, :],
                            in_=w1bf[:, 4:8, :]), s_wc1, 16)
            g.wait_ge(s_w2, 64)
            for q in range(4):
                inc(g.dma_start(out=w2c_d.ap()[0][:, 4 * q:4 * q + 4, :],
                                in_=w2bf[:, 4 * q:4 * q + 4, :]), s_wc2, 16)

        @blk.vector
        def _(v):
            v.memset(onec_bf[:], 1.0)
            v.memset(onesr_bf[:], 1.0)
            v.memset(au_tm[:, 16:32], 0.0)
            v.memset(au_tm[:, 48:64], 0.0)
            v.memset(epsk_s[:], EPSK)
            v.memset(zz_sb[:], 0.0)
            v.memset(zeroT[:], 0.0)
            v.memset(ind32[:, 8:32], 0.0)

        @blk.scalar
        def _(s):
            s.wait_ge(s_c, M["consts"])
            s.copy(abhi[:], ab_f)
            s.copy(chi[:], c_f)
            s.copy(binhi[:], bin_f)
            inc(s.copy(nbhi[:], nb_f), s_sc)
            M["hi_consts"] = val(s_sc)

        @blk.vector
        def _(v):
            v.wait_ge(s_sc, M["hi_consts"])
            v.tensor_sub(ablo[:], ab_f, abhi[:])
            v.tensor_sub(clo[:], c_f, chi[:])
            v.tensor_sub(binlo[:], bin_f, binhi[:])
            inc(v.tensor_sub(nblo[:], nb_f, nbhi[:]), s_v)
            M["lo_consts"] = val(s_v)

        caps = [ps_h[:, 0:32], ps_h[:, 512:544],
                ps_ya[:, 0:32], ps_ya[:, 512:544],
                ps_yb[:, 0:32], ps_yb[:, 512:544],
                ps_misc[:, 0:32], ps_misc[:, 128:160]]
        for ec in range(NEC):
            @blk.sync
            def _(sy, ec=ec):
                if ec >= 2:
                    sy.wait_ge(s_sc, M[("whi", ec - 2)])
                    sy.wait_ge(s_v, M[("wlo", ec - 2)])
                else:
                    sy.wait_ge(s_sc, M["hi_consts"])
                    sy.wait_ge(s_v, M["lo_consts"])
                a = sy.dma_start(out=xst[:, ec % 2, :],
                                 in_=win_in.ap()[ec * 128:(ec + 1) * 128, :])
                M[("wld", ec)] = inc(a, s_xs[ec % 2], 16)

            @blk.scalar
            def _(s, ec=ec):
                s.wait_ge(s_xs[ec % 2], M[("wld", ec)])
                if ec >= 1:
                    s.wait_ge(s_t, M[("cabmm", ec - 1)])
                M[("whi", ec)] = inc(s.copy(xnhi_tm[:], xst[:, ec % 2, :]), s_sc)

            @blk.vector
            def _(v, ec=ec):
                v.wait_ge(s_sc, M[("whi", ec)])
                M[("wlo", ec)] = inc(
                    v.tensor_sub(xnlo_tm[:], xst[:, ec % 2, :], xnhi_tm[:]), s_v)

            @blk.tensor
            def _(t, ec=ec):
                t.wait_ge(s_v, M[("wlo", ec)])
                if ec == 0:
                    t.wait_ge(s_sc, M["hi_consts"])
                    t.wait_ge(s_v, M["lo_consts"])
                last = None
                for dc in range(NDC):
                    for p, (lh, rh) in enumerate([
                            (xnhi_tm, abhi), (xnlo_tm, abhi), (xnhi_tm, ablo)]):
                        last = t.matmul(caps[dc], lh[:, dc * 128:(dc + 1) * 128],
                                        rh[:, ec, :],
                                        start=(ec == 0 and p == 0 and dc != 7),
                                        stop=(ec == NEC - 1 and p == 2
                                              and dc != 6))
                M[("cabmm", ec)] = inc(last, s_t)

        @blk.vector
        def _(v):
            v.wait_ge(s_t, M[("cabmm", NEC - 1)])
            for dc in range(NDC):
                v.tensor_copy(cab_f[:, dc, :], caps[dc])
            v.drain()
            last = None
            for dc in range(NDC):
                last = v.tensor_scalar(cabw_f[:, dc, :], cab_f[:, dc, :],
                                       nw_f[:, dc:dc + 1], None, ALU.mult)
            M["cab_f"] = inc(last, s_v)

        @blk.scalar
        def _(s):
            s.wait_ge(s_v, M["cab_f"])
            s.copy(cabhi[:], cab_f[:])
            M["cabhi"] = inc(s.copy(cabwhi[:], cabw_f[:]), s_sc)

        @blk.vector
        def _(v):
            v.wait_ge(s_sc, M["cabhi"])
            v.tensor_sub(cablo[:], cab_f[:], cabhi[:])
            M["cablo"] = inc(v.tensor_sub(cabwlo[:], cabw_f[:], cabwhi[:]), s_v)

        @blk.tensor
        def _(t):
            t.wait_ge(s_v, M["cablo"])
            for dc in range(NDC):
                for rh in (cabwhi, cabwlo):
                    t.matmul(ps_ap[0:1, :], onec_bf[:], rh[:, dc, :],
                             start=(dc == 0 and rh is cabwhi), stop=False)
            last2 = None
            for dc in range(NDC):
                for lh, rh in [(nbhi, cabhi), (nblo, cabhi), (nbhi, cablo)]:
                    last2 = t.matmul(ps_misc[0:1, 0:32], lh[:, dc:dc + 1],
                                     rh[:, dc, :], start=False, stop=False)
            for ec in range(NEC):
                for p, (lh, rh) in enumerate([(binhi, abhi), (binlo, abhi),
                                              (binhi, ablo)]):
                    last2 = t.matmul(ps_misc[0:1, 0:32], lh[:, ec:ec + 1],
                                     rh[:, ec, :], start=False,
                                     stop=(ec == NEC - 1 and p == 2))
            M["vwcb"] = inc(last2, s_t)

        @blk.vector
        def _(v):
            v.wait_ge(s_t, M["vwcb"])
            v.tensor_copy(vwrow[:], ps_ap[0:1, :])
            M["rows"] = inc(v.tensor_copy(cbrow[:], ps_misc[0:1, 0:32]), s_v)

        @blk.scalar
        def _(s):
            s.wait_ge(s_v, M["rows"])
            s.copy(vwhi[:], vwrow[:])
            M["rephi"] = inc(s.copy(cbhi[:], cbrow[:]), s_sc)

        @blk.vector
        def _(v):
            v.wait_ge(s_sc, M["rephi"])
            v.tensor_sub(vwlo[:], vwrow[:], vwhi[:])
            M["replo"] = inc(v.tensor_sub(cblo[:], cbrow[:], cbhi[:]), s_v)

        @blk.tensor
        def _(t):
            t.wait_ge(s_v, M["replo"])
            t.matmul(ps_ap[:], onesr_bf[:], vwhi[:], start=True, stop=False)
            M["rep1"] = inc(t.matmul(ps_ap[:], onesr_bf[:], vwlo[:],
                                     start=False, stop=True), s_t)

        @blk.vector
        def _(v):
            v.wait_ge(s_t, M["rep1"])
            M["vwrep"] = inc(v.tensor_copy(vwrep[:], ps_ap[:]), s_v)

        @blk.tensor
        def _(t):
            t.wait_ge(s_v, M["vwrep"])
            t.matmul(ps_ap[:], onesr_bf[:], cbhi[:], start=True, stop=False)
            M["rep2"] = inc(t.matmul(ps_ap[:], onesr_bf[:], cblo[:],
                                     start=False, stop=True), s_t)

        @blk.vector
        def _(v):
            v.wait_ge(s_t, M["rep2"])
            M["cbrep"] = inc(v.tensor_copy(cbrep[:], ps_ap[:]), s_v)

        for ec in range(NEC):
            @blk.sync
            def _(sy, ec=ec):
                sy.wait_ge(s_sc, M[("mwhi", ec - 2)] if ec >= 2 else M[("whi", NEC - 1)])
                sy.wait_ge(s_v, M[("mwlo", ec - 2)] if ec >= 2 else M[("wlo", NEC - 1)])
                a = sy.dma_start(out=xst[:, ec % 2, :],
                                 in_=woutt_in.ap()[ec * 128:(ec + 1) * 128, :])
                M[("mwld", ec)] = inc(a, s_xs[ec % 2], 16)

            @blk.scalar
            def _(s, ec=ec):
                s.wait_ge(s_xs[ec % 2], M[("mwld", ec)])
                s.wait_ge(s_t, M[("mtmm", ec - 1)] if ec >= 1 else M[("cabmm", NEC - 1)])
                M[("mwhi", ec)] = inc(s.copy(xnhi_tm[:], xst[:, ec % 2, :]), s_sc)

            @blk.vector
            def _(v, ec=ec):
                v.wait_ge(s_sc, M[("mwhi", ec)])
                M[("mwlo", ec)] = inc(
                    v.tensor_sub(xnlo_tm[:], xst[:, ec % 2, :], xnhi_tm[:]), s_v)

            @blk.tensor
            def _(t, ec=ec):
                t.wait_ge(s_v, M[("mwlo", ec)])
                if ec == 0:
                    t.wait_ge(s_v, M["cbrep"])
                last = None
                for hf in range(2):
                    sl = slice(hf * 512, hf * 512 + 512)
                    for p, (lh, rh) in enumerate([(chi, xnhi_tm), (clo, xnhi_tm),
                                                  (chi, xnlo_tm)]):
                        last = t.matmul(ps_yb[0:NS, sl], lh[:, ec, :], rh[:, sl],
                                        start=(ec == 0 and p == 0),
                                        stop=(ec == NEC - 1 and p == 2))
                M[("mtmm", ec)] = inc(last, s_t)

        @blk.vector
        def _(v):
            v.wait_ge(s_t, M[("mtmm", NEC - 1)])
            M["mtbf"] = inc(v.tensor_copy(mtbf[:], ps_yb[0:NS, :]), s_v)

        ge_w1free = {}
        ge_w2free = {}
        xfree = {}
        stores = [0]

        def s1VA(ti, c):
            g = ti * NCH + c
            gp = g % 2
            g3 = g % 3
            g4 = g % 4
            xs = g % 3
            t0 = g * 128

            @blk.sync
            def _(sy):
                if g >= 3:
                    sy.wait_ge(s_v, xfree[g - 3])
                else:
                    sy.wait_ge(s_sc, M[("mwhi", NEC - 1)])
                    sy.wait_ge(s_v, M[("mwlo", NEC - 1)])
                a = sy.dma_start(out=xst[:, xs, :], in_=x_in.ap()[t0:t0 + 128, :])
                M[("x", g)] = inc(a, s_xs[xs], 16)

            @blk.vector
            def _(v):
                v.wait_ge(s_xs[xs], M[("x", g)])
                xt = xst[:, xs, :]
                v.reduce_sum(sum_s[:], xt, AX.X)
                v.drain()
                v.tensor_scalar_mul(mu_s[:], sum_s[:], R1024)
                v.drain()
                v.tensor_scalar_sub(xc_s[:, g3, :], xt, mu_s[:])
                v.drain()
                v.tensor_mul(sq_s[:], xc_s[:, g3, :], xc_s[:, g3, :])
                v.reduce_sum(sxc_s[:, g4:g4 + 1], xc_s[:, g3, :], AX.X)
                v.drain()
                v.reduce_sum(vs_s[:, g3:g3 + 1], sq_s[:], AX.X)
                v.tensor_mul(tq_s[:], sxc_s[:, g4:g4 + 1], sxc_s[:, g4:g4 + 1])
                v.drain()
                v.scalar_tensor_tensor(q_s[:], tq_s[:], -R1024, vs_s[:, g3:g3 + 1],
                                       ALU.mult, ALU.add)
                v.drain()
                v.scalar_tensor_tensor(w_s[:, g3:g3 + 1], vs_s[:, g3:g3 + 1], EPS,
                                       q_s[:], ALU.mult, ALU.add)
                v.drain()
                M[("m1", g)] = inc(
                    v.tensor_scalar_add(w_s[:, g3:g3 + 1], w_s[:, g3:g3 + 1], K2),
                    s_v)

        def s1SB(ti, c):
            g = ti * NCH + c
            g3 = g % 3

            @blk.scalar
            def _(s):
                s.wait_ge(s_v, M[("m1", g)])
                s.activation(u1_s[:], vs_s[:, g3:g3 + 1], AF.Ln, bias=epsk_s[:])
                s.drain()
                s.mul(hu1_s[:], u1_s[:], 0.5)
                s.activation(rs_s[:, g3:g3 + 1], u1_s[:], AF.Exp, scale=-0.5)
                s.drain()
                s.mul(rs_s[:, g3:g3 + 1], rs_s[:, g3:g3 + 1], 32.0)
                s.drain()
                s.activation(u2_s[:], w_s[:, g3:g3 + 1], AF.Ln)
                s.drain()
                M[("m2", g)] = inc(
                    s.activation(rs2_c[:, c:c + 1], u2_s[:], AF.Exp,
                                 scale=-0.5, bias=hu1_s[:]), s_sc)

        def s1VC(ti, c):
            g = ti * NCH + c
            gp = g % 2
            g3 = g % 3
            g4 = g % 4
            xs = g % 3
            tb = ti % 2

            @blk.vector
            def _(v):
                v.wait_ge(s_sc, M[("m2", g)])
                if g >= 1:
                    v.wait_ge(s_tps[(g - 1) % 8], M[("tpd", g - 1)])
                    v.wait_ge(s_xnd, M[("xnd", g - 1)])
                else:
                    v.wait_ge(s_t, M[("mtmm", NEC - 1)])
                    v.wait_ge(s_c, M["consts"])
                    v.wait_ge(s_wg, 48)
                v.tensor_scalar_mul(xn_tm[:], xc_s[:, g3, :], rs_s[:, g3:g3 + 1])
                v.drain()
                v.tensor_copy(xnhi_tm[:], xn_tm[:])
                v.drain()
                v.tensor_sub(xnlo_tm[:], xn_tm[:], xnhi_tm[:])
                v.tensor_scalar_mul(mu2_s[:], sxc_s[:, g4:g4 + 1], R1024)
                v.drain()
                v.tensor_mul(mu2_s[:], mu2_s[:], rs_s[:, g3:g3 + 1])
                v.drain()
                v.tensor_mul(rm2_c[:, c:c + 1], mu2_s[:], rs2_c[:, c:c + 1])
                v.tensor_add(oacc[:, tb, c, :], xst[:, xs, :], xn_tm[:])
                v.drain()
                ii = v.tensor_add(oacc[:, tb, c, :], oacc[:, tb, c, :], bout_rep[:])
                xfree[g] = inc(ii, s_v)

            @blk.sync
            def _(sy):
                sy.wait_ge(s_v, xfree[g])
                if g >= 1:
                    sy.wait_ge(s_t, LATE["apmm"])
                stp = s_tps[g % 8]
                for dc in range(NDC):
                    inc(sy.dma_start_transpose(
                        xnhi_fm[:, g % 3, dc, :],
                        xnhi_tm[:, dc * 128:(dc + 1) * 128]), stp, 16)
                for dc in range(NDC):
                    inc(sy.dma_start_transpose(
                        xnlo_fm[:, dc, :],
                        xnlo_tm[:, dc * 128:(dc + 1) * 128]), stp, 16)
                M[("tpd", g)] = val(stp)
                M[("xnd", g)] = inc(
                    sy.dma_start(out=xnd_d.ap()[g * 128:(g + 1) * 128, :],
                                 in_=xnhi_tm[:]), s_xnd, 16)

        def s1B1(ti, c):
            g = ti * NCH + c
            tb = ti % 2
            pc = c % 2

            @blk.tensor
            def _(t):
                if g == 0:
                    t.wait_ge(s_wg, 48)
                    t.wait_ge(s_sc, M["cabhi"])
                    t.wait_ge(s_v, M["cablo"])
                    t.wait_ge(s_v, M["mtbf"])
                    t.wait_ge(s_v, M["cbrep"])
                if LATE["m6"]:
                    t.wait_ge(s_v, LATE["m6"])
                t.wait_ge(s_tps[g % 8], M[("tpd", g)])
                last = None
                for dc in range(NDC):
                    last = t.matmul(ps_lg, xnhi_fm[:, g % 3, dc, :],
                                    wgbf[:, dc, :],
                                    start=(dc == 0), stop=(dc == NDC - 1))
                M[("lgmm", g)] = inc(last, s_t)

            @blk.vector
            def _(v):
                if g >= 2:
                    v.wait_ge(s_sc, M[("m10", g - 2)])
                v.wait_ge(s_t, M[("lgmm", g)])
                M[("lgcp", g)] = inc(v.tensor_copy(lg_sb[:, pc, :], ps_lg), s_v)
                LATE["lgcp"] = M[("lgcp", g)]

            @blk.tensor
            def _(t):
                t.wait_ge(s_v, M[("lgcp", g)])
                last = None
                for p in range(3):
                    for dc in range(NDC):
                        lh = (xnhi_fm[:, g % 3, dc, :]
                              if p != 1 else xnlo_fm[:, dc, :])
                        rh = cabwhi[:, dc, :] if p != 2 else cabwlo[:, dc, :]
                        last = t.matmul(ps_ap[:], lh, rh,
                                        start=(p == 0 and dc == 0),
                                        stop=(p == 2 and dc == NDC - 1))
                M[("apmm", g)] = inc(last, s_t)
                LATE["apmm"] = M[("apmm", g)]

            @blk.vector
            def _(v):
                v.wait_ge(s_t, M[("apmm", g)])
                v.scalar_tensor_tensor(term[:], vwrep[:], rm2_c[:, c:c + 1],
                                       cbrep[:], ALU.mult, ALU.subtract)
                v.drain()
                v.scalar_tensor_tensor(apre[:, pc, :], ps_ap[:],
                                        rs2_c[:, c:c + 1], term[:],
                                        ALU.mult, ALU.subtract)
                v.drain()
                M[("m6", g)] = inc(
                    v.tensor_scalar_max(apre[:, pc, 0:NS], apre[:, pc, 0:NS],
                                        -30.0), s_v)
                LATE["m6"] = M[("m6", g)]
                v.reduce_max(m1_s[:, pc:pc + 1], lg_sb[:, pc, :], AX.X)
                v.reduce_max(nm1_s[:, pc:pc + 1], lg_sb[:, pc, :], AX.X, negate=True)
                v.drain()
                v.tensor_scalar(eq4[:, c, :], lg_sb[:, pc, :], m1_s[:, pc:pc + 1],
                                None, ALU.is_equal)
                v.drain()
                v.scalar_tensor_tensor(msk_sb[:], eq4[:, c, :], -1e30, lg_sb[:, pc, :],
                                       ALU.mult, ALU.add)
                v.drain()
                M[("m9", g)] = inc(
                    v.reduce_max(m2_s[:, pc:pc + 1], msk_sb[:], AX.X), s_v)

        def s1B2(ti, c):
            g = ti * NCH + c
            tb = ti % 2
            pc = c % 2

            @blk.scalar
            def _(s):
                if g >= 1:
                    s.wait_ge(s_v, M[("router", g - 1)])
                s.wait_ge(s_v, M[("m6", g)])
                M[("m7", g)] = inc(
                    s.activation(sn1[:], apre[:, pc, 0:NS], AF.Exp, scale=-1.0),
                    s_sc)
                s.wait_ge(s_v, M[("m9", g)])
                s.activation(e_sb[:], lg_sb[:, pc, :], AF.Exp,
                             bias=nm1_s[:, pc:pc + 1])
                M[("m10", g)] = inc(
                    s.activation(e2_s[:], m2_s[:, pc:pc + 1], AF.Exp,
                                 bias=nm1_s[:, pc:pc + 1]), s_sc)

            @blk.vector
            def _(v):
                v.wait_ge(s_sc, M[("m7", g)])
                v.tensor_copy(au_tm[:, 32:48], apre[:, pc, 16:32])
                v.tensor_scalar_add(sn2[:], sn1[:], 1.0)
                v.drain()
                v.reciprocal(au_tm[:, 0:16], sn2[:])
                v.drain()
                for rb in range(4):
                    for dst, cbk in ((auf_a, 0), (auf_u, 1)):
                        v.transpose(
                            dst[0:32, c * 128 + 32 * rb:c * 128 + 32 * rb + 32],
                            au_tm[32 * rb:32 * rb + 32, 32 * cbk:32 * cbk + 32])
                v.drain()
                v.wait_ge(s_sc, M[("m10", g)])
                v.tensor_scalar_add(den_s[:], e2_s[:], 1.0)
                v.drain()
                v.reciprocal(rden_s[:], den_s[:])
                v.tensor_scalar(ind4[:, c, :], lg_sb[:, pc, :], m2_s[:, pc:pc + 1],
                                None, ALU.is_ge)
                v.drain()
                v.tensor_copy(ind32[:, 0:8], ind4[:, c, :])
                v.drain()
                for rb in range(4):
                    v.transpose(indT[0:32, c * 128 + 32 * rb:c * 128 + 32 * rb + 32],
                                ind32[32 * rb:32 * rb + 32, 0:32])
                v.drain()
                M[("router", g)] = inc(
                    v.scalar_tensor_tensor(c_sb[:, tb, c, :], ind4[:, c, :], rden_s[:],
                                           e_sb[:], ALU.mult, ALU.mult), s_v)

        def scan_emit(ti):
            tb = ti % 2

            @blk.vector
            def _(v):
                if ti % 4 == 0:
                    v.memset(carry_s[:], 0.0)
                v.drain()
                v.tensor_tensor_scan(st_f[:], auf_a[0:NS, :], auf_u[0:NS, :],
                                     carry_s[:], ALU.mult, ALU.add)
                v.drain()
                v.tensor_copy(carry_s[:], st_f[:, TT - 1:TT])
                M[("scan", ti)] = inc(v.tensor_copy(st_bf[:, tb, :], st_f[:]), s_v)

        def sx_prefetch(ti, e):
            if e >= NE:
                return
            ge = ti * NE + e
            tp = ti % 2

            @blk.gpsimd
            def _(g):
                g.wait_ge(s_scat[tp], M[("scat", ti)])
                if ge >= 1:
                    g.wait_ge(s_t, M[("trp2", ge - 1)])
                for cc in range(CAP // 128):
                    r0 = e * CAP + cc * 128
                    inc(g.dma_start(out=xg_sb[:, cc, :],
                                    in_=xsl_ds[tp].ap()[r0:r0 + 128, :]),
                        s_xg, 16)
                M[("xg", ge)] = val(s_xg)

        def trx_batch(t, ge, cc):
            last = None
            for dc in range(NDC):
                last = t.transpose(ps_trx[:, dc * 128:(dc + 1) * 128],
                                   xg_sb[:, cc, dc * 128:(dc + 1) * 128],
                                   identb[:])
            M[("trp%d" % (cc + 1), ge)] = inc(last, s_t)

        def trx_copies(s, ge, pe):
            s.wait_ge(s_t, M[("trp1", ge)])
            if ge >= 2:
                s.wait_ge(s_t, M[("hps", ge - 2, NFC - 1)])
            for dc in range(NDC):
                inc(s.copy(sx_fm[:, pe, dc, 0:128],
                           ps_trx[:, dc * 128:(dc + 1) * 128]), s_sxc)
            s.wait_ge(s_t, M[("trp2", ge)])
            for dc in range(NDC):
                inc(s.copy(sx_fm[:, pe, dc, 128:256],
                           ps_trx[:, dc * 128:(dc + 1) * 128]), s_sxc)

        def zero_emit(ti):
            tp = ti % 2

            @blk.sync
            def _(sy):
                sy.wait_ge(s_v, M["lo_consts"])
                if ti >= 2:
                    sy.wait_ge(s_t, ge_w1free[(ti - 1) * NE])
                for r in range(SLOTS // 128):
                    inc(sy.dma_start(out=xsl_ds[tp].ap()[r * 128:(r + 1) * 128, :],
                                     in_=zz_sb[:]), s_zz[tp], 16)
                M[("zz", ti)] = val(s_zz[tp])

        def idx_vec(ti, c):
            tp = ti % 2
            tb = ti % 2

            @blk.vector
            def _(v):
                if ti >= 2:
                    v.wait_ge(s_yg[c % 2], M[("gath", ti - 2, c)])
                cs = slice(c * 128, (c + 1) * 128)
                init = zeroT[:, 0:1] if c == 0 else carryI[:]
                v.tensor_tensor_scan(cnt_fm[:, cs], indT[:, cs],
                                     zeroT[:, 0:128], init, ALU.add, ALU.add)
                v.drain()
                if c < NCH - 1:
                    v.tensor_copy(carryI[:],
                                  cnt_fm[:, c * 128 + 127:c * 128 + 128])
                v.tensor_sub(indT[:, cs], cnt_fm[:, cs], indT[:, cs])
                v.drain()
                v.tensor_scalar_add(indT[:, cs], indT[:, cs], basec_s[:])
                v.drain()
                v.tensor_scalar_min(indT[:, cs], indT[:, cs], capc_s[:])
                v.drain()
                for rb in range(4):
                    v.transpose(stm[32 * rb:32 * rb + 32, 0:32],
                                indT[0:32, c * 128 + 32 * rb:c * 128 + 32 * rb + 32])
                v.drain()
                v.tensor_mul(tsa[:], stm[:, 0:8], eq4[:, c, :])
                v.tensor_sub(tsb[:], ind4[:, c, :], eq4[:, c, :])
                v.drain()
                v.reduce_sum(s12f[:, 0:1], tsa[:], AX.X)
                v.drain()
                v.tensor_mul(tsa[:], stm[:, 0:8], tsb[:])
                v.drain()
                v.reduce_sum(s12f[:, 1:2], tsa[:], AX.X)
                v.drain()
                v.tensor_mul(tsa[:], c_sb[:, tb, c, :], eq4[:, c, :])
                v.drain()
                v.tensor_copy(i12[:, tp, c, 0:2], s12f[:])
                v.reduce_sum(c12[:, tp, c, 0:1], tsa[:], AX.X)
                v.drain()
                v.tensor_mul(tsa[:], c_sb[:, tb, c, :], tsb[:])
                v.drain()
                v.reduce_sum(c12[:, tp, c, 1:2], tsa[:], AX.X)
                v.drain()
                if c == NCH - 1:
                    M[("idx", ti)] = inc(v.engine_nop(), s_v)

        def idx_emit(ti):
            tp = ti % 2
            tb = ti % 2

            @blk.gpsimd
            def _(g):
                g.wait_ge(s_v, M[("idx", ti)])
                g.wait_ge(s_zz[tp], M[("zz", ti)])
                g.wait_ge(s_xnd, M[("xnd", ti * NCH + NCH - 1)])
                if ti >= 1:
                    g.wait_ge(s_t, M[("trp2", ti * NE - 1)])
                from concourse.bass_core_rust import IndirectOffsetOnAxis as IOA
                for cp in range(NCH // 2):
                    if cp >= 1:
                        g.wait_ge(s_scat[tp], M[("scat2", ti, cp - 1)])
                    elif ti >= 2:
                        g.wait_ge(s_scat[tp], M[("scat", ti - 2)])
                    wv = None
                    for c in (2 * cp, 2 * cp + 1):
                        t0 = (ti * NCH + c) * 128
                        a = g.dma_start(out=xg_sb[:, c % 2, :],
                                        in_=xnd_d.ap()[t0:t0 + 128, :])
                        wv = inc(a, s_xb[0], 16)
                    g.wait_ge(s_xb[0], wv)
                    for c in (2 * cp, 2 * cp + 1):
                        inc(g.indirect_dma_start(
                            xsl_ds[tp].ap(), IOA(i12[:, tp, c, 0:1], 0),
                            xg_sb[:, c % 2, :], None), s_scat[tp], 16)
                        inc(g.indirect_dma_start(
                            xsl_ds[tp].ap(), IOA(i12[:, tp, c, 1:2], 0),
                            xg_sb[:, c % 2, :], None), s_scat[tp], 16)
                    M[("scat2", ti, cp)] = val(s_scat[tp])
                M[("scat", ti)] = val(s_scat[tp])

        def expert_emit(ti, e):
            ge = ti * NE + e
            tb = ti % 2
            tp = ti % 2
            pe = e % 2
            sx_prefetch(ti, e + 1)

            nxt = ge + 1 if ge + 1 < NTILE * NE else None
            npe = (e + 1) % 2

            @blk.tensor
            def _(t):
                t.wait_ge(s_w1, 32 * (ge + 1))
                t.wait_ge(s_sxc, 16 * ge + 16)
                for fc in range(NFC):
                    pp = (fc % 2) * 512
                    rk = 16 * ge + fc - 1
                    if rk > 0:
                        t.wait_ge(s_re, rk)
                    for dc in range(NDC):
                        last = t.matmul(ps_h[:, pp:pp + CAP],
                                        w1bf[:, dc, fc * 128:(fc + 1) * 128],
                                        sx_fm[:, pe, dc, :],
                                        start=(dc == 0), stop=(dc == NDC - 1))
                    M[("hps", ge, fc)] = inc(last, s_t)
                ge_w1free[ge + 1] = val(s_t)
                t.wait_ge(s_w2, 64 * (ge + 1))
                t.wait_ge(s_re, 16 * ge + 16)
                if e == 0:
                    t.wait_ge(s_v, val(s_v))
                cross = nxt is not None and e == NE - 1
                if nxt is not None and not cross:
                    t.wait_ge(s_xg, M[("xg", nxt)])
                    trx_batch(t, nxt, 0)
                for tc in range(CAP // 128):
                    if 2 * ge + tc - 1 > 0:
                        t.wait_ge(s_ye, 2 * ge + tc - 1)
                    if tc == 1 and nxt is not None and not cross:
                        t.wait_ge(s_sxc, 16 * nxt + 8)
                        trx_batch(t, nxt, 1)
                    ps = ps_ya if tc % 2 == 0 else ps_yb
                    for hf in range(2):
                        sl = slice(hf * 512, hf * 512 + 512)
                        for fc in range(NFC):
                            last = t.matmul(ps[:, sl],
                                            h_sb[:, fc, tc * 128:(tc + 1) * 128],
                                            w2bf[:, fc, sl],
                                            start=(fc == 0), stop=(fc == NFC - 1))
                    M[("yps", ge, tc)] = inc(last, s_t)
                    if tc == 0 and cross:
                        t.wait_ge(s_xg, M[("xg", nxt)])
                        trx_batch(t, nxt, 0)
                if cross:
                    t.wait_ge(s_sxc, 16 * nxt + 8)
                    trx_batch(t, nxt, 1)
                ge_w2free[ge + 1] = val(s_t)

            @blk.scalar
            def _(s):
                for fc in range(NFC):
                    s.wait_ge(s_t, M[("hps", ge, fc)])
                    pp = (fc % 2) * 512
                    inc(s.activation(h_sb[:, fc, :], ps_h[:, pp:pp + CAP],
                                     AF.Relu), s_re)
                if nxt is not None:
                    trx_copies(s, nxt, npe)

            @blk.vector
            def _(v):
                for tc in range(CAP // 128):
                    v.wait_ge(s_t, M[("yps", ge, tc)])
                    if ("ysl", ge - 1, tc) in M:
                        ptp = ((ge - 1) // NE) % 2
                        v.wait_ge(s_ysl2[ptp * 2 + tc], M[("ysl", ge - 1, tc)])
                    ps = ps_ya if tc % 2 == 0 else ps_yb
                    M[("yev", ge, tc)] = inc(
                        v.tensor_copy(ysl_sb[:, tc, :], ps[:]), s_ye)

            @blk.sync
            def _(sy):
                for tc in range(CAP // 128):
                    sy.wait_ge(s_ye, M[("yev", ge, tc)])
                    r0 = e * CAP + tc * 128
                    M[("ysl", ge, tc)] = inc(
                        sy.dma_start(out=ysl_ds[tp].ap()[r0:r0 + 128, :],
                                     in_=ysl_sb[:, tc, :]), s_ysl2[tp * 2 + tc], 16)

            @blk.gpsimd
            def _(g):
                nge = ge + 1
                if nge >= NTILE * NE:
                    return
                nti, ne_ = nge // NE, nge % NE
                g.wait_ge(s_t, ge_w1free[nge])
                if nti == 0:
                    g.wait_ge(s_wc1, 32 * nge)
                    w1r = w1_in.ap()[ne_].rearrange("(dc p) f -> p dc f", p=128)
                elif nge == NE:
                    g.wait_ge(s_wc1, 32 * NE)
                    w1r = w1c_d.ap()[ne_]
                else:
                    w1r = w1c_d.ap()[ne_]
                inc(g.dma_start(out=w1bf[:, 0:4, :], in_=w1r[:, 0:4, :]), s_w1, 16)
                inc(g.dma_start(out=w1bf[:, 4:8, :], in_=w1r[:, 4:8, :]), s_w1, 16)
                if nti == 0:
                    g.wait_ge(s_w1, 32 * (nge + 1))
                    inc(g.dma_start(out=w1c_d.ap()[ne_][:, 0:4, :],
                                    in_=w1bf[:, 0:4, :]), s_wc1, 16)
                    inc(g.dma_start(out=w1c_d.ap()[ne_][:, 4:8, :],
                                    in_=w1bf[:, 4:8, :]), s_wc1, 16)
                g.wait_ge(s_t, ge_w2free[nge])
                if nti == 0:
                    g.wait_ge(s_wc2, 64 * nge)
                    w2r = w2_in.ap()[ne_].rearrange("(fc p) d -> p fc d", p=128)
                elif nge == NE:
                    g.wait_ge(s_wc2, 64 * NE)
                    w2r = w2c_d.ap()[ne_]
                else:
                    w2r = w2c_d.ap()[ne_]
                for q in range(4):
                    inc(g.dma_start(out=w2bf[:, 4 * q:4 * q + 4, :],
                                    in_=w2r[:, 4 * q:4 * q + 4, :]), s_w2, 16)
                if nti == 0:
                    g.wait_ge(s_w2, 64 * (nge + 1))
                    for q in range(4):
                        inc(g.dma_start(out=w2c_d.ap()[ne_][:, 4 * q:4 * q + 4, :],
                                        in_=w2bf[:, 4 * q:4 * q + 4, :]), s_wc2, 16)

        def final_emit(ti):
            tb = ti % 2
            tp = ti % 2
            toff = ti * TT
            from concourse.bass_core_rust import IndirectOffsetOnAxis as IOA
            for c in range(NCH):
                t0 = toff + c * 128

                @blk.gpsimd
                def _(g, c=c):
                    if c == 0:
                        for tc in range(2):
                            g.wait_ge(s_ysl2[tp * 2 + tc],
                                      M[("ysl", ti * NE + NE - 1, tc)])
                    if ("fin", ti, c - 2) in M:
                        g.wait_ge(s_v, M[("fin", ti, c - 2)])
                    elif ti >= 1 and ("fin", ti - 1, c + 2) in M:
                        g.wait_ge(s_v, M[("fin", ti - 1, c + 2)])
                    if ("fin", ti, c - 1) in M:
                        g.wait_ge(s_v, M[("fin", ti, c - 1)])
                    elif ti >= 1 and c == 0:
                        g.wait_ge(s_v, M[("fin", ti - 1, NCH - 1)])
                    inc(g.indirect_dma_start(
                        y1g[:], None,
                        ysl_ds[tp].ap(), IOA(i12[:, tp, c, 0:1], 0)), s_yg[c % 2], 16)
                    M[("gath", ti, c)] = inc(g.indirect_dma_start(
                        y2g[:], None,
                        ysl_ds[tp].ap(), IOA(i12[:, tp, c, 1:2], 0)), s_yg[c % 2], 16)

                @blk.tensor
                def _(t, c=c, t0=t0):
                    t.wait_ge(s_v, M[("scan", ti)])
                    if c < 2:
                        t.wait_ge(s_ye, val(s_ye))
                    else:
                        t.wait_ge(s_v, M[("fin", ti, c - 2)])
                    ps = ps_ya if c % 2 == 0 else ps_yb
                    last = None
                    for hf in range(2):
                        sl = slice(hf * 512, hf * 512 + 512)
                        last = t.matmul(ps[:, sl],
                                        st_bf[:, tb, c * 128:(c + 1) * 128],
                                        mtbf[:, sl], start=True, stop=True)
                    M[("ssm", ti, c)] = inc(last, s_t)

                @blk.vector
                def _(v, c=c):
                    v.wait_ge(s_t, M[("ssm", ti, c)])
                    v.wait_ge(s_yg[c % 2], M[("gath", ti, c)])
                    if ("store", c % 2) in M:
                        v.wait_ge(s_outs[c % 2], M[("store", c % 2)])
                    v.drain()
                    ps = ps_ya if c % 2 == 0 else ps_yb
                    v.scalar_tensor_tensor(outst[:, c % 2, :], y1g[:],
                                           c12[:, tp, c, 0:1], oacc[:, tb, c, :],
                                           ALU.mult, ALU.add)
                    v.drain()
                    v.scalar_tensor_tensor(outst[:, c % 2, :], y2g[:],
                                           c12[:, tp, c, 1:2], outst[:, c % 2, :],
                                           ALU.mult, ALU.add)
                    v.drain()
                    M[("fin", ti, c)] = inc(
                        v.tensor_add(outst[:, c % 2, :], outst[:, c % 2, :], ps[:]),
                        s_v)

                @blk.sync
                def _(sy, c=c, t0=t0):
                    sy.wait_ge(s_v, M[("fin", ti, c)])
                    M[("store", c % 2)] = inc(
                        sy.dma_start(out=out_d.ap()[t0:t0 + 128, :],
                                     in_=outst[:, c % 2, :]), s_outs[c % 2], 16)
                stores[0] += 1

        zero_emit(0)
        for c in range(NCH):
            s1VA(0, c)
            s1SB(0, c)
            s1VC(0, c)
            s1B1(0, c)
            s1B2(0, c)
        scan_emit(0)
        for c in range(NCH):
            idx_vec(0, c)
        idx_emit(0)
        sx_prefetch(0, 0)

        @blk.tensor
        def _(t):
            t.wait_ge(s_wg, 48)
            t.wait_ge(s_xg, M[("xg", 0)])
            trx_batch(t, 0, 0)
            t.wait_ge(s_sxc, 8)
            trx_batch(t, 0, 1)

        @blk.scalar
        def _(s):
            trx_copies(s, 0, 0)

        for ti in range(NTILE):
            for e in range(NE):
                expert_emit(ti, e)
                if ti + 1 < NTILE:
                    if e == 0:
                        zero_emit(ti + 1)
                    if 2 <= e <= 5:
                        s1B1(ti + 1, e - 2)
                    if 3 <= e <= 6:
                        s1B2(ti + 1, e - 3)
                        idx_vec(ti + 1, e - 3)
                    if e <= 3:
                        s1VA(ti + 1, e)
                    if 1 <= e <= 4:
                        s1SB(ti + 1, e - 1)
                    if 1 <= e <= 4:
                        s1VC(ti + 1, e - 1)
                    if e == 6:
                        idx_emit(ti + 1)
                        sx_prefetch(ti + 1, 0)
                    if e == 7:
                        scan_emit(ti + 1)
            final_emit(ti)

        @blk.sync
        def _(sy):
            sy.wait_ge(s_outs[0], val(s_outs[0]))
            sy.wait_ge(s_outs[1], val(s_outs[1]))

    return nc


_NC_CACHE = [None]


def kernel(**inputs):
    x = np.ascontiguousarray(np.asarray(inputs["L0_attn_x"], dtype=np.float32))
    W1 = np.ascontiguousarray(np.asarray(inputs["L0_ffn_W1"], dtype=np.float32))
    W2 = np.ascontiguousarray(np.asarray(inputs["L0_ffn_W2"], dtype=np.float32))
    Wg = np.asarray(inputs["L0_ffn_Wg"], dtype=np.float32)
    W_in = np.ascontiguousarray(np.asarray(inputs["L0_attn_W_in"], dtype=np.float32))
    A = np.asarray(inputs["L0_attn_A"], dtype=np.float32)
    B = np.asarray(inputs["L0_attn_B_ssm"], dtype=np.float32)
    C = np.asarray(inputs["L0_attn_C_ssm"], dtype=np.float32)
    W_out = np.asarray(inputs["L0_attn_W_out"], dtype=np.float32)
    b_in = np.asarray(inputs["L0_attn_b_in"], dtype=np.float32)
    b_out = np.asarray(inputs["L0_attn_b_out"], dtype=np.float32)
    norm_w = np.asarray(inputs["L0_attn_norm_w"], dtype=np.float32)
    norm_b = np.asarray(inputs["L0_attn_norm_b"], dtype=np.float32)

    AB = np.concatenate([A, B], axis=1)
    ab_l = np.ascontiguousarray(AB.reshape(16, 128, 32).transpose(1, 0, 2))
    c_l = np.ascontiguousarray(C.reshape(16, 128, 16).transpose(1, 0, 2))
    woutt = np.ascontiguousarray(W_out.T)
    wg_l = np.ascontiguousarray(Wg.reshape(8, 128, 8).transpose(1, 0, 2))
    bin_l = np.ascontiguousarray(b_in.reshape(16, 128).T)
    nw_l = np.ascontiguousarray(norm_w.reshape(8, 128).T)
    nb_l = np.ascontiguousarray(norm_b.reshape(8, 128).T)
    bout_l = np.ascontiguousarray(b_out.reshape(1, 1024))
    basec = np.zeros((32, 1), np.float32)
    capc = np.zeros((32, 1), np.float32)
    basec[0:8, 0] = np.arange(8, dtype=np.float32) * 256.0
    capc[0:8, 0] = basec[0:8, 0] + 255.0
    identf = np.eye(128, dtype=np.float32)

    if _NC_CACHE[0] is None:
        _NC_CACHE[0] = _build()
    nc = _NC_CACHE[0]

    in_maps = []
    for i in range(8):
        xi = np.ascontiguousarray(x[2 * i:2 * i + 2].reshape(T, D))
        in_maps.append({
            "x": xi, "w1": W1, "w2": W2, "wg": wg_l, "win": W_in,
            "ab": ab_l, "cm": c_l, "woutt": woutt, "binc": bin_l,
            "nwc": nw_l, "nbc": nb_l, "bout": bout_l,
            "basec": basec, "capc": capc, "identf": identf,
        })
    res = run_bass_kernel_spmd(nc, in_maps, core_ids=list(range(8)))
    out = np.stack([np.asarray(r["out"]) for r in res.results])
    return np.ascontiguousarray(out.reshape(16, 2048, 1024)).astype(np.float32, copy=False)


# version 22
# speedup vs baseline: 1.4456x; 1.0290x over previous
import numpy as np
import concourse.bass as bass
import concourse.mybir as mybir
from concourse.bass_utils import run_bass_kernel_spmd
from contextlib import ExitStack

F32 = mybir.dt.float32
BF16 = mybir.dt.bfloat16
AF = mybir.ActivationFunctionType
ALU = mybir.AluOpType
AX = mybir.AxisListType

D = 1024
DI = 2048
NS = 16
T = 4096
TT = 512
NTILE = T // TT
NCH = TT // 128
NE = 8
NDC = D // 128
NFC = DI // 128
NEC = DI // 128
CAP = 256
SLOTS = NE * CAP
EPS = 1e-5
R1024 = 1.0 / 1024.0
EPSK = 1024.0 * EPS
K2 = 1024.0 * EPS * EPS
HL1024 = 3.4657359027997265


def _build():
    nc = bass.Bass("TRN2", target_bir_lowering=False)

    x_in = nc.dram_tensor("x", [T, D], F32, kind="ExternalInput")
    w1_in = nc.dram_tensor("w1", [NE, D, DI], F32, kind="ExternalInput")
    w2_in = nc.dram_tensor("w2", [NE, DI, D], F32, kind="ExternalInput")
    wg_in = nc.dram_tensor("wg", [128, NDC, NE], F32, kind="ExternalInput")
    win_in = nc.dram_tensor("win", [DI, D], F32, kind="ExternalInput")
    ab_in = nc.dram_tensor("ab", [128, NEC, 32], F32, kind="ExternalInput")
    c_in = nc.dram_tensor("cm", [128, NEC, NS], F32, kind="ExternalInput")
    woutt_in = nc.dram_tensor("woutt", [DI, D], F32, kind="ExternalInput")
    bin_in = nc.dram_tensor("binc", [128, NEC], F32, kind="ExternalInput")
    nw_in = nc.dram_tensor("nwc", [128, NDC], F32, kind="ExternalInput")
    nb_in = nc.dram_tensor("nbc", [128, NDC], F32, kind="ExternalInput")
    bout_in = nc.dram_tensor("bout", [1, D], F32, kind="ExternalInput")
    basec_in = nc.dram_tensor("basec", [32, 1], F32, kind="ExternalInput")
    capc_in = nc.dram_tensor("capc", [32, 1], F32, kind="ExternalInput")
    identf_in = nc.dram_tensor("identf", [128, 128], F32, kind="ExternalInput")
    out_d = nc.dram_tensor("out", [T, D], F32, kind="ExternalOutput")
    xnd_d = nc.dram_tensor("xnd", [T, D], BF16)
    xsl_ds = [nc.dram_tensor("xsl%d" % j, [SLOTS, D], BF16) for j in range(2)]
    ysl_ds = [nc.dram_tensor("ysl%d" % j, [SLOTS, D], BF16) for j in range(2)]
    w1c_d = nc.dram_tensor("w1c", [NE, 128, NDC, DI], BF16)
    w2c_d = nc.dram_tensor("w2c", [NE, 128, NFC, D], BF16)

    cnt = {}

    def inc(ins, sem, n=1):
        ins.then_inc(sem, n)
        k = id(sem)
        cnt[k] = cnt.get(k, 0) + n
        return cnt[k]

    def val(sem):
        return cnt.get(id(sem), 0)

    with ExitStack() as ctx:
        E = ctx.enter_context
        xnhi_fm = E(nc.sbuf_tensor("xnhi_fm", [128, 3, NDC, 128], BF16, is_transpose_dest=True))
        xnlo_fm = E(nc.sbuf_tensor("xnlo_fm", [128, NDC, 128], BF16, is_transpose_dest=True))
        w1bf = E(nc.sbuf_tensor("w1bf", [128, NDC, DI], BF16))
        w2bf = E(nc.sbuf_tensor("w2bf", [128, NFC, D], BF16))
        h_sb = E(nc.sbuf_tensor("h_sb", [128, NFC, CAP], BF16))
        sx_fm = E(nc.sbuf_tensor("sx_fm", [128, 2, NDC, CAP], BF16))
        xg_sb = E(nc.sbuf_tensor("xg_sb", [128, 2, D], BF16))
        identb = E(nc.sbuf_tensor("identb", [128, 128], BF16))
        ysl_sb = E(nc.sbuf_tensor("ysl_sb", [128, 2, D], BF16))
        y1g = E(nc.sbuf_tensor("y1g", [128, D], BF16))
        y2g = E(nc.sbuf_tensor("y2g", [128, D], BF16))
        zz_sb = E(nc.sbuf_tensor("zz_sb", [128, D], BF16))
        ind32 = E(nc.sbuf_tensor("ind32", [128, 32], F32))
        indT = E(nc.sbuf_tensor("indT", [32, TT], F32))
        zeroT = E(nc.sbuf_tensor("zeroT", [32, TT], F32))
        cnt_fm = E(nc.sbuf_tensor("cnt_fm", [32, TT], F32))
        stm = E(nc.sbuf_tensor("stm", [128, 32], F32))
        eq4 = E(nc.sbuf_tensor("eq4", [128, NCH, NE], F32))
        ind4 = E(nc.sbuf_tensor("ind4", [128, NCH, NE], F32))
        tsa = E(nc.sbuf_tensor("tsa", [128, NE], F32))
        tsb = E(nc.sbuf_tensor("tsb", [128, NE], F32))
        s12f = E(nc.sbuf_tensor("s12f", [128, 2], F32))
        i12 = E(nc.sbuf_tensor("i12", [128, 2, NCH, 2], mybir.dt.int32))
        c12 = E(nc.sbuf_tensor("c12", [128, 2, NCH, 2], F32))
        basec_s = E(nc.sbuf_tensor("basec_s", [32, 1], F32))
        carryI = E(nc.sbuf_tensor("carryI", [32, 1], F32))
        capc_s = E(nc.sbuf_tensor("capc_s", [32, 1], F32))
        oacc = E(nc.sbuf_tensor("oacc", [128, 2, NCH, D], F32))
        xst = E(nc.sbuf_tensor("xst", [128, 3, D], F32))
        outst = E(nc.sbuf_tensor("outst", [128, 2, D], F32))
        xn_tm = E(nc.sbuf_tensor("xn_tm", [128, D], F32))
        xc_s = E(nc.sbuf_tensor("xc_s", [128, 3, D], F32))
        sq_s = E(nc.sbuf_tensor("sq_s", [128, D], F32))
        xnhi_tm = E(nc.sbuf_tensor("xnhi_tm", [128, D], BF16))
        xnlo_tm = E(nc.sbuf_tensor("xnlo_tm", [128, D], BF16))
        auf_a = E(nc.sbuf_tensor("auf_a", [32, TT], F32))
        auf_u = E(nc.sbuf_tensor("auf_u", [32, TT], F32))
        st_f = E(nc.sbuf_tensor("st_f", [NS, TT], F32))
        st_bf = E(nc.sbuf_tensor("st_bf", [NS, 2, TT], BF16))
        carry_s = E(nc.sbuf_tensor("carry_s", [NS, 1], F32))
        au_tm = E(nc.sbuf_tensor("au_tm", [128, 128], F32))
        apre = E(nc.sbuf_tensor("apre", [128, 2, 32], F32))
        term = E(nc.sbuf_tensor("term", [128, 32], F32))
        sn1 = E(nc.sbuf_tensor("sn1", [128, NS], F32))
        sn2 = E(nc.sbuf_tensor("sn2", [128, NS], F32))
        lg_sb = E(nc.sbuf_tensor("lg_sb", [128, 2, NE], F32))
        eq_sb = E(nc.sbuf_tensor("eq_sb", [128, NE], F32))
        msk_sb = E(nc.sbuf_tensor("msk_sb", [128, NE], F32))
        ind_sb = E(nc.sbuf_tensor("ind_sb", [128, NE], F32))
        e_sb = E(nc.sbuf_tensor("e_sb", [128, NE], F32))
        c_sb = E(nc.sbuf_tensor("c_sb", [128, 2, NCH, NE], F32))
        sum_s = E(nc.sbuf_tensor("sum_s", [128, 1], F32))
        mu_s = E(nc.sbuf_tensor("mu_s", [128, 1], F32))
        tq_s = E(nc.sbuf_tensor("tq_s", [128, 1], F32))
        q_s = E(nc.sbuf_tensor("q_s", [128, 1], F32))
        vs_s = E(nc.sbuf_tensor("vs_s", [128, 3], F32))
        sxc_s = E(nc.sbuf_tensor("sxc_s", [128, 4], F32))
        w_s = E(nc.sbuf_tensor("w_s", [128, 3], F32))
        rs_s = E(nc.sbuf_tensor("rs_s", [128, 3], F32))
        u1_s = E(nc.sbuf_tensor("u1_s", [128, 1], F32))
        hu1_s = E(nc.sbuf_tensor("hu1_s", [128, 1], F32))
        u2_s = E(nc.sbuf_tensor("u2_s", [128, 1], F32))
        mu2_s = E(nc.sbuf_tensor("mu2_s", [128, 1], F32))
        rs2_c = E(nc.sbuf_tensor("rs2_c", [128, NCH], F32))
        rm2_c = E(nc.sbuf_tensor("rm2_c", [128, NCH], F32))
        m1_s = E(nc.sbuf_tensor("m1_s", [128, 2], F32))
        nm1_s = E(nc.sbuf_tensor("nm1_s", [128, 2], F32))
        m2_s = E(nc.sbuf_tensor("m2_s", [128, 2], F32))
        e2_s = E(nc.sbuf_tensor("e2_s", [128, 1], F32))
        den_s = E(nc.sbuf_tensor("den_s", [128, 1], F32))
        rden_s = E(nc.sbuf_tensor("rden_s", [128, 1], F32))
        epsk_s = E(nc.sbuf_tensor("epsk_s", [128, 1], F32))
        wgbf = E(nc.sbuf_tensor("wgbf", [128, NDC, NE], BF16))

        abhi = E(nc.sbuf_tensor("abhi", [128, NEC, 32], BF16))
        ablo = E(nc.sbuf_tensor("ablo", [128, NEC, 32], BF16))

        chi = E(nc.sbuf_tensor("chi", [128, NEC, NS], BF16))
        clo = E(nc.sbuf_tensor("clo", [128, NEC, NS], BF16))

        binhi = E(nc.sbuf_tensor("binhi", [128, NEC], BF16))
        binlo = E(nc.sbuf_tensor("binlo", [128, NEC], BF16))
        nw_f = E(nc.sbuf_tensor("nw_f", [128, NDC], F32))
        ab_f = xst[:, 0, 0:512].rearrange("p (a b) -> p a b", b=32)
        c_f = xst[:, 0, 512:768].rearrange("p (a b) -> p a b", b=16)
        bin_f = xst[:, 1, 0:16]
        nb_f = xst[:, 1, 16:24]

        nbhi = E(nc.sbuf_tensor("nbhi", [128, NDC], BF16))
        nblo = E(nc.sbuf_tensor("nblo", [128, NDC], BF16))
        bout_rep = E(nc.sbuf_tensor("bout_rep", [128, D], BF16))
        onesr_bf = E(nc.sbuf_tensor("onesr_bf", [1, 128], BF16))
        onec_bf = E(nc.sbuf_tensor("onec_bf", [128, 1], BF16))
        vwhi = E(nc.sbuf_tensor("vwhi", [1, 32], BF16))
        vwlo = E(nc.sbuf_tensor("vwlo", [1, 32], BF16))
        cbhi = E(nc.sbuf_tensor("cbhi", [1, 32], BF16))
        cblo = E(nc.sbuf_tensor("cblo", [1, 32], BF16))
        cab_f = E(nc.sbuf_tensor("cab_f", [128, NDC, 32], F32))
        cabhi = E(nc.sbuf_tensor("cabhi", [128, NDC, 32], BF16))
        cablo = E(nc.sbuf_tensor("cablo", [128, NDC, 32], BF16))
        cabw_f = E(nc.sbuf_tensor("cabw_f", [128, NDC, 32], F32))
        cabwhi = E(nc.sbuf_tensor("cabwhi", [128, NDC, 32], BF16))
        cabwlo = E(nc.sbuf_tensor("cabwlo", [128, NDC, 32], BF16))
        vwrow = E(nc.sbuf_tensor("vwrow", [1, 32], F32))
        cbrow = E(nc.sbuf_tensor("cbrow", [1, 32], F32))
        vwrep = E(nc.sbuf_tensor("vwrep", [128, 32], F32))
        cbrep = E(nc.sbuf_tensor("cbrep", [128, 32], F32))
        mtbf = E(nc.sbuf_tensor("mtbf", [NS, D], BF16))
        ps_h = E(nc.psum_tensor("ps_h", [128, 1024], F32))
        ps_ya = E(nc.psum_tensor("ps_ya", [128, 1024], F32))
        ps_yb = E(nc.psum_tensor("ps_yb", [128, 1024], F32))
        ps_misc = E(nc.psum_tensor("ps_misc", [128, 512], F32))
        ps_trx = E(nc.psum_tensor("ps_trx", [128, 1024], BF16))
        ps_ap = ps_misc[:, 256:288]
        ps_lg = ps_misc[:, 128:136]
        s_c = E(nc.semaphore("s_c"))
        s_xs = [E(nc.semaphore("s_x%d" % j)) for j in range(4)]
        s_tps = [E(nc.semaphore("s_tp%d" % j)) for j in range(8)]
        s_outs = [E(nc.semaphore("s_out%d" % j)) for j in range(2)]
        s_wg = E(nc.semaphore("s_wg"))
        s_w1 = E(nc.semaphore("s_w1"))
        s_w2 = E(nc.semaphore("s_w2"))
        s_t = E(nc.semaphore("s_t"))
        s_v = E(nc.semaphore("s_v"))
        s_sc = E(nc.semaphore("s_sc"))
        s_ye = E(nc.semaphore("s_ye"))
        s_re = E(nc.semaphore("s_re"))
        s_zz = [E(nc.semaphore("s_zz%d" % j)) for j in range(2)]
        s_scat = [E(nc.semaphore("s_scat%d" % j)) for j in range(2)]
        s_xg = E(nc.semaphore("s_xg"))
        s_sxc = E(nc.semaphore("s_sxc"))
        s_wc1 = E(nc.semaphore("s_wc1"))
        s_wc2 = E(nc.semaphore("s_wc2"))
        s_ysl2 = [E(nc.semaphore("s_ysl%d" % j)) for j in range(4)]
        s_yg = [E(nc.semaphore("s_yg%d" % j)) for j in range(2)]
        s_xb = [E(nc.semaphore("s_xb%d" % j)) for j in range(2)]
        s_xnd = E(nc.semaphore("s_xnd"))

        M = {}
        LATE = {"lgcp": 0, "m6": 0, "apmm": 0}
        blk = E(nc.Block())

        @blk.sync
        def _(sy):
            pass
            inc(sy.dma_start(out=nw_f[:], in_=nw_in.ap()), s_c, 16)
            inc(sy.dma_start(out=nb_f, in_=nb_in.ap()), s_c, 16)
            inc(sy.dma_start(out=bin_f, in_=bin_in.ap()), s_c, 16)
            inc(sy.dma_start(out=ab_f, in_=ab_in.ap()), s_c, 16)
            inc(sy.dma_start(out=c_f, in_=c_in.ap()), s_c, 16)
            inc(sy.dma_start(out=basec_s[:], in_=basec_in.ap()), s_c, 16)
            inc(sy.dma_start(out=capc_s[:], in_=capc_in.ap()), s_c, 16)
            M["consts"] = val(s_c)

        @blk.gpsimd
        def _(g):
            inc(g.dma_start(out=bout_rep[:],
                            in_=bout_in.ap()[0:1, :].partition_broadcast(128)), s_wg, 16)
            inc(g.dma_start(out=wgbf[:], in_=wg_in.ap()), s_wg, 16)
            inc(g.dma_start(out=identb[:], in_=identf_in.ap()), s_wg, 16)
            for e_ in range(NE):
                w1r = w1_in.ap()[e_].rearrange("(dc p) f -> p dc f", p=128)
                inc(g.dma_start(out=w1c_d.ap()[e_][:, 0:4, :],
                                in_=w1r[:, 0:4, :]), s_wc1, 16)
                inc(g.dma_start(out=w1c_d.ap()[e_][:, 4:8, :],
                                in_=w1r[:, 4:8, :]), s_wc1, 16)
                w2r = w2_in.ap()[e_].rearrange("(fc p) d -> p fc d", p=128)
                for q in range(4):
                    inc(g.dma_start(out=w2c_d.ap()[e_][:, 4 * q:4 * q + 4, :],
                                    in_=w2r[:, 4 * q:4 * q + 4, :]), s_wc2, 16)
            g.wait_ge(s_wc1, 32 * NE)
            inc(g.dma_start(out=w1bf[:, 0:4, :], in_=w1c_d.ap()[0][:, 0:4, :]),
                s_w1, 16)
            inc(g.dma_start(out=w1bf[:, 4:8, :], in_=w1c_d.ap()[0][:, 4:8, :]),
                s_w1, 16)
            g.wait_ge(s_wc2, 64 * NE)
            for q in range(4):
                inc(g.dma_start(out=w2bf[:, 4 * q:4 * q + 4, :],
                                in_=w2c_d.ap()[0][:, 4 * q:4 * q + 4, :]),
                    s_w2, 16)

        @blk.vector
        def _(v):
            v.memset(onec_bf[:], 1.0)
            v.memset(onesr_bf[:], 1.0)
            v.memset(au_tm[:, 16:32], 0.0)
            v.memset(au_tm[:, 48:64], 0.0)
            v.memset(epsk_s[:], EPSK)
            v.memset(zz_sb[:], 0.0)
            v.memset(zeroT[:], 0.0)
            v.memset(ind32[:, 8:32], 0.0)

        @blk.scalar
        def _(s):
            s.wait_ge(s_c, M["consts"])
            s.copy(abhi[:], ab_f)
            s.copy(chi[:], c_f)
            s.copy(binhi[:], bin_f)
            inc(s.copy(nbhi[:], nb_f), s_sc)
            M["hi_consts"] = val(s_sc)

        @blk.vector
        def _(v):
            v.wait_ge(s_sc, M["hi_consts"])
            v.tensor_sub(ablo[:], ab_f, abhi[:])
            v.tensor_sub(clo[:], c_f, chi[:])
            v.tensor_sub(binlo[:], bin_f, binhi[:])
            inc(v.tensor_sub(nblo[:], nb_f, nbhi[:]), s_v)
            M["lo_consts"] = val(s_v)

        caps = [ps_h[:, 0:32], ps_h[:, 512:544],
                ps_ya[:, 0:32], ps_ya[:, 512:544],
                ps_yb[:, 0:32], ps_yb[:, 512:544],
                ps_misc[:, 0:32], ps_misc[:, 128:160]]
        for ec in range(NEC):
            @blk.sync
            def _(sy, ec=ec):
                if ec >= 2:
                    sy.wait_ge(s_sc, M[("whi", ec - 2)])
                    sy.wait_ge(s_v, M[("wlo", ec - 2)])
                else:
                    sy.wait_ge(s_sc, M["hi_consts"])
                    sy.wait_ge(s_v, M["lo_consts"])
                a = sy.dma_start(out=xst[:, ec % 2, :],
                                 in_=win_in.ap()[ec * 128:(ec + 1) * 128, :])
                M[("wld", ec)] = inc(a, s_xs[ec % 2], 16)

            @blk.scalar
            def _(s, ec=ec):
                s.wait_ge(s_xs[ec % 2], M[("wld", ec)])
                if ec >= 1:
                    s.wait_ge(s_t, M[("cabmm", ec - 1)])
                M[("whi", ec)] = inc(s.copy(xnhi_tm[:], xst[:, ec % 2, :]), s_sc)

            @blk.vector
            def _(v, ec=ec):
                v.wait_ge(s_sc, M[("whi", ec)])
                M[("wlo", ec)] = inc(
                    v.tensor_sub(xnlo_tm[:], xst[:, ec % 2, :], xnhi_tm[:]), s_v)

            @blk.tensor
            def _(t, ec=ec):
                t.wait_ge(s_v, M[("wlo", ec)])
                if ec == 0:
                    t.wait_ge(s_sc, M["hi_consts"])
                    t.wait_ge(s_v, M["lo_consts"])
                last = None
                for dc in range(NDC):
                    for p, (lh, rh) in enumerate([
                            (xnhi_tm, abhi), (xnlo_tm, abhi), (xnhi_tm, ablo)]):
                        last = t.matmul(caps[dc], lh[:, dc * 128:(dc + 1) * 128],
                                        rh[:, ec, :],
                                        start=(ec == 0 and p == 0 and dc != 7),
                                        stop=(ec == NEC - 1 and p == 2
                                              and dc != 6))
                M[("cabmm", ec)] = inc(last, s_t)

        @blk.vector
        def _(v):
            v.wait_ge(s_t, M[("cabmm", NEC - 1)])
            for dc in range(NDC):
                v.tensor_copy(cab_f[:, dc, :], caps[dc])
            v.drain()
            last = None
            for dc in range(NDC):
                last = v.tensor_scalar(cabw_f[:, dc, :], cab_f[:, dc, :],
                                       nw_f[:, dc:dc + 1], None, ALU.mult)
            M["cab_f"] = inc(last, s_v)

        @blk.scalar
        def _(s):
            s.wait_ge(s_v, M["cab_f"])
            s.copy(cabhi[:], cab_f[:])
            M["cabhi"] = inc(s.copy(cabwhi[:], cabw_f[:]), s_sc)

        @blk.vector
        def _(v):
            v.wait_ge(s_sc, M["cabhi"])
            v.tensor_sub(cablo[:], cab_f[:], cabhi[:])
            M["cablo"] = inc(v.tensor_sub(cabwlo[:], cabw_f[:], cabwhi[:]), s_v)

        @blk.tensor
        def _(t):
            t.wait_ge(s_v, M["cablo"])
            for dc in range(NDC):
                for rh in (cabwhi, cabwlo):
                    t.matmul(ps_ap[0:1, :], onec_bf[:], rh[:, dc, :],
                             start=(dc == 0 and rh is cabwhi), stop=False)
            last2 = None
            for dc in range(NDC):
                for lh, rh in [(nbhi, cabhi), (nblo, cabhi), (nbhi, cablo)]:
                    last2 = t.matmul(ps_misc[0:1, 0:32], lh[:, dc:dc + 1],
                                     rh[:, dc, :], start=False, stop=False)
            for ec in range(NEC):
                for p, (lh, rh) in enumerate([(binhi, abhi), (binlo, abhi),
                                              (binhi, ablo)]):
                    last2 = t.matmul(ps_misc[0:1, 0:32], lh[:, ec:ec + 1],
                                     rh[:, ec, :], start=False,
                                     stop=(ec == NEC - 1 and p == 2))
            M["vwcb"] = inc(last2, s_t)

        @blk.vector
        def _(v):
            v.wait_ge(s_t, M["vwcb"])
            v.tensor_copy(vwrow[:], ps_ap[0:1, :])
            M["rows"] = inc(v.tensor_copy(cbrow[:], ps_misc[0:1, 0:32]), s_v)

        @blk.scalar
        def _(s):
            s.wait_ge(s_v, M["rows"])
            s.copy(vwhi[:], vwrow[:])
            M["rephi"] = inc(s.copy(cbhi[:], cbrow[:]), s_sc)

        @blk.vector
        def _(v):
            v.wait_ge(s_sc, M["rephi"])
            v.tensor_sub(vwlo[:], vwrow[:], vwhi[:])
            M["replo"] = inc(v.tensor_sub(cblo[:], cbrow[:], cbhi[:]), s_v)

        @blk.tensor
        def _(t):
            t.wait_ge(s_v, M["replo"])
            t.matmul(ps_ap[:], onesr_bf[:], vwhi[:], start=True, stop=False)
            M["rep1"] = inc(t.matmul(ps_ap[:], onesr_bf[:], vwlo[:],
                                     start=False, stop=True), s_t)

        @blk.vector
        def _(v):
            v.wait_ge(s_t, M["rep1"])
            M["vwrep"] = inc(v.tensor_copy(vwrep[:], ps_ap[:]), s_v)

        @blk.tensor
        def _(t):
            t.wait_ge(s_v, M["vwrep"])
            t.matmul(ps_ap[:], onesr_bf[:], cbhi[:], start=True, stop=False)
            M["rep2"] = inc(t.matmul(ps_ap[:], onesr_bf[:], cblo[:],
                                     start=False, stop=True), s_t)

        @blk.vector
        def _(v):
            v.wait_ge(s_t, M["rep2"])
            M["cbrep"] = inc(v.tensor_copy(cbrep[:], ps_ap[:]), s_v)

        for ec in range(NEC):
            @blk.sync
            def _(sy, ec=ec):
                sy.wait_ge(s_sc, M[("mwhi", ec - 2)] if ec >= 2 else M[("whi", NEC - 1)])
                sy.wait_ge(s_v, M[("mwlo", ec - 2)] if ec >= 2 else M[("wlo", NEC - 1)])
                a = sy.dma_start(out=xst[:, ec % 2, :],
                                 in_=woutt_in.ap()[ec * 128:(ec + 1) * 128, :])
                M[("mwld", ec)] = inc(a, s_xs[ec % 2], 16)

            @blk.scalar
            def _(s, ec=ec):
                s.wait_ge(s_xs[ec % 2], M[("mwld", ec)])
                s.wait_ge(s_t, M[("mtmm", ec - 1)] if ec >= 1 else M[("cabmm", NEC - 1)])
                M[("mwhi", ec)] = inc(s.copy(xnhi_tm[:], xst[:, ec % 2, :]), s_sc)

            @blk.vector
            def _(v, ec=ec):
                v.wait_ge(s_sc, M[("mwhi", ec)])
                M[("mwlo", ec)] = inc(
                    v.tensor_sub(xnlo_tm[:], xst[:, ec % 2, :], xnhi_tm[:]), s_v)

            @blk.tensor
            def _(t, ec=ec):
                t.wait_ge(s_v, M[("mwlo", ec)])
                if ec == 0:
                    t.wait_ge(s_v, M["cbrep"])
                last = None
                for hf in range(2):
                    sl = slice(hf * 512, hf * 512 + 512)
                    for p, (lh, rh) in enumerate([(chi, xnhi_tm), (clo, xnhi_tm),
                                                  (chi, xnlo_tm)]):
                        last = t.matmul(ps_yb[0:NS, sl], lh[:, ec, :], rh[:, sl],
                                        start=(ec == 0 and p == 0),
                                        stop=(ec == NEC - 1 and p == 2))
                M[("mtmm", ec)] = inc(last, s_t)

        @blk.vector
        def _(v):
            v.wait_ge(s_t, M[("mtmm", NEC - 1)])
            M["mtbf"] = inc(v.tensor_copy(mtbf[:], ps_yb[0:NS, :]), s_v)

        ge_w1free = {}
        ge_w2free = {}
        xfree = {}
        stores = [0]

        def s1VA(ti, c):
            g = ti * NCH + c
            gp = g % 2
            g3 = g % 3
            g4 = g % 4
            xs = g % 3
            t0 = g * 128

            @blk.sync
            def _(sy):
                if g >= 3:
                    sy.wait_ge(s_v, xfree[g - 3])
                else:
                    sy.wait_ge(s_sc, M[("mwhi", NEC - 1)])
                    sy.wait_ge(s_v, M[("mwlo", NEC - 1)])
                a = sy.dma_start(out=xst[:, xs, :], in_=x_in.ap()[t0:t0 + 128, :])
                M[("x", g)] = inc(a, s_xs[xs], 16)

            @blk.vector
            def _(v):
                v.wait_ge(s_xs[xs], M[("x", g)])
                xt = xst[:, xs, :]
                v.reduce_sum(sum_s[:], xt, AX.X)
                v.drain()
                v.tensor_scalar_mul(mu_s[:], sum_s[:], R1024)
                v.drain()
                v.tensor_scalar_sub(xc_s[:, g3, :], xt, mu_s[:])
                v.drain()
                v.tensor_mul(sq_s[:], xc_s[:, g3, :], xc_s[:, g3, :])
                v.reduce_sum(sxc_s[:, g4:g4 + 1], xc_s[:, g3, :], AX.X)
                v.drain()
                v.reduce_sum(vs_s[:, g3:g3 + 1], sq_s[:], AX.X)
                v.tensor_mul(tq_s[:], sxc_s[:, g4:g4 + 1], sxc_s[:, g4:g4 + 1])
                v.drain()
                v.scalar_tensor_tensor(q_s[:], tq_s[:], -R1024, vs_s[:, g3:g3 + 1],
                                       ALU.mult, ALU.add)
                v.drain()
                v.scalar_tensor_tensor(w_s[:, g3:g3 + 1], vs_s[:, g3:g3 + 1], EPS,
                                       q_s[:], ALU.mult, ALU.add)
                v.drain()
                M[("m1", g)] = inc(
                    v.tensor_scalar_add(w_s[:, g3:g3 + 1], w_s[:, g3:g3 + 1], K2),
                    s_v)

        def s1SB(ti, c):
            g = ti * NCH + c
            g3 = g % 3

            @blk.scalar
            def _(s):
                s.wait_ge(s_v, M[("m1", g)])
                s.activation(u1_s[:], vs_s[:, g3:g3 + 1], AF.Ln, bias=epsk_s[:])
                s.drain()
                s.mul(hu1_s[:], u1_s[:], 0.5)
                s.activation(rs_s[:, g3:g3 + 1], u1_s[:], AF.Exp, scale=-0.5)
                s.drain()
                s.mul(rs_s[:, g3:g3 + 1], rs_s[:, g3:g3 + 1], 32.0)
                s.drain()
                s.activation(u2_s[:], w_s[:, g3:g3 + 1], AF.Ln)
                s.drain()
                M[("m2", g)] = inc(
                    s.activation(rs2_c[:, c:c + 1], u2_s[:], AF.Exp,
                                 scale=-0.5, bias=hu1_s[:]), s_sc)

        def s1VC(ti, c):
            g = ti * NCH + c
            gp = g % 2
            g3 = g % 3
            g4 = g % 4
            xs = g % 3
            tb = ti % 2

            @blk.vector
            def _(v):
                v.wait_ge(s_sc, M[("m2", g)])
                if g >= 1:
                    v.wait_ge(s_tps[(g - 1) % 8], M[("tpd", g - 1)])
                    v.wait_ge(s_xnd, M[("xnd", g - 1)])
                else:
                    v.wait_ge(s_t, M[("mtmm", NEC - 1)])
                    v.wait_ge(s_c, M["consts"])
                    v.wait_ge(s_wg, 48)
                v.tensor_scalar_mul(xn_tm[:], xc_s[:, g3, :], rs_s[:, g3:g3 + 1])
                v.drain()
                v.tensor_copy(xnhi_tm[:], xn_tm[:])
                v.drain()
                v.tensor_sub(xnlo_tm[:], xn_tm[:], xnhi_tm[:])
                v.tensor_scalar_mul(mu2_s[:], sxc_s[:, g4:g4 + 1], R1024)
                v.drain()
                v.tensor_mul(mu2_s[:], mu2_s[:], rs_s[:, g3:g3 + 1])
                v.drain()
                v.tensor_mul(rm2_c[:, c:c + 1], mu2_s[:], rs2_c[:, c:c + 1])
                v.tensor_add(oacc[:, tb, c, :], xst[:, xs, :], xn_tm[:])
                v.drain()
                ii = v.tensor_add(oacc[:, tb, c, :], oacc[:, tb, c, :], bout_rep[:])
                xfree[g] = inc(ii, s_v)

            @blk.sync
            def _(sy):
                sy.wait_ge(s_v, xfree[g])
                if g >= 1:
                    sy.wait_ge(s_t, LATE["apmm"])
                stp = s_tps[g % 8]
                for dc in range(NDC):
                    inc(sy.dma_start_transpose(
                        xnhi_fm[:, g % 3, dc, :],
                        xnhi_tm[:, dc * 128:(dc + 1) * 128]), stp, 16)
                for dc in range(NDC):
                    inc(sy.dma_start_transpose(
                        xnlo_fm[:, dc, :],
                        xnlo_tm[:, dc * 128:(dc + 1) * 128]), stp, 16)
                M[("tpd", g)] = val(stp)
                M[("xnd", g)] = inc(
                    sy.dma_start(out=xnd_d.ap()[g * 128:(g + 1) * 128, :],
                                 in_=xnhi_tm[:]), s_xnd, 16)

        def s1B1(ti, c):
            g = ti * NCH + c
            tb = ti % 2
            pc = c % 2

            @blk.tensor
            def _(t):
                if g == 0:
                    t.wait_ge(s_wg, 48)
                    t.wait_ge(s_sc, M["cabhi"])
                    t.wait_ge(s_v, M["cablo"])
                    t.wait_ge(s_v, M["mtbf"])
                    t.wait_ge(s_v, M["cbrep"])
                if LATE["m6"]:
                    t.wait_ge(s_v, LATE["m6"])
                t.wait_ge(s_tps[g % 8], M[("tpd", g)])
                last = None
                for dc in range(NDC):
                    last = t.matmul(ps_lg, xnhi_fm[:, g % 3, dc, :],
                                    wgbf[:, dc, :],
                                    start=(dc == 0), stop=(dc == NDC - 1))
                M[("lgmm", g)] = inc(last, s_t)

            @blk.vector
            def _(v):
                if g >= 2:
                    v.wait_ge(s_sc, M[("m10", g - 2)])
                v.wait_ge(s_t, M[("lgmm", g)])
                M[("lgcp", g)] = inc(v.tensor_copy(lg_sb[:, pc, :], ps_lg), s_v)
                LATE["lgcp"] = M[("lgcp", g)]

            @blk.tensor
            def _(t):
                t.wait_ge(s_v, M[("lgcp", g)])
                last = None
                for p in range(3):
                    for dc in range(NDC):
                        lh = (xnhi_fm[:, g % 3, dc, :]
                              if p != 1 else xnlo_fm[:, dc, :])
                        rh = cabwhi[:, dc, :] if p != 2 else cabwlo[:, dc, :]
                        last = t.matmul(ps_ap[:], lh, rh,
                                        start=(p == 0 and dc == 0),
                                        stop=(p == 2 and dc == NDC - 1))
                M[("apmm", g)] = inc(last, s_t)
                LATE["apmm"] = M[("apmm", g)]

            @blk.vector
            def _(v):
                v.wait_ge(s_t, M[("apmm", g)])
                v.scalar_tensor_tensor(term[:], vwrep[:], rm2_c[:, c:c + 1],
                                       cbrep[:], ALU.mult, ALU.subtract)
                v.drain()
                v.scalar_tensor_tensor(apre[:, pc, :], ps_ap[:],
                                        rs2_c[:, c:c + 1], term[:],
                                        ALU.mult, ALU.subtract)
                v.drain()
                M[("m6", g)] = inc(
                    v.tensor_scalar_max(apre[:, pc, 0:NS], apre[:, pc, 0:NS],
                                        -30.0), s_v)
                LATE["m6"] = M[("m6", g)]
                v.reduce_max(m1_s[:, pc:pc + 1], lg_sb[:, pc, :], AX.X)
                v.reduce_max(nm1_s[:, pc:pc + 1], lg_sb[:, pc, :], AX.X, negate=True)
                v.drain()
                v.tensor_scalar(eq4[:, c, :], lg_sb[:, pc, :], m1_s[:, pc:pc + 1],
                                None, ALU.is_equal)
                v.drain()
                v.scalar_tensor_tensor(msk_sb[:], eq4[:, c, :], -1e30, lg_sb[:, pc, :],
                                       ALU.mult, ALU.add)
                v.drain()
                M[("m9", g)] = inc(
                    v.reduce_max(m2_s[:, pc:pc + 1], msk_sb[:], AX.X), s_v)

        def s1B2(ti, c):
            g = ti * NCH + c
            tb = ti % 2
            pc = c % 2

            @blk.scalar
            def _(s):
                if g >= 1:
                    s.wait_ge(s_v, M[("router", g - 1)])
                s.wait_ge(s_v, M[("m6", g)])
                M[("m7", g)] = inc(
                    s.activation(sn1[:], apre[:, pc, 0:NS], AF.Exp, scale=-1.0),
                    s_sc)
                s.wait_ge(s_v, M[("m9", g)])
                s.activation(e_sb[:], lg_sb[:, pc, :], AF.Exp,
                             bias=nm1_s[:, pc:pc + 1])
                M[("m10", g)] = inc(
                    s.activation(e2_s[:], m2_s[:, pc:pc + 1], AF.Exp,
                                 bias=nm1_s[:, pc:pc + 1]), s_sc)

            @blk.vector
            def _(v):
                v.wait_ge(s_sc, M[("m7", g)])
                v.tensor_copy(au_tm[:, 32:48], apre[:, pc, 16:32])
                v.tensor_scalar_add(sn2[:], sn1[:], 1.0)
                v.drain()
                v.reciprocal(au_tm[:, 0:16], sn2[:])
                v.drain()
                for rb in range(4):
                    for dst, cbk in ((auf_a, 0), (auf_u, 1)):
                        v.transpose(
                            dst[0:32, c * 128 + 32 * rb:c * 128 + 32 * rb + 32],
                            au_tm[32 * rb:32 * rb + 32, 32 * cbk:32 * cbk + 32])
                v.drain()
                v.wait_ge(s_sc, M[("m10", g)])
                v.tensor_scalar_add(den_s[:], e2_s[:], 1.0)
                v.drain()
                v.reciprocal(rden_s[:], den_s[:])
                v.tensor_scalar(ind4[:, c, :], lg_sb[:, pc, :], m2_s[:, pc:pc + 1],
                                None, ALU.is_ge)
                v.drain()
                v.tensor_copy(ind32[:, 0:8], ind4[:, c, :])
                v.drain()
                for rb in range(4):
                    v.transpose(indT[0:32, c * 128 + 32 * rb:c * 128 + 32 * rb + 32],
                                ind32[32 * rb:32 * rb + 32, 0:32])
                v.drain()
                M[("router", g)] = inc(
                    v.scalar_tensor_tensor(c_sb[:, tb, c, :], ind4[:, c, :], rden_s[:],
                                           e_sb[:], ALU.mult, ALU.mult), s_v)

        def scan_emit(ti):
            tb = ti % 2

            @blk.vector
            def _(v):
                if ti % 4 == 0:
                    v.memset(carry_s[:], 0.0)
                v.drain()
                v.tensor_tensor_scan(st_f[:], auf_a[0:NS, :], auf_u[0:NS, :],
                                     carry_s[:], ALU.mult, ALU.add)
                v.drain()
                v.tensor_copy(carry_s[:], st_f[:, TT - 1:TT])
                M[("scan", ti)] = inc(v.tensor_copy(st_bf[:, tb, :], st_f[:]), s_v)

        def sx_prefetch(ti, e):
            if e >= NE:
                return
            ge = ti * NE + e
            tp = ti % 2

            @blk.gpsimd
            def _(g):
                g.wait_ge(s_scat[tp], M[("scat", ti)])
                if ge >= 1:
                    g.wait_ge(s_t, M[("trp2", ge - 1)])
                for cc in range(CAP // 128):
                    r0 = e * CAP + cc * 128
                    inc(g.dma_start(out=xg_sb[:, cc, :],
                                    in_=xsl_ds[tp].ap()[r0:r0 + 128, :]),
                        s_xg, 16)
                M[("xg", ge)] = val(s_xg)

        def trx_batch(t, ge, cc):
            last = None
            for dc in range(NDC):
                last = t.transpose(ps_trx[:, dc * 128:(dc + 1) * 128],
                                   xg_sb[:, cc, dc * 128:(dc + 1) * 128],
                                   identb[:])
            M[("trp%d" % (cc + 1), ge)] = inc(last, s_t)

        def trx_copies(s, ge, pe):
            s.wait_ge(s_t, M[("trp1", ge)])
            if ge >= 2:
                s.wait_ge(s_t, M[("hps", ge - 2, NFC - 1)])
            for dc in range(NDC):
                inc(s.copy(sx_fm[:, pe, dc, 0:128],
                           ps_trx[:, dc * 128:(dc + 1) * 128]), s_sxc)
            s.wait_ge(s_t, M[("trp2", ge)])
            for dc in range(NDC):
                inc(s.copy(sx_fm[:, pe, dc, 128:256],
                           ps_trx[:, dc * 128:(dc + 1) * 128]), s_sxc)

        def zero_emit(ti):
            tp = ti % 2

            @blk.sync
            def _(sy):
                sy.wait_ge(s_v, M["lo_consts"])
                if ti >= 2:
                    sy.wait_ge(s_t, ge_w1free[(ti - 1) * NE])
                for r in range(SLOTS // 128):
                    inc(sy.dma_start(out=xsl_ds[tp].ap()[r * 128:(r + 1) * 128, :],
                                     in_=zz_sb[:]), s_zz[tp], 16)
                M[("zz", ti)] = val(s_zz[tp])

        def idx_vec(ti, c):
            tp = ti % 2
            tb = ti % 2

            @blk.vector
            def _(v):
                if ti >= 2:
                    v.wait_ge(s_yg[c % 2], M[("gath", ti - 2, c)])
                cs = slice(c * 128, (c + 1) * 128)
                init = zeroT[:, 0:1] if c == 0 else carryI[:]
                v.tensor_tensor_scan(cnt_fm[:, cs], indT[:, cs],
                                     zeroT[:, 0:128], init, ALU.add, ALU.add)
                v.drain()
                if c < NCH - 1:
                    v.tensor_copy(carryI[:],
                                  cnt_fm[:, c * 128 + 127:c * 128 + 128])
                v.tensor_sub(indT[:, cs], cnt_fm[:, cs], indT[:, cs])
                v.drain()
                v.tensor_scalar_add(indT[:, cs], indT[:, cs], basec_s[:])
                v.drain()
                v.tensor_scalar_min(indT[:, cs], indT[:, cs], capc_s[:])
                v.drain()
                for rb in range(4):
                    v.transpose(stm[32 * rb:32 * rb + 32, 0:32],
                                indT[0:32, c * 128 + 32 * rb:c * 128 + 32 * rb + 32])
                v.drain()
                v.tensor_mul(tsa[:], stm[:, 0:8], eq4[:, c, :])
                v.tensor_sub(tsb[:], ind4[:, c, :], eq4[:, c, :])
                v.drain()
                v.reduce_sum(s12f[:, 0:1], tsa[:], AX.X)
                v.drain()
                v.tensor_mul(tsa[:], stm[:, 0:8], tsb[:])
                v.drain()
                v.reduce_sum(s12f[:, 1:2], tsa[:], AX.X)
                v.drain()
                v.tensor_mul(tsa[:], c_sb[:, tb, c, :], eq4[:, c, :])
                v.drain()
                v.tensor_copy(i12[:, tp, c, 0:2], s12f[:])
                v.reduce_sum(c12[:, tp, c, 0:1], tsa[:], AX.X)
                v.drain()
                v.tensor_mul(tsa[:], c_sb[:, tb, c, :], tsb[:])
                v.drain()
                v.reduce_sum(c12[:, tp, c, 1:2], tsa[:], AX.X)
                v.drain()
                if c == NCH - 1:
                    M[("idx", ti)] = inc(v.engine_nop(), s_v)

        def idx_emit(ti):
            tp = ti % 2
            tb = ti % 2

            @blk.gpsimd
            def _(g):
                g.wait_ge(s_v, M[("idx", ti)])
                g.wait_ge(s_zz[tp], M[("zz", ti)])
                g.wait_ge(s_xnd, M[("xnd", ti * NCH + NCH - 1)])
                if ti >= 1:
                    g.wait_ge(s_t, M[("trp2", ti * NE - 1)])
                from concourse.bass_core_rust import IndirectOffsetOnAxis as IOA
                for cp in range(NCH // 2):
                    if cp >= 1:
                        g.wait_ge(s_scat[tp], M[("scat2", ti, cp - 1)])
                    elif ti >= 2:
                        g.wait_ge(s_scat[tp], M[("scat", ti - 2)])
                    wv = None
                    for c in (2 * cp, 2 * cp + 1):
                        t0 = (ti * NCH + c) * 128
                        a = g.dma_start(out=xg_sb[:, c % 2, :],
                                        in_=xnd_d.ap()[t0:t0 + 128, :])
                        wv = inc(a, s_xb[0], 16)
                    g.wait_ge(s_xb[0], wv)
                    for c in (2 * cp, 2 * cp + 1):
                        inc(g.indirect_dma_start(
                            xsl_ds[tp].ap(), IOA(i12[:, tp, c, 0:1], 0),
                            xg_sb[:, c % 2, :], None), s_scat[tp], 16)
                        inc(g.indirect_dma_start(
                            xsl_ds[tp].ap(), IOA(i12[:, tp, c, 1:2], 0),
                            xg_sb[:, c % 2, :], None), s_scat[tp], 16)
                    M[("scat2", ti, cp)] = val(s_scat[tp])
                M[("scat", ti)] = val(s_scat[tp])

        def expert_emit(ti, e):
            ge = ti * NE + e
            tb = ti % 2
            tp = ti % 2
            pe = e % 2
            sx_prefetch(ti, e + 1)

            nxt = ge + 1 if ge + 1 < NTILE * NE else None
            npe = (e + 1) % 2

            @blk.tensor
            def _(t):
                t.wait_ge(s_w1, 32 * (ge + 1))
                t.wait_ge(s_sxc, 16 * ge + 16)
                for fc in range(NFC):
                    pp = (fc % 2) * 512
                    rk = 16 * ge + fc - 1
                    if rk > 0:
                        t.wait_ge(s_re, rk)
                    for dc in range(NDC):
                        last = t.matmul(ps_h[:, pp:pp + CAP],
                                        w1bf[:, dc, fc * 128:(fc + 1) * 128],
                                        sx_fm[:, pe, dc, :],
                                        start=(dc == 0), stop=(dc == NDC - 1))
                    M[("hps", ge, fc)] = inc(last, s_t)
                ge_w1free[ge + 1] = val(s_t)
                t.wait_ge(s_w2, 64 * (ge + 1))
                t.wait_ge(s_re, 16 * ge + 16)
                if e == 0:
                    t.wait_ge(s_v, val(s_v))
                cross = nxt is not None and e == NE - 1
                if nxt is not None and not cross:
                    t.wait_ge(s_xg, M[("xg", nxt)])
                    trx_batch(t, nxt, 0)
                for tc in range(CAP // 128):
                    if 2 * ge + tc - 1 > 0:
                        t.wait_ge(s_ye, 2 * ge + tc - 1)
                    if tc == 1 and nxt is not None and not cross:
                        t.wait_ge(s_sxc, 16 * nxt + 8)
                        trx_batch(t, nxt, 1)
                    ps = ps_ya if tc % 2 == 0 else ps_yb
                    for hf in range(2):
                        sl = slice(hf * 512, hf * 512 + 512)
                        for fc in range(NFC):
                            last = t.matmul(ps[:, sl],
                                            h_sb[:, fc, tc * 128:(tc + 1) * 128],
                                            w2bf[:, fc, sl],
                                            start=(fc == 0), stop=(fc == NFC - 1))
                    M[("yps", ge, tc)] = inc(last, s_t)
                    if tc == 0 and cross:
                        t.wait_ge(s_xg, M[("xg", nxt)])
                        trx_batch(t, nxt, 0)
                if cross:
                    t.wait_ge(s_sxc, 16 * nxt + 8)
                    trx_batch(t, nxt, 1)
                ge_w2free[ge + 1] = val(s_t)

            @blk.scalar
            def _(s):
                for fc in range(NFC):
                    s.wait_ge(s_t, M[("hps", ge, fc)])
                    pp = (fc % 2) * 512
                    inc(s.activation(h_sb[:, fc, :], ps_h[:, pp:pp + CAP],
                                     AF.Relu), s_re)
                if nxt is not None:
                    trx_copies(s, nxt, npe)

            @blk.vector
            def _(v):
                for tc in range(CAP // 128):
                    v.wait_ge(s_t, M[("yps", ge, tc)])
                    if ("ysl", ge - 1, tc) in M:
                        ptp = ((ge - 1) // NE) % 2
                        v.wait_ge(s_ysl2[ptp * 2 + tc], M[("ysl", ge - 1, tc)])
                    ps = ps_ya if tc % 2 == 0 else ps_yb
                    M[("yev", ge, tc)] = inc(
                        v.tensor_copy(ysl_sb[:, tc, :], ps[:]), s_ye)

            @blk.sync
            def _(sy):
                for tc in range(CAP // 128):
                    sy.wait_ge(s_ye, M[("yev", ge, tc)])
                    r0 = e * CAP + tc * 128
                    M[("ysl", ge, tc)] = inc(
                        sy.dma_start(out=ysl_ds[tp].ap()[r0:r0 + 128, :],
                                     in_=ysl_sb[:, tc, :]), s_ysl2[tp * 2 + tc], 16)

            @blk.gpsimd
            def _(g):
                nge = ge + 1
                if nge >= NTILE * NE:
                    return
                nti, ne_ = nge // NE, nge % NE
                g.wait_ge(s_t, ge_w1free[nge])
                if nti == 0:
                    g.wait_ge(s_wc1, 32 * NE)
                w1r = w1c_d.ap()[ne_]
                inc(g.dma_start(out=w1bf[:, 0:4, :], in_=w1r[:, 0:4, :]), s_w1, 16)
                inc(g.dma_start(out=w1bf[:, 4:8, :], in_=w1r[:, 4:8, :]), s_w1, 16)
                g.wait_ge(s_t, ge_w2free[nge])
                if nti == 0:
                    g.wait_ge(s_wc2, 64 * NE)
                w2r = w2c_d.ap()[ne_]
                for q in range(4):
                    inc(g.dma_start(out=w2bf[:, 4 * q:4 * q + 4, :],
                                    in_=w2r[:, 4 * q:4 * q + 4, :]), s_w2, 16)

        def final_emit(ti):
            tb = ti % 2
            tp = ti % 2
            toff = ti * TT
            from concourse.bass_core_rust import IndirectOffsetOnAxis as IOA
            for c in range(NCH):
                t0 = toff + c * 128

                @blk.gpsimd
                def _(g, c=c):
                    if c == 0:
                        for tc in range(2):
                            g.wait_ge(s_ysl2[tp * 2 + tc],
                                      M[("ysl", ti * NE + NE - 1, tc)])
                    if ("fin", ti, c - 2) in M:
                        g.wait_ge(s_v, M[("fin", ti, c - 2)])
                    elif ti >= 1 and ("fin", ti - 1, c + 2) in M:
                        g.wait_ge(s_v, M[("fin", ti - 1, c + 2)])
                    if ("fin", ti, c - 1) in M:
                        g.wait_ge(s_v, M[("fin", ti, c - 1)])
                    elif ti >= 1 and c == 0:
                        g.wait_ge(s_v, M[("fin", ti - 1, NCH - 1)])
                    inc(g.indirect_dma_start(
                        y1g[:], None,
                        ysl_ds[tp].ap(), IOA(i12[:, tp, c, 0:1], 0)), s_yg[c % 2], 16)
                    M[("gath", ti, c)] = inc(g.indirect_dma_start(
                        y2g[:], None,
                        ysl_ds[tp].ap(), IOA(i12[:, tp, c, 1:2], 0)), s_yg[c % 2], 16)

                @blk.tensor
                def _(t, c=c, t0=t0):
                    t.wait_ge(s_v, M[("scan", ti)])
                    if c < 2:
                        t.wait_ge(s_ye, val(s_ye))
                    else:
                        t.wait_ge(s_v, M[("fin", ti, c - 2)])
                    ps = ps_ya if c % 2 == 0 else ps_yb
                    last = None
                    for hf in range(2):
                        sl = slice(hf * 512, hf * 512 + 512)
                        last = t.matmul(ps[:, sl],
                                        st_bf[:, tb, c * 128:(c + 1) * 128],
                                        mtbf[:, sl], start=True, stop=True)
                    M[("ssm", ti, c)] = inc(last, s_t)

                @blk.vector
                def _(v, c=c):
                    v.wait_ge(s_t, M[("ssm", ti, c)])
                    v.wait_ge(s_yg[c % 2], M[("gath", ti, c)])
                    if ("store", c % 2) in M:
                        v.wait_ge(s_outs[c % 2], M[("store", c % 2)])
                    v.drain()
                    ps = ps_ya if c % 2 == 0 else ps_yb
                    v.scalar_tensor_tensor(outst[:, c % 2, :], y1g[:],
                                           c12[:, tp, c, 0:1], oacc[:, tb, c, :],
                                           ALU.mult, ALU.add)
                    v.drain()
                    v.scalar_tensor_tensor(outst[:, c % 2, :], y2g[:],
                                           c12[:, tp, c, 1:2], outst[:, c % 2, :],
                                           ALU.mult, ALU.add)
                    v.drain()
                    M[("fin", ti, c)] = inc(
                        v.tensor_add(outst[:, c % 2, :], outst[:, c % 2, :], ps[:]),
                        s_v)

                @blk.sync
                def _(sy, c=c, t0=t0):
                    sy.wait_ge(s_v, M[("fin", ti, c)])
                    M[("store", c % 2)] = inc(
                        sy.dma_start(out=out_d.ap()[t0:t0 + 128, :],
                                     in_=outst[:, c % 2, :]), s_outs[c % 2], 16)
                stores[0] += 1

        zero_emit(0)
        for c in range(NCH):
            s1VA(0, c)
            s1SB(0, c)
            s1VC(0, c)
            s1B1(0, c)
            s1B2(0, c)
        scan_emit(0)
        for c in range(NCH):
            idx_vec(0, c)
        idx_emit(0)
        sx_prefetch(0, 0)

        @blk.tensor
        def _(t):
            t.wait_ge(s_wg, 48)
            t.wait_ge(s_xg, M[("xg", 0)])
            trx_batch(t, 0, 0)
            t.wait_ge(s_sxc, 8)
            trx_batch(t, 0, 1)

        @blk.scalar
        def _(s):
            trx_copies(s, 0, 0)

        for ti in range(NTILE):
            for e in range(NE):
                expert_emit(ti, e)
                if ti + 1 < NTILE:
                    if e == 0:
                        zero_emit(ti + 1)
                    if 2 <= e <= 5:
                        s1B1(ti + 1, e - 2)
                    if 3 <= e <= 6:
                        s1B2(ti + 1, e - 3)
                        idx_vec(ti + 1, e - 3)
                    if e <= 3:
                        s1VA(ti + 1, e)
                    if 1 <= e <= 4:
                        s1SB(ti + 1, e - 1)
                    if 1 <= e <= 4:
                        s1VC(ti + 1, e - 1)
                    if e == 6:
                        idx_emit(ti + 1)
                        sx_prefetch(ti + 1, 0)
                    if e == 7:
                        scan_emit(ti + 1)
            final_emit(ti)

        @blk.sync
        def _(sy):
            sy.wait_ge(s_outs[0], val(s_outs[0]))
            sy.wait_ge(s_outs[1], val(s_outs[1]))

    return nc


_NC_CACHE = [None]


def kernel(**inputs):
    x = np.ascontiguousarray(np.asarray(inputs["L0_attn_x"], dtype=np.float32))
    W1 = np.ascontiguousarray(np.asarray(inputs["L0_ffn_W1"], dtype=np.float32))
    W2 = np.ascontiguousarray(np.asarray(inputs["L0_ffn_W2"], dtype=np.float32))
    Wg = np.asarray(inputs["L0_ffn_Wg"], dtype=np.float32)
    W_in = np.ascontiguousarray(np.asarray(inputs["L0_attn_W_in"], dtype=np.float32))
    A = np.asarray(inputs["L0_attn_A"], dtype=np.float32)
    B = np.asarray(inputs["L0_attn_B_ssm"], dtype=np.float32)
    C = np.asarray(inputs["L0_attn_C_ssm"], dtype=np.float32)
    W_out = np.asarray(inputs["L0_attn_W_out"], dtype=np.float32)
    b_in = np.asarray(inputs["L0_attn_b_in"], dtype=np.float32)
    b_out = np.asarray(inputs["L0_attn_b_out"], dtype=np.float32)
    norm_w = np.asarray(inputs["L0_attn_norm_w"], dtype=np.float32)
    norm_b = np.asarray(inputs["L0_attn_norm_b"], dtype=np.float32)

    AB = np.concatenate([A, B], axis=1)
    ab_l = np.ascontiguousarray(AB.reshape(16, 128, 32).transpose(1, 0, 2))
    c_l = np.ascontiguousarray(C.reshape(16, 128, 16).transpose(1, 0, 2))
    woutt = np.ascontiguousarray(W_out.T)
    wg_l = np.ascontiguousarray(Wg.reshape(8, 128, 8).transpose(1, 0, 2))
    bin_l = np.ascontiguousarray(b_in.reshape(16, 128).T)
    nw_l = np.ascontiguousarray(norm_w.reshape(8, 128).T)
    nb_l = np.ascontiguousarray(norm_b.reshape(8, 128).T)
    bout_l = np.ascontiguousarray(b_out.reshape(1, 1024))
    basec = np.zeros((32, 1), np.float32)
    capc = np.zeros((32, 1), np.float32)
    basec[0:8, 0] = np.arange(8, dtype=np.float32) * 256.0
    capc[0:8, 0] = basec[0:8, 0] + 255.0
    identf = np.eye(128, dtype=np.float32)

    if _NC_CACHE[0] is None:
        _NC_CACHE[0] = _build()
    nc = _NC_CACHE[0]

    in_maps = []
    for i in range(8):
        xi = np.ascontiguousarray(x[2 * i:2 * i + 2].reshape(T, D))
        in_maps.append({
            "x": xi, "w1": W1, "w2": W2, "wg": wg_l, "win": W_in,
            "ab": ab_l, "cm": c_l, "woutt": woutt, "binc": bin_l,
            "nwc": nw_l, "nbc": nb_l, "bout": bout_l,
            "basec": basec, "capc": capc, "identf": identf,
        })
    res = run_bass_kernel_spmd(nc, in_maps, core_ids=list(range(8)))
    out = np.stack([np.asarray(r["out"]) for r in res.results])
    return np.ascontiguousarray(out.reshape(16, 2048, 1024)).astype(np.float32, copy=False)
